# version 16
# baseline (speedup 1.0000x reference)
import contextlib
import numpy as np
import ml_dtypes
import concourse.bass as bass
import concourse.mybir as mybir
from concourse.bass import IndirectOffsetOnAxis
from concourse.bass_utils import run_bass_kernel_spmd

F32 = mybir.dt.float32
BF16 = mybir.dt.bfloat16
I32 = mybir.dt.int32
U32 = mybir.dt.uint32
AF = mybir.ActivationFunctionType
ALU = mybir.AluOpType
AX = mybir.AxisListType

B, L, D, H, HD = 8, 2048, 1024, 16, 64
NT = L // 128
ND = D // 128
NCORES = 8
LN_EPS = 1e-5
ATTN_EPS = 1e-6

DEBUG_TAPS = ()



class Prog:
    ENGINES = ("tensor", "vector", "scalar", "gpsimd", "sync")

    def __init__(self, nc, es):
        self.nc = nc
        self.es = es
        self.q = {e: [] for e in self.ENGINES}
        self.sems = {}
        self.scount = {}

    def sem(self, name):
        if name not in self.sems:
            self.sems[name] = self.es.enter_context(self.nc.semaphore(name))
            self.scount[name] = 0
        return self.sems[name]

    ENGSEM = {"vector": "V", "scalar": "S", "gpsimd": "P", "tensor": "T"}

    def op(self, eng, fn, wait=(), inc=None, inc_amt=1, dma_=None):
        if dma_ is not None:
            return self.dma(eng, fn, dma_, wait=wait)
        wait = list(wait)
        inc2 = None
        if eng in self.ENGSEM and inc_amt != 16:
            sname = self.ENGSEM[eng]
            assert inc is None or inc == sname, (eng, inc)
            self.sem(sname)
            if eng != "tensor":
                wait.append((sname, self.scount[sname]))
            self.scount[sname] += 1
            inc2 = (sname, 1)
        elif inc is not None:
            self.sem(inc)
            self.scount[inc] += inc_amt
            inc2 = (inc, inc_amt)
        for sn, _ in wait:
            self.sem(sn)
        wcopy = tuple(wait)

        def run(e, wcopy=wcopy, fn=fn, inc2=inc2):
            for (sn, cnt) in wcopy:
                e.wait_ge(self.sems[sn], cnt)
            r = fn(e)
            if inc2 is not None:
                if r is None:
                    r = e.nop()
                r.then_inc(self.sems[inc2[0]], inc2[1])
        self.q[eng].append(run)

    NDMACH = 8

    def dma(self, eng, fn, base, wait=()):
        idx = self.scount.get("#" + base, 0)
        self.scount["#" + base] = idx + 1
        ch = base + str(idx % self.NDMACH)
        self.sem(ch)
        wait = list(wait)
        if idx >= self.NDMACH:
            wait.append((ch, self.scount[ch]))
        self.op(eng, fn, wait=wait, inc=ch, inc_amt=16)
        return (ch, self.scount[ch])

    def dma_all(self, base):
        n = self.scount.get("#" + base, 0)
        out = []
        for c in range(min(n, self.NDMACH)):
            ch = base + str(c)
            out.append((ch, self.scount[ch]))
        return out

    def now(self, semname):
        self.sem(semname)
        return (semname, self.scount[semname])

    def after(self, semname):
        return self.now(semname)

    def flush(self, block):
        for ename in self.ENGINES:
            ops = self.q[ename]
            if not ops:
                continue

            def body(e, ops=tuple(ops)):
                for r in ops:
                    r(e)
            getattr(block, ename)(body)
            self.q[ename] = []



def _bf(x):
    return np.ascontiguousarray(x).astype(ml_dtypes.bfloat16)


def stage_inputs(inputs):
    f = {k: np.asarray(v) for k, v in inputs.items()}
    common = {}
    common["identb"] = np.eye(128, dtype=ml_dtypes.bfloat16)
    common["identf"] = np.eye(128, dtype=np.float32)
    common["WqT"] = np.ascontiguousarray(f["L0_attn_Wq"].T)
    common["WkT"] = np.ascontiguousarray(f["L0_attn_Wk"].T)
    common["WvT"] = np.ascontiguousarray(f["L0_attn_Wv"].T)
    common["WoT"] = np.ascontiguousarray(f["L0_attn_Wo"].T)
    common["WbetaT"] = np.ascontiguousarray(f["L0_attn_Wbeta"].T)
    for nm, key in (("bq", "L0_attn_bq"), ("bk", "L0_attn_bk"), ("bv", "L0_attn_bv"),
                    ("bo", "L0_attn_bo")):
        common[nm] = np.ascontiguousarray(f[key].reshape(ND, 128).T)
    common["bbeta"] = np.ascontiguousarray(f["L0_attn_bbeta"].reshape(16, 1))
    for nm, key in (("wqc", "L0_attn_wq_conv"), ("wkc", "L0_attn_wk_conv"),
                    ("wvc", "L0_attn_wv_conv")):
        common[nm] = np.ascontiguousarray(
            f[key].reshape(ND, 128, 4).transpose(1, 0, 2).reshape(128, ND * 4))
    for nm, key in (("bqc", "L0_attn_bq_conv"), ("bkc", "L0_attn_bk_conv"),
                    ("bvc", "L0_attn_bv_conv")):
        common[nm] = np.ascontiguousarray(f[key].reshape(ND, 128).T)
    common["normw0"] = np.ascontiguousarray(np.tile(f["L0_attn_norm_w"], 2).reshape(128, 1))
    ii = np.arange(128)[:, None]
    jj = np.arange(128)[None, :]
    sameblk = (ii // 64) == (jj // 64)
    common["maskX8"] = np.ascontiguousarray(np.tile(
        (-(((jj < ii) & sameblk).astype(np.float32))), (1, 8)))
    common["maskG8"] = np.ascontiguousarray(np.tile(
        ((ii <= jj) & sameblk).astype(np.float32), (1, 8)))
    common["maskXT8"] = np.ascontiguousarray(np.tile(
        (-(((jj > ii) & sameblk).astype(np.float32))), (1, 8)))
    bd = np.zeros((128, 128), dtype=ml_dtypes.bfloat16)
    bd[0:64, 0:64] = 1; bd[64:128, 64:128] = 1
    common["onesbd"] = bd
    sel = np.zeros((16, ND * 128), dtype=ml_dtypes.bfloat16)
    for ft in range(ND):
        for p in range(128):
            sel[2 * ft + p // 64, ft * 128 + p] = 1
    common["selbeta"] = sel

    common["gate0"] = np.ascontiguousarray(f["L0_ffn_gate"])
    common["W1e"] = np.ascontiguousarray(f["L0_ffn_W1"])
    common["W2e"] = np.ascontiguousarray(f["L0_ffn_W2"])
    common["b1s"] = np.ascontiguousarray(
        f["L0_ffn_b1"].reshape(16, 16, 128).transpose(2, 0, 1).reshape(128, 256))
    common["b2m"] = np.ascontiguousarray(f["L0_ffn_b2"])
    selE = np.zeros((16, 16 * 128), dtype=ml_dtypes.bfloat16)
    for e in range(16):
        selE[e, e * 128:(e + 1) * 128] = 1
    common["selE"] = selE

    common["cw1"] = np.ascontiguousarray(
        f["L1_attn_conv_w"][:, 0, :].reshape(4, ND, 128).transpose(2, 1, 0).reshape(128, ND * 4))
    common["cb1"] = np.ascontiguousarray(f["L1_attn_conv_b"].reshape(ND, 128).T)
    common["WpN"] = np.ascontiguousarray(f["L1_attn_Wp"])
    common["WsN"] = np.ascontiguousarray(f["L1_attn_Ws"])
    common["Wo1"] = np.ascontiguousarray(f["L1_attn_Wo"])
    common["bpB"] = np.ascontiguousarray(f["L1_attn_bp"][0:64].reshape(64, 1))
    common["bpC"] = np.ascontiguousarray(f["L1_attn_bp"][64:128].reshape(64, 1))
    common["bs1"] = np.ascontiguousarray(f["L1_attn_bs"].reshape(64, 1))
    common["adt"] = np.ascontiguousarray(
        np.stack([f["L1_attn_A_log"], f["L1_attn_dt_log"]], axis=1))
    common["bo1"] = np.ascontiguousarray(f["L1_attn_bo"].reshape(ND, 128).T)
    common["rwT"] = np.ascontiguousarray(f["L1_ffn_router_weight"].T)
    common["rbias"] = np.ascontiguousarray(f["L1_ffn_router_bias"].reshape(1, 16))
    common["normw1"] = np.ascontiguousarray(f["L1_ffn_norm_weight"].reshape(1, D))
    common["nw128"] = np.ascontiguousarray(f["L1_ffn_norm_weight"].reshape(ND, 128).T)
    common["iota16"] = np.ascontiguousarray(np.tile(np.arange(16, dtype=np.float32), (128, 1)))
    common["ones1"] = np.ones((1, 128), dtype=ml_dtypes.bfloat16)

    FULLE = [4, 6, 7, 9, 10, 12, 13, 15]
    HALFE = [5, 8, 11, 14]
    Arow = np.full(16, 1.0e7, np.float32)
    Brow = np.zeros(16, np.float32)
    for c, e in enumerate(FULLE):
        Arow[e] = 3072.0 * c
    for i, e in enumerate(HALFE):
        Arow[e] = 3072.0 * (2 * i) + 2048.0
        Brow[e] = 2048.0
    ii16 = np.arange(128)
    common["ecol"] = np.ascontiguousarray((ii16 % 16).astype(np.float32).reshape(128, 1))
    common["Acol"] = np.ascontiguousarray(Arow[(ii16 % 16)].reshape(128, 1))
    common["Bcol"] = np.ascontiguousarray(Brow[(ii16 % 16)].reshape(128, 1))
    Mfix = ((ii16[:, None] % 16 == ii16[None, :] % 16) &
            (ii16[:, None] // 16 < ii16[None, :] // 16)).astype(np.float32)
    common["Mfix"] = np.ascontiguousarray(Mfix)
    gsum = (ii16[:, None] // 16 == ii16[None, :] // 16).astype(np.float32)
    common["gsum"] = np.ascontiguousarray(gsum)
    sel8 = np.zeros((8, 128), dtype=ml_dtypes.bfloat16)
    for ch in range(8):
        sel8[ch, ch * 16:(ch + 1) * 16] = 1
    common["sel8"] = sel8
    ohch = np.zeros((128, 8), np.float32)
    for ch in range(8):
        ohch[ch * 16, ch] = 1
    common["ohch"] = ohch
    common["cvm"] = np.ascontiguousarray(f["L1_ffn_constant_vectors"])
    pp48 = np.arange(48)
    common["ex48sum"] = np.ascontiguousarray(
        (pp48[:, None] % 16 == pp48[None, :] % 16).astype(np.float32))
    common["exgt48"] = np.ascontiguousarray(
        ((pp48[:, None] % 16 == pp48[None, :] % 16) &
         (pp48[None, :] // 16 > pp48[:, None] // 16)).astype(np.float32))
    common["exblk48"] = np.ascontiguousarray(
        (pp48[:, None] // 16 == pp48[None, :] // 16).astype(np.float32))
    common["ecol48"] = np.ascontiguousarray((pp48 % 16).astype(np.float32).reshape(48, 1))
    common["acol48"] = np.ascontiguousarray(
        ((pp48 % 16) * 1024.0).astype(np.float32).reshape(48, 1))
    sel3 = np.zeros((3, 48), dtype=ml_dtypes.bfloat16)
    for kk3 in range(3):
        sel3[kk3, kk3 * 16:(kk3 + 1) * 16] = 1
    common["sel3"] = sel3
    selx = np.zeros((48, 3), dtype=np.float32)
    for kk3 in range(3):
        selx[kk3 * 16, kk3] = 1
    common["selx"] = selx

    shards = np.split(np.ascontiguousarray(f["L0_attn_x"]), NCORES, axis=0)
    in_maps = []
    for c in range(NCORES):
        m = dict(common)
        m["x"] = np.ascontiguousarray(shards[c][0])
        in_maps.append(m)
    return in_maps



def build_program(taps=(), detect_races=True, n_exp=16, fused=False):
    nc = bass.Bass("TRN2", target_bir_lowering=False, detect_race_conditions=detect_races)
    es = contextlib.ExitStack()
    g = {}

    def din(name, shape, dtype):
        g[name] = nc.dram_tensor(name, list(shape), dtype, kind="ExternalInput").ap()
        return g[name]

    def dout(name, shape, dtype):
        g[name] = nc.dram_tensor(name, list(shape), dtype, kind="ExternalOutput").ap()
        return g[name]

    def dram(name, shape, dtype):
        g[name] = nc.dram_tensor(name, list(shape), dtype, kind="Internal").ap()
        return g[name]

    def sb(name, shape, dtype):
        g[name] = es.enter_context(nc.sbuf_tensor(name, list(shape), dtype))
        return g[name]

    def ps(name, shape, dtype):
        g[name] = es.enter_context(nc.psum_tensor(name, list(shape), dtype))
        return g[name]

    din("x", (L, D), F32)
    din("identb", (128, 128), BF16)
    din("identf", (128, 128), F32)
    for nm in ("WqT", "WkT", "WvT", "WoT"):
        din(nm, (D, D), F32)
    din("WbetaT", (D, 16), F32)
    for nm in ("bq", "bk", "bv", "bo", "bqc", "bkc", "bvc"):
        din(nm, (128, ND), F32)
    din("bbeta", (16, 1), F32)
    for nm in ("wqc", "wkc", "wvc"):
        din(nm, (128, ND * 4), F32)
    din("normw0", (128, 1), F32)
    din("maskX8", (128, 1024), F32)
    din("maskG8", (128, 1024), F32)
    din("selbeta", (16, ND * 128), BF16)
    din("onesbd", (128, 128), BF16)
    din("gate0", (D, 16), F32)
    din("W1e", (16, D, 2048), F32)
    din("W2e", (16, 2048, D), F32)
    din("b1s", (128, 256), F32)
    din("b2m", (16, D), F32)
    din("selE", (16, 16 * 128), BF16)
    din("ex48sum", (48, 48), F32)
    din("exgt48", (48, 48), F32)
    din("exblk48", (48, 48), F32)
    din("ecol48", (48, 1), F32)
    din("acol48", (48, 1), F32)
    din("sel3", (3, 48), BF16)
    din("selx", (48, 3), F32)
    din("cw1", (128, ND * 4), F32)
    din("cb1", (128, ND), F32)
    din("WpN", (D, 128), F32)
    din("WsN", (D, 64), F32)
    din("Wo1", (64, D), F32)
    din("bpB", (64, 1), F32)
    din("bpC", (64, 1), F32)
    din("bs1", (64, 1), F32)
    din("adt", (64, 2), F32)
    din("bo1", (128, ND), F32)
    din("rwT", (D, 16), F32)
    din("rbias", (1, 16), F32)
    din("normw1", (1, D), F32)
    din("nw128", (128, ND), F32)
    din("iota16", (128, 16), F32)
    din("ones1", (1, 128), BF16)

    with es:
        p = Prog(nc, es)
        block = es.enter_context(nc.Block())

        es_xnf = contextlib.ExitStack()
        g["xnFMb"] = es_xnf.enter_context(
            nc.sbuf_tensor("xnFMb", [128, ND * 2048], BF16, side="right"))
        sb("ident_b", (128, 128), BF16)
        sb("ident_f", (128, 128), F32)
        sb("stat", (128, 6 * NT), F32)
        sb("sq_scr", (128, 1024), BF16)
        sb("bias_qkv", (128, ND * 8), F32)
        sb("wconv", (128, ND * 4 * 3), F32)
        sb("bbeta_s", (16, 1), F32)
        sb("normw_s", (128, 1), F32)

        es_p0 = contextlib.ExitStack()
        g["xTM"] = es_p0.enter_context(nc.sbuf_tensor("xTM", [128, NT * 1024], F32, side="right"))
        g["xnTMb"] = es_p0.enter_context(nc.sbuf_tensor("xnTMb", [128, NT * 1024], BF16, side="right"))

        es_attn = contextlib.ExitStack()

        def sba(name, shape, dtype):
            g[name] = es_attn.enter_context(nc.sbuf_tensor(name, list(shape), dtype))
            return g[name]

        def psa(name, shape, dtype):
            g[name] = es_attn.enter_context(nc.psum_tensor(name, list(shape), dtype))
            return g[name]

        sba("wq_bf", (128, ND * 1024), BF16)
        sba("wk_bf", (128, ND * 1024), BF16)
        sba("wv_bf", (128, ND * 1024), BF16)
        sba("wbeta_bf", (128, ND * 16), BF16)
        sba("selb_bf", (16, ND * 128), BF16)
        sba("onesbd_b", (128, 128), BF16)

        g["pT0"] = es_p0.enter_context(nc.psum_tensor("pT0", [128, 128], BF16, side="right"))
        g["pT1"] = es_p0.enter_context(nc.psum_tensor("pT1", [128, 128], BF16, side="right"))

        for tt in range(NT):
            p.op("sync", lambda e, tt=tt: e.dma_start(
                out=g["xTM"][:, tt * 1024:(tt + 1) * 1024],
                in_=g["x"][tt * 128:(tt + 1) * 128, :]), dma_="ld")
        p.op("sync", lambda e: e.dma_start(out=g["ident_b"][:], in_=g["identb"]), dma_="ld")
        p.op("sync", lambda e: e.dma_start(out=g["ident_f"][:], in_=g["identf"]), dma_="ld")
        for kc in range(ND):
            for wname, dname in (("wq_bf", "WqT"), ("wk_bf", "WkT"), ("wv_bf", "WvT")):
                p.op("gpsimd", lambda e, kc=kc, wname=wname, dname=dname: e.dma_start(
                    out=g[wname][:, kc * 1024:(kc + 1) * 1024],
                    in_=g[dname][kc * 128:(kc + 1) * 128, :]), dma_="wld")
            p.op("gpsimd", lambda e, kc=kc: e.dma_start(
                out=g["wbeta_bf"][:, kc * 16:(kc + 1) * 16],
                in_=g["WbetaT"][kc * 128:(kc + 1) * 128, :]), dma_="wld")
        p.op("gpsimd", lambda e: e.dma_start(out=g["selb_bf"][:], in_=g["selbeta"]), dma_="wld")
        p.op("gpsimd", lambda e: e.dma_start(out=g["onesbd_b"][:], in_=g["onesbd"]), dma_="wld")
        for j, nm in enumerate(("bq", "bk", "bv", "bo", "bqc", "bkc", "bvc")):
            p.op("gpsimd", lambda e, j=j, nm=nm: e.dma_start(
                out=g["bias_qkv"][:, j * ND:(j + 1) * ND], in_=g[nm]), dma_="wld")
        for j, nm in enumerate(("wqc", "wkc", "wvc")):
            p.op("gpsimd", lambda e, j=j, nm=nm: e.dma_start(
                out=g["wconv"][:, j * ND * 4:(j + 1) * ND * 4], in_=g[nm]), dma_="wld")
        p.op("gpsimd", lambda e: e.dma_start(out=g["bbeta_s"][:], in_=g["bbeta"]), dma_="wld")
        p.op("gpsimd", lambda e: e.dma_start(out=g["normw_s"][:], in_=g["normw0"]), dma_="wld")
        wld_all = None
        ld_all = None

        for tt in range(NT):
            wt = list(p.dma_all("ld"))[:0] or p.dma_all("ld")
            p.op("scalar", lambda e, tt=tt: e.activation(
                g["sq_scr"][:, 0:1024], g["xTM"][:, tt * 1024:(tt + 1) * 1024],
                AF.Square, accum_out=g["stat"][:, NT + tt:NT + tt + 1]), wait=wt)
            p.op("scalar", lambda e, tt=tt: e.activation(
                g["xnTMb"][:, tt * 1024:(tt + 1) * 1024], g["xTM"][:, tt * 1024:(tt + 1) * 1024],
                AF.Copy, accum_out=g["stat"][:, tt:tt + 1]), wait=wt)
        stats_done = p.now("S")
        MEAN, RSTD, TMP = 2 * NT, 3 * NT, 4 * NT
        p.op("vector", lambda e: e.tensor_scalar(
            g["stat"][:, MEAN:MEAN + NT], g["stat"][:, 0:NT], 1.0 / D, None, ALU.mult),
            wait=[stats_done])
        p.op("vector", lambda e: e.tensor_tensor(
            g["stat"][:, TMP:TMP + NT], g["stat"][:, MEAN:MEAN + NT],
            g["stat"][:, MEAN:MEAN + NT], ALU.mult))
        p.op("vector", lambda e: e.tensor_scalar(
            g["stat"][:, RSTD:RSTD + NT], g["stat"][:, NT:NT + NT], 1.0 / D, None, ALU.mult))
        p.op("vector", lambda e: e.tensor_tensor(
            g["stat"][:, RSTD:RSTD + NT], g["stat"][:, RSTD:RSTD + NT],
            g["stat"][:, TMP:TMP + NT], ALU.subtract))
        p.op("vector", lambda e: e.tensor_scalar(
            g["stat"][:, RSTD:RSTD + NT], g["stat"][:, RSTD:RSTD + NT], LN_EPS, None, ALU.add))
        p.op("scalar", lambda e: e.activation(
            g["stat"][:, TMP:TMP + NT], g["stat"][:, RSTD:RSTD + NT], AF.Ln),
            wait=[p.now("V")])
        p.op("vector", lambda e: e.tensor_scalar(
            g["stat"][:, TMP:TMP + NT], g["stat"][:, TMP:TMP + NT], -0.5, None, ALU.mult),
            wait=[p.now("S")])
        p.op("scalar", lambda e: e.activation(
            g["stat"][:, RSTD:RSTD + NT], g["stat"][:, TMP:TMP + NT], AF.Exp),
            wait=[p.now("V")])
        rstd_done = p.now("S")

        for tt in range(NT):
            p.op("vector", lambda e, tt=tt: e.tensor_scalar(
                g["xnTMb"][:, tt * 1024:(tt + 1) * 1024],
                g["xTM"][:, tt * 1024:(tt + 1) * 1024],
                g["stat"][:, MEAN + tt:MEAN + tt + 1],
                g["stat"][:, RSTD + tt:RSTD + tt + 1],
                ALU.subtract, ALU.mult), wait=[rstd_done])
        xn_done = p.now("V")
        dram("xn_d", (L, D), BF16)
        xn_st = []
        for tt in range(NT):
            xn_st.append(p.op("sync", lambda e, tt=tt: e.dma_start(
                out=g["xn_d"][tt * 128:(tt + 1) * 128, :],
                in_=g["xnTMb"][:, tt * 1024:(tt + 1) * 1024]), wait=[xn_done], dma_="st"))

        tr_cnt = 0
        evac_done = []
        for ft in range(ND):
            for tt in range(NT):
                buf = "pT0" if tr_cnt % 2 == 0 else "pT1"
                wt = [xn_done]
                if tr_cnt >= 2:
                    wt.append(evac_done[tr_cnt - 2])
                p.op("tensor", lambda e, buf=buf, tt=tt, ft=ft: e.transpose(
                    g[buf][:], g["xnTMb"][:, tt * 1024 + ft * 128: tt * 1024 + ft * 128 + 128],
                    g["ident_b"][:]), wait=wt)
                trd = p.now("T")
                eng = "vector" if tr_cnt % 2 == 0 else "scalar"
                off = ft * 2048 + tt * 128
                if eng == "vector":
                    p.op(eng, lambda e, buf=buf, off=off: e.tensor_copy(
                        g["xnFMb"][:, off:off + 128], g[buf][:]), wait=[trd])
                else:
                    p.op(eng, lambda e, buf=buf, off=off: e.copy(
                        g["xnFMb"][:, off:off + 128], g[buf][:]), wait=[trd])
                evac_done.append(p.now("V" if eng == "vector" else "S"))
                tr_cnt += 1
        xn_fm_done_v = p.now("V")
        xn_fm_done_s = p.now("S")
        xn_fm_done = xn_fm_done_v



        es_p0.close()

        sba("maskX_s", (128, 1024), F32)
        sba("maskG_s", (128, 1024), F32)
        sba("wo_bf", (128, ND * 1024), BF16)
        sba("raw", (128, ND * 520), BF16)
        sba("hist", (128, 72), BF16)
        sba("qn_b", (128, ND * 512), BF16)
        sba("kn_b", (128, ND * 512), BF16)
        sba("kb_b", (128, ND * 512), BF16)
        sba("vb_b", (128, ND * 512), BF16)
        sba("betaB_s", (128, 512), BF16)
        sba("beta16g", (16, 512), BF16)
        sba("Zb0", (128, 1024), F32)
        sba("Zb1", (128, 1024), F32)
        sba("Wb0", (128, 1024), F32)
        sba("Wb1", (128, 1024), F32)
        sba("Gb0", (128, 1024), BF16)
        sba("Gb1", (128, 1024), BF16)
        sba("Ub", (128, 512), BF16)
        sba("Yb", (128, 512), F32)
        sba("VKtm0", (128, 1024), BF16)
        sba("VKtm1", (128, 1024), BF16)
        sba("Tb", (128, 1024), F32)
        sba("Tt0", (128, 1024), F32)
        sba("Tt1", (128, 1024), F32)
        sba("S_f32", (128, 512), F32)
        sba("S_bf", (128, 512), BF16)
        sba("O_grp", (128, 512), F32)
        sba("o_n", (128, 512), BF16)
        sba("o_fm", (128, 1024), BF16)
        sba("s_scr", (128, 1024), BF16)
        sba("nrm_scr", (128, 1024), F32)
        sba("stat2", (128, 64), F32)
        sba("zeros_b", (64, 128), BF16)
        psa("pZa", (128, 512), F32)
        psa("pZb", (128, 512), F32)
        psa("pWa", (128, 512), F32)
        psa("pWb", (128, 512), F32)
        psa("pTa", (128, 512), F32)
        psa("pTb", (128, 512), F32)
        psa("pO", (128, 512), F32)
        psa("pM", (128, 512), F32)
        dram("s_tm_d", (L, D), BF16)

        p.op("sync", lambda e: e.dma_start(out=g["maskX_s"][:], in_=g["maskX8"]),
             wait=[xn_fm_done_v, xn_fm_done_s] + xn_st, dma_="ld")
        p.op("sync", lambda e: e.dma_start(out=g["maskG_s"][:], in_=g["maskG8"]),
             wait=[xn_fm_done_v, xn_fm_done_s] + xn_st, dma_="ld")
        masks_ld = None
        for kc in range(ND):
            p.op("gpsimd", lambda e, kc=kc: e.dma_start(
                out=g["wo_bf"][:, kc * 1024:(kc + 1) * 1024],
                in_=g["WoT"][kc * 128:(kc + 1) * 128, :]),
                wait=[xn_fm_done_v, xn_fm_done_s] + xn_st, dma_="wld")
        wo_ld = None
        p.op("vector", lambda e: e.tensor_scalar(
            g["wo_bf"][:], g["wo_bf"][:], g["normw_s"][:, 0:1], None, ALU.mult),
            wait=p.dma_all("wld"))
        wo_ready = p.now("V")
        p.op("gpsimd", lambda e: e.memset(g["raw"][:], 0.0),
             wait=[xn_fm_done_v, xn_fm_done_s] + xn_st)
        p.op("gpsimd", lambda e: e.memset(g["zeros_b"][:], 0.0),
             wait=[xn_fm_done_v, xn_fm_done_s] + xn_st)
        pads_z = p.now("P")
        dram("w1b_d", (16, 1024, 2048), BF16)
        dram("w2b_d", (16, 2048, 1024), BF16)
        cvt_done = {}
        for ex in range(16):
            e1 = p.op("gpsimd", lambda e, ex=ex: e.dma_start(
                out=g["w1b_d"][ex], in_=g["W1e"][ex]),
                wait=[xn_fm_done_v] if ex == 0 else [], dma_="mwf")
            e2 = p.op("gpsimd", lambda e, ex=ex: e.dma_start(
                out=g["w2b_d"][ex], in_=g["W2e"][ex]), dma_="mwf")
            cvt_done[ex] = [e1, e2]

        pZ_free = [pads_z, pads_z]
        pW_free = pads_z
        pT_free = pads_z
        pO_free = pads_z
        pM_free = pads_z
        O_free = pads_z
        Tt_ready = [pads_z, pads_z]
        VK_ready = [pads_z, pads_z]
        G_ready = [pads_z, pads_z]
        S_bf_ready = None
        s_written = []
        group_ready = None
        prev_group_read = pads_z
        prev_conv_v = pads_z

        s_store_prev = None
        CVJ = {"q": 0, "k": 1, "v": 2}
        BJ = {"q": 0, "k": 1, "v": 2}
        BCJ = {"q": 4, "k": 5, "v": 6}
        RAW = {"q": "raw_q", "k": "raw_k", "v": "raw_v"}
        DST = {"q": "qn_b", "k": "kn_b", "v": "vb_b"}

        for grp in range(4):
            tok0 = grp * 512
            proj_evac = {}
            half = 0
            for tname in ("q", "k", "v"):
                wname = {"q": "wq_bf", "k": "wk_bf", "v": "wv_bf"}[tname]
                dst = DST[tname]
                hco = {"q": 0, "k": 24, "v": 48}[tname]
                if grp > 0:
                    def hrest(e, hco=hco):
                        r3 = g["raw"][:].rearrange("p (f c) -> p f c", c=520)
                        h3 = g["hist"][:, hco:hco + 24].rearrange("p (f c) -> p f c", c=3)
                        return e.tensor_copy(r3[:, :, 5:8], h3)
                    p.op("vector", hrest, wait=[prev_conv_v])
                hist_restored = p.now("V")
                for fo in range(ND):
                    psA = g["pZa" if half == 0 else "pZb"][:]
                    def mmproj(e, wname=wname, fo=fo, tok0=tok0, psA=psA):
                        r = None
                        for kc in range(ND):
                            r = e.matmul(psA, g[wname][:, kc * 1024 + fo * 128: kc * 1024 + fo * 128 + 128],
                                         g["xnFMb"][:, kc * 2048 + tok0: kc * 2048 + tok0 + 512],
                                         start=(kc == 0), stop=(kc == ND - 1))
                        return r
                    p.op("tensor", mmproj, wait=p.dma_all("wld") + [xn_fm_done_v, xn_fm_done_s, pZ_free[half]], inc="T")
                    mmdone = p.now("T")
                    def evproj(e, tname=tname, fo=fo, psA=psA):
                        return e.activation(
                            g["raw"][:, fo * 520 + 8: fo * 520 + 520], psA, AF.Identity,
                            bias=g["bias_qkv"][:, BJ[tname] * ND + fo: BJ[tname] * ND + fo + 1])
                    p.op("scalar", evproj, wait=[mmdone, prev_conv_v, hist_restored], inc="S")
                    pZ_free[half] = p.now("S")
                    proj_evac[(tname, fo)] = pZ_free[half]
                    half ^= 1
                for ft in range(ND):
                    wbase = CVJ[tname] * ND * 4 + ft * 4
                    def cnv0(e, dst=dst, ft=ft, wbase=wbase, tname=tname):
                        return e.tensor_scalar(
                            g[dst][:, ft * 512:(ft + 1) * 512],
                            g["raw"][:, ft * 520 + 8: ft * 520 + 520],
                            g["wconv"][:, wbase + 3: wbase + 4],
                            g["bias_qkv"][:, BCJ[tname] * ND + ft: BCJ[tname] * ND + ft + 1],
                            ALU.mult, ALU.add)
                    p.op("vector", cnv0, wait=[proj_evac[(tname, ft)], prev_group_read, hist_restored])
                    for tap in (2, 1, 0):
                        def cnvi(e, dst=dst, ft=ft, wbase=wbase, tap=tap):
                            return e.scalar_tensor_tensor(
                                g[dst][:, ft * 512:(ft + 1) * 512],
                                g["raw"][:, ft * 520 + 5 + tap: ft * 520 + 5 + tap + 512],
                                g["wconv"][:, wbase + tap: wbase + tap + 1],
                                g[dst][:, ft * 512:(ft + 1) * 512],
                                ALU.mult, ALU.add)
                        p.op("vector", cnvi)
                def hsave(e, hco=hco):
                    r3 = g["raw"][:].rearrange("p (f c) -> p f c", c=520)
                    h3 = g["hist"][:, hco:hco + 24].rearrange("p (f c) -> p f c", c=3)
                    return e.tensor_copy(h3, r3[:, :, 517:520])
                p.op("vector", hsave)
                prev_conv_v = p.now("V")
            conv_all = p.now("V")

            def mmbeta(e, tok0=tok0):
                r = None
                for kc in range(ND):
                    r = e.matmul(g["pM"][0:16, 0:512], g["wbeta_bf"][:, kc * 16:(kc + 1) * 16],
                                 g["xnFMb"][:, kc * 2048 + tok0: kc * 2048 + tok0 + 512],
                                 start=(kc == 0), stop=(kc == ND - 1))
                return r
            p.op("tensor", mmbeta, wait=p.dma_all("wld") + [xn_fm_done_v, xn_fm_done_s, pM_free], inc="T")
            p.op("scalar", lambda e: e.activation(
                g["beta16g"][:], g["pM"][0:16, 0:512], AF.Sigmoid,
                bias=g["bbeta_s"][:, 0:1]), wait=[p.now("T")], inc="S")
            beta16_done = p.now("S")
            pM_free = beta16_done

            nrm_free = p.now("V")
            for ft in range(ND):
                conv_q = conv_all
                conv_v = conv_all
                p.op("scalar", lambda e, ft=ft: e.activation(
                    g["sq_scr"][:, 0:512], g["kn_b"][:, ft * 512:(ft + 1) * 512], AF.Square),
                    wait=[conv_q, p.now("T")], inc="S")
                p.op("tensor", lambda e, ft=ft: e.matmul(
                    g["pO"][:], g["onesbd_b"][:], g["sq_scr"][:, 0:512],
                    start=True, stop=True), wait=p.dma_all("wld") + [p.now("S"), pO_free], inc="T")
                p.op("scalar", lambda e: e.activation(
                    g["nrm_scr"][:, 0:512], g["pO"][:], AF.Sqrt),
                    wait=[p.now("T"), nrm_free], inc="S")
                pO_free = p.now("S")
                p.op("vector", lambda e: e.tensor_scalar(
                    g["nrm_scr"][:, 0:512], g["nrm_scr"][:, 0:512], ATTN_EPS, None, ALU.add),
                    wait=[pO_free], inc="V")
                p.op("vector", lambda e: e.reciprocal(
                    g["nrm_scr"][:, 0:512], g["nrm_scr"][:, 0:512]), inc="V")
                kinv_done = p.now("V")
                p.op("scalar", lambda e, ft=ft: e.activation(
                    g["sq_scr"][:, 512:1024], g["qn_b"][:, ft * 512:(ft + 1) * 512], AF.Square),
                    wait=[conv_q, p.now("T")], inc="S")
                p.op("tensor", lambda e, ft=ft: e.matmul(
                    g["pTa"][:], g["onesbd_b"][:], g["sq_scr"][:, 512:1024],
                    start=True, stop=True), wait=p.dma_all("wld") + [p.now("S"), pT_free], inc="T")
                p.op("scalar", lambda e: e.activation(
                    g["nrm_scr"][:, 512:1024], g["pTa"][:], AF.Sqrt),
                    wait=[p.now("T"), nrm_free], inc="S")
                pT_free = p.now("S")
                p.op("vector", lambda e: e.tensor_scalar(
                    g["nrm_scr"][:, 512:1024], g["nrm_scr"][:, 512:1024], ATTN_EPS, None, ALU.add),
                    wait=[pT_free], inc="V")
                p.op("vector", lambda e: e.reciprocal(
                    g["nrm_scr"][:, 512:1024], g["nrm_scr"][:, 512:1024]), inc="V")
                p.op("scalar", lambda e, ft=ft: e.activation(
                    g["vb_b"][:, ft * 512:(ft + 1) * 512], g["vb_b"][:, ft * 512:(ft + 1) * 512],
                    AF.Silu), wait=[conv_v], inc="S")
                silu_done = p.now("S")
                p.op("vector", lambda e, ft=ft: e.tensor_tensor(
                    g["kn_b"][:, ft * 512:(ft + 1) * 512], g["kn_b"][:, ft * 512:(ft + 1) * 512],
                    g["nrm_scr"][:, 0:512], ALU.mult), wait=[kinv_done], inc="V")
                p.op("vector", lambda e, ft=ft: e.tensor_tensor(
                    g["vb_b"][:, ft * 512:(ft + 1) * 512], g["vb_b"][:, ft * 512:(ft + 1) * 512],
                    g["nrm_scr"][:, 0:512], ALU.mult), wait=[silu_done], inc="V")
                p.op("vector", lambda e, ft=ft: e.tensor_tensor(
                    g["qn_b"][:, ft * 512:(ft + 1) * 512], g["qn_b"][:, ft * 512:(ft + 1) * 512],
                    g["nrm_scr"][:, 512:1024], ALU.mult), inc="V")
                kqv_done = p.now("V")
                p.op("tensor", lambda e, ft=ft: e.matmul(
                    g["pM"][:], g["selb_bf"][:, ft * 128:(ft + 1) * 128], g["beta16g"][:],
                    start=True, stop=True), wait=p.dma_all("wld") + [beta16_done, pM_free], inc="T")
                p.op("vector", lambda e: e.tensor_copy(
                    g["betaB_s"][:], g["pM"][:]), wait=[p.now("T")], inc="V")
                pM_free = p.now("V")
                p.op("vector", lambda e, ft=ft: e.tensor_tensor(
                    g["kb_b"][:, ft * 512:(ft + 1) * 512], g["kn_b"][:, ft * 512:(ft + 1) * 512],
                    g["betaB_s"][:], ALU.mult), wait=[kqv_done], inc="V")
                p.op("vector", lambda e, ft=ft: e.tensor_tensor(
                    g["vb_b"][:, ft * 512:(ft + 1) * 512], g["vb_b"][:, ft * 512:(ft + 1) * 512],
                    g["betaB_s"][:], ALU.mult), inc="V")
                nrm_free = p.now("V")
            group_ready = p.now("V")
            prev_group_read = group_ready

            def HSg(h, cc):
                return (h % 2) * 64, (h // 2) * 512 + cc * 64

            PS = {"pZ": list(pZ_free), "pW": [pW_free], "pT": [pT_free],
                  "pM": [pM_free], "pO": [pO_free], "O_free": O_free,
                  "Tb_rd": pads_z, "on_free": pads_z,
                  "Yb_free": pads_z, "Ub_free": pads_z,
                  "Ub_done": None, "Yb_done": None, "O_done": None,
                  "sstore": s_store_prev}

            def prep_main(cc):
                par = cc % 2
                def amm(e, cc=cc):
                    r = None
                    for pl in range(8):
                        pz = g["pZa" if pl < 4 else "pZb"]
                        c0 = (pl % 4) * 128
                        r = e.matmul(pz[:, c0: c0 + 128],
                                     g["zeros_b"][:], g["kn_b"][0:64, 0:128],
                                     start=True, stop=False, skip_group_check=True)
                        for sub in range(2):
                            h = 2 * pl + sub
                            r0, cs = HSg(h, cc)
                            r = e.matmul(pz[r0:r0 + 64, c0 + r0: c0 + r0 + 64],
                                         g["kb_b"][r0:r0 + 64, cs:cs + 64],
                                         g["kn_b"][r0:r0 + 64, cs:cs + 64],
                                         start=False, stop=(sub == 1), skip_group_check=True)
                    return r
                p.op("tensor", amm, wait=[group_ready] + PS["pZ"], inc="T")
                _t = p.now("T")
                p.op("vector", lambda e: e.tensor_tensor(
                    g["Zb0"][:, 0:512], g["pZa"][:], g["maskX_s"][:, 0:512], ALU.mult),
                    wait=[_t] + p.dma_all("ld"))
                p.op("vector", lambda e: e.tensor_tensor(
                    g["Zb0"][:, 512:1024], g["pZb"][:], g["maskX_s"][:, 512:1024], ALU.mult))
                Z_d = p.now("V")
                PS["pZ"] = [Z_d]
                def ammT(e):
                    r = None
                    for pl in range(8):
                        pw = g["pWa" if pl < 4 else "pWb"]
                        r = e.matmul(pw[:, (pl % 4) * 128: (pl % 4) * 128 + 128],
                                     g["Zb0"][:, pl * 128: pl * 128 + 128],
                                     g["ident_f"][:], start=True, stop=True)
                    return r
                p.op("tensor", ammT, wait=[Z_d] + PS["pW"], inc="T")
                _t = p.now("T")
                p.op("scalar", lambda e: e.copy(g["Wb0"][:, 0:512], g["pWa"][:]),
                     wait=[_t], inc="S")
                p.op("scalar", lambda e: e.copy(g["Wb0"][:, 512:1024], g["pWb"][:]), inc="S")
                W_d = p.now("S")
                PS["pW"] = [W_d]
                for pl in range(8):
                    p.op("vector", lambda e, pl=pl: e.tensor_tensor(
                        g["Tb"][:, pl * 128:(pl + 1) * 128], g["Zb0"][:, pl * 128:(pl + 1) * 128],
                        g["ident_f"][:], ALU.add),
                        wait=[Z_d, PS["Tb_rd"]] if pl == 0 else ())
                T_d = p.now("V")
                Zp, Wp = "Zb0", "Wb0"
                Zp_d, Wp_d, Tp_d = Z_d, W_d, T_d
                for j in range(1, 6):
                    Zc = "Zb1" if j % 2 == 1 else "Zb0"
                    Wc = "Wb1" if j % 2 == 1 else "Wb0"
                    def sqmm(e, Zp=Zp, Wp=Wp):
                        r = None
                        for pl in range(8):
                            pz = g["pZa" if pl < 4 else "pZb"]
                            r = e.matmul(pz[:, (pl % 4) * 128: (pl % 4) * 128 + 128],
                                         g[Wp][:, pl * 128: pl * 128 + 128],
                                         g[Zp][:, pl * 128: pl * 128 + 128],
                                         start=True, stop=True)
                        return r
                    p.op("tensor", sqmm, wait=[Zp_d, Wp_d] + PS["pZ"], inc="T")
                    _t1 = p.now("T")
                    def sqmmT(e, Zp=Zp, Wp=Wp):
                        r = None
                        for pl in range(8):
                            pw = g["pWa" if pl < 4 else "pWb"]
                            r = e.matmul(pw[:, (pl % 4) * 128: (pl % 4) * 128 + 128],
                                         g[Zp][:, pl * 128: pl * 128 + 128],
                                         g[Wp][:, pl * 128: pl * 128 + 128],
                                         start=True, stop=True)
                        return r
                    p.op("tensor", sqmmT, wait=PS["pW"], inc="T")
                    _t2 = p.now("T")
                    p.op("vector", lambda e, Zc=Zc: e.tensor_copy(
                        g[Zc][:, 0:512], g["pZa"][:]), wait=[_t1])
                    p.op("vector", lambda e, Zc=Zc: e.tensor_copy(
                        g[Zc][:, 512:1024], g["pZb"][:]))
                    Zc_d = p.now("V")
                    PS["pZ"] = [Zc_d]
                    p.op("scalar", lambda e, Wc=Wc: e.copy(
                        g[Wc][:, 0:512], g["pWa"][:]), wait=[_t2], inc="S")
                    p.op("scalar", lambda e, Wc=Wc: e.copy(
                        g[Wc][:, 512:1024], g["pWb"][:]), inc="S")
                    Wc_d = p.now("S")
                    PS["pW"] = [Wc_d]
                    def tmm(e, Wc=Wc):
                        r = None
                        for pl in range(8):
                            pt = g["pTa" if pl < 4 else "pTb"]
                            r = e.matmul(pt[:, (pl % 4) * 128: (pl % 4) * 128 + 128],
                                         g[Wc][:, pl * 128: pl * 128 + 128],
                                         g["Tb"][:, pl * 128: pl * 128 + 128],
                                         start=True, stop=True)
                        return r
                    p.op("tensor", tmm, wait=[Wc_d, Tp_d] + PS["pT"], inc="T")
                    _t3 = p.now("T")
                    p.op("vector", lambda e: e.tensor_tensor(
                        g["Tb"][:, 0:512], g["Tb"][:, 0:512], g["pTa"][:], ALU.add),
                        wait=[_t3])
                    p.op("vector", lambda e: e.tensor_tensor(
                        g["Tb"][:, 512:1024], g["Tb"][:, 512:1024], g["pTb"][:], ALU.add))
                    Tp_d = p.now("V")
                    PS["pT"] = [Tp_d]
                    Zp, Wp = Zc, Wc
                    Zp_d, Wp_d = Zc_d, Wc_d
                def ttr(e):
                    r = None
                    for pl in range(8):
                        pw = g["pWa" if pl < 4 else "pWb"]
                        r = e.matmul(pw[:, (pl % 4) * 128: (pl % 4) * 128 + 128],
                                     g["Tb"][:, pl * 128: pl * 128 + 128],
                                     g["ident_f"][:], start=True, stop=True)
                    return r
                p.op("tensor", ttr, wait=[Tp_d] + PS["pW"], inc="T")
                _t = p.now("T")
                PS["Tb_rd"] = _t
                p.op("vector", lambda e, par=par: e.tensor_copy(
                    g["Tt" + str(par)][:, 0:512], g["pWa"][:]), wait=[_t])
                p.op("vector", lambda e, par=par: e.tensor_copy(
                    g["Tt" + str(par)][:, 512:1024], g["pWb"][:]))
                Tt_ready[par] = p.now("V")
                PS["pW"] = [Tt_ready[par]]

            def prep_gtrvk(cc):
                par = cc % 2
                def gmm(e, cc=cc):
                    r = None
                    for pl in range(8):
                        pz = g["pZa" if pl < 4 else "pZb"]
                        c0 = (pl % 4) * 128
                        r = e.matmul(pz[:, c0: c0 + 128],
                                     g["zeros_b"][:], g["kn_b"][0:64, 0:128],
                                     start=True, stop=False, skip_group_check=True)
                        for sub in range(2):
                            h = 2 * pl + sub
                            r0, cs = HSg(h, cc)
                            r = e.matmul(pz[r0:r0 + 64, c0 + r0: c0 + r0 + 64],
                                         g["kn_b"][r0:r0 + 64, cs:cs + 64],
                                         g["qn_b"][r0:r0 + 64, cs:cs + 64],
                                         start=False, stop=(sub == 1), skip_group_check=True)
                    return r
                p.op("tensor", gmm, wait=[group_ready] + PS["pZ"], inc="T")
                _t = p.now("T")
                p.op("vector", lambda e, par=par: e.tensor_tensor(
                    g["Gb" + str(par)][:, 0:512], g["pZa"][:], g["maskG_s"][:, 0:512], ALU.mult),
                    wait=[_t] + p.dma_all("ld"))
                p.op("vector", lambda e, par=par: e.tensor_tensor(
                    g["Gb" + str(par)][:, 512:1024], g["pZb"][:], g["maskG_s"][:, 512:1024],
                    ALU.mult))
                G_ready[par] = p.now("V")
                PS["pZ"] = [G_ready[par]]
                def trvk(e, cc=cc):
                    r = None
                    for pl in range(8):
                        for sub in range(2):
                            h = 2 * pl + sub
                            r0, cs = HSg(h, cc)
                            r = e.matmul(g["pZa"][r0:r0 + 64, pl * 64: pl * 64 + 64],
                                         g["vb_b"][r0:r0 + 64, cs:cs + 64],
                                         g["ident_b"][r0:r0 + 64, r0:r0 + 64],
                                         start=True, stop=True, skip_group_check=True)
                            r = e.matmul(g["pZb"][r0:r0 + 64, pl * 64: pl * 64 + 64],
                                         g["kn_b"][r0:r0 + 64, cs:cs + 64],
                                         g["ident_b"][r0:r0 + 64, r0:r0 + 64],
                                         start=True, stop=True, skip_group_check=True)
                    return r
                p.op("tensor", trvk, wait=[group_ready] + PS["pZ"], inc="T")
                _t = p.now("T")
                p.op("vector", lambda e, par=par: e.tensor_copy(
                    g["VKtm" + str(par)][:, 0:512], g["pZa"][:]), wait=[_t])
                p.op("vector", lambda e, par=par: e.tensor_copy(
                    g["VKtm" + str(par)][:, 512:1024], g["pZb"][:]))
                VK_ready[par] = p.now("V")
                PS["pZ"] = [VK_ready[par]]

            def ser1(cc, c):
                nonlocal S_bf_ready
                par = cc % 2
                if c > 0:
                    def ks0(e, cc=cc):
                        r = None
                        for pl in range(8):
                            for sub in range(2):
                                h = 2 * pl + sub
                                r0, cs = HSg(h, cc)
                                r = e.matmul(g["pM"][r0:r0 + 64, pl * 64: pl * 64 + 64],
                                             g["kb_b"][r0:r0 + 64, cs:cs + 64],
                                             g["S_bf"][r0:r0 + 64, (h // 2) * 64:(h // 2) * 64 + 64],
                                             start=True, stop=True, skip_group_check=True)
                        return r
                    p.op("tensor", ks0, wait=[S_bf_ready, group_ready] + PS["pM"], inc="T")
                    ks0_d = p.now("T")
                    p.op("vector", lambda e, par=par: e.tensor_tensor(
                        g["Yb"][:], g["VKtm" + str(par)][:, 0:512], g["pM"][:], ALU.subtract),
                        wait=[ks0_d, VK_ready[par], PS["Yb_free"]])
                else:
                    p.op("vector", lambda e, par=par: e.tensor_copy(
                        g["Yb"][:], g["VKtm" + str(par)][:, 0:512]),
                        wait=[VK_ready[par], PS["Yb_free"]])
                PS["Yb_done"] = p.now("V")
                PS["pM"] = [PS["Yb_done"]]

            def ser2(cc):
                par = cc % 2
                def umm(e, par=par):
                    r = None
                    for pl in range(8):
                        r = e.matmul(g["pM"][:, pl * 64: pl * 64 + 64],
                                     g["Tt" + str(par)][:, pl * 128: pl * 128 + 128],
                                     g["Yb"][:, pl * 64: pl * 64 + 64],
                                     start=(pl == 0), stop=(pl == 7))
                    return r
                p.op("tensor", umm, wait=[PS["Yb_done"], Tt_ready[par]] + PS["pM"], inc="T")
                _t = p.now("T")
                PS["Yb_free"] = _t
                p.op("scalar", lambda e: e.copy(g["Ub"][:], g["pM"][:]),
                     wait=[_t, PS["Ub_free"]], inc="S")
                PS["Ub_done"] = p.now("S")
                PS["pM"] = [PS["Ub_done"]]

            def ser3(cc, c):
                par = cc % 2
                def omm(e, cc=cc, c=c, par=par):
                    r = None
                    for pl in range(8):
                        if c > 0:
                            for sub in range(2):
                                h = 2 * pl + sub
                                r0, cs = HSg(h, cc)
                                r = e.matmul(g["pO"][r0:r0 + 64, pl * 64: pl * 64 + 64],
                                             g["qn_b"][r0:r0 + 64, cs:cs + 64],
                                             g["S_bf"][r0:r0 + 64, (h // 2) * 64:(h // 2) * 64 + 64],
                                             start=True, stop=False, skip_group_check=True)
                        r = e.matmul(g["pO"][:, pl * 64: pl * 64 + 64],
                                     g["Gb" + str(par)][:, pl * 128: pl * 128 + 128],
                                     g["Ub"][:, pl * 64: pl * 64 + 64],
                                     start=(c == 0), stop=True, skip_group_check=True)
                    return r
                wt = [G_ready[par], PS["Ub_done"]] + PS["pO"]
                if c > 0:
                    wt.append(S_bf_ready)
                p.op("tensor", omm, wait=wt, inc="T")
                p.op("scalar", lambda e: e.copy(g["O_grp"][:], g["pO"][:]),
                     wait=[p.now("T"), PS["O_free"]], inc="S")
                PS["O_done"] = p.now("S")
                PS["pO"] = [PS["O_done"]]

            def ser4(cc, c):
                nonlocal S_bf_ready
                par = cc % 2
                def smm(e, cc=cc, par=par):
                    r = None
                    for pl in range(8):
                        for sub in range(2):
                            h = 2 * pl + sub
                            r0, _ = HSg(h, cc)
                            r = e.matmul(g["pM"][r0:r0 + 64, (h // 2) * 64 - (h // 2) // 8 * 512:
                                                 (h // 2) * 64 - (h // 2) // 8 * 512 + 64],
                                         g["VKtm" + str(par)][r0:r0 + 64, 512 + pl * 64: 512 + pl * 64 + 64],
                                         g["Ub"][r0:r0 + 64, pl * 64: pl * 64 + 64],
                                         start=True, stop=True, skip_group_check=True)
                    return r
                p.op("tensor", smm, wait=[PS["Ub_done"], VK_ready[par]] + PS["pM"], inc="T")
                smm_d = p.now("T")
                PS["Ub_free"] = smm_d
                if c == 0:
                    p.op("vector", lambda e: e.tensor_copy(
                        g["S_f32"][:], g["pM"][:]), wait=[smm_d])
                else:
                    p.op("vector", lambda e: e.tensor_tensor(
                        g["S_f32"][:], g["S_f32"][:], g["pM"][:], ALU.add),
                        wait=[smm_d])
                p.op("vector", lambda e: e.tensor_copy(g["S_bf"][:], g["S_f32"][:]))
                PS["pM"] = [p.now("V")]
                S_bf_ready = p.now("V")

            def tail(cc, c):
                for pl in range(8):
                    p.op("scalar", lambda e, pl=pl: e.activation(
                        g["sq_scr"][:, 0:64],
                        g["O_grp"][:, pl * 64: pl * 64 + 64], AF.Square,
                        accum_out=g["stat2"][:, pl: pl + 1]),
                        wait=[PS["O_done"]] if pl == 0 else (), inc="S")
                p.op("vector", lambda e: e.tensor_scalar(
                    g["stat2"][:, 8:16], g["stat2"][:, 0:8], 1.0 / 64, ATTN_EPS,
                    ALU.mult, ALU.add), wait=[p.now("S")])
                p.op("scalar", lambda e: e.activation(
                    g["stat2"][:, 16:24], g["stat2"][:, 8:16], AF.Ln),
                    wait=[p.now("V")], inc="S")
                p.op("vector", lambda e: e.tensor_scalar(
                    g["stat2"][:, 16:24], g["stat2"][:, 16:24], -0.5, None, ALU.mult),
                    wait=[p.now("S")])
                p.op("scalar", lambda e: e.activation(
                    g["stat2"][:, 24:32], g["stat2"][:, 16:24], AF.Exp),
                    wait=[p.now("V")], inc="S")
                rstdH_done = p.now("S")
                for pl in range(8):
                    p.op("vector", lambda e, pl=pl: e.tensor_scalar(
                        g["o_n"][:, pl * 64: pl * 64 + 64], g["O_grp"][:, pl * 64: pl * 64 + 64],
                        g["stat2"][:, 24 + pl: 25 + pl], None, ALU.mult),
                        wait=[rstdH_done, PS["on_free"]] if pl == 0 else [rstdH_done])
                on_done = p.now("V")
                PS["O_free"] = on_done
                def otr(e):
                    r = None
                    for pl in range(8):
                        for sub in range(2):
                            r0 = sub * 64
                            r = e.matmul(g["pO"][r0:r0 + 64, pl * 64: pl * 64 + 64],
                                         g["o_n"][r0:r0 + 64, pl * 64: pl * 64 + 64],
                                         g["ident_b"][r0:r0 + 64, r0:r0 + 64],
                                         start=True, stop=True, skip_group_check=True)
                    return r
                p.op("tensor", otr, wait=[on_done] + PS["pO"], inc="T")
                otr_d = p.now("T")
                PS["on_free"] = otr_d
                def oev(e, cc=cc):
                    dst = g["o_fm"][:].rearrange("p (pl t) -> p pl t", t=128)
                    return e.tensor_copy(dst[:, :, (cc % 2) * 64:(cc % 2) * 64 + 64],
                                         g["pO"][:, 0:512].rearrange("p (pl t) -> p pl t", t=64))
                p.op("vector", oev, wait=[otr_d])
                ofm_done = p.now("V")
                PS["pO"] = [ofm_done]

                if cc % 2 == 1:
                    def womm(e):
                        r = None
                        for fo in range(ND):
                            pdst = (g["pM"][:, (fo % 4) * 128:(fo % 4) * 128 + 128] if fo < 4
                                    else g["pO"][:, (fo - 4) * 128:(fo - 4) * 128 + 128])
                            for kc in range(ND):
                                r = e.matmul(pdst,
                                             g["wo_bf"][:, kc * 1024 + fo * 128: kc * 1024 + fo * 128 + 128],
                                             g["o_fm"][:, kc * 128: kc * 128 + 128],
                                             start=(kc == 0), stop=(kc == ND - 1))
                        return r
                    p.op("tensor", womm,
                         wait=p.dma_all("wld") + [ofm_done, wo_ready] + PS["pM"] + PS["pO"],
                         inc="T")
                    wodone = p.now("T")
                    def sev(e):
                        r = None
                        for fo in range(ND):
                            psrc = (g["pM"][:, (fo % 4) * 128:(fo % 4) * 128 + 128] if fo < 4
                                    else g["pO"][:, (fo - 4) * 128:(fo - 4) * 128 + 128])
                            r = e.activation(g["s_scr"][:, fo * 128: fo * 128 + 128], psrc,
                                             AF.Identity,
                                             bias=g["bias_qkv"][:, 3 * ND + fo: 3 * ND + fo + 1])
                        return r
                    p.op("scalar", sev,
                         wait=[wodone, PS["sstore"]] if PS["sstore"] else [wodone], inc="S")
                    sfm_done = p.now("S")
                    def strn(e):
                        r = None
                        for fo in range(ND):
                            pdst = (g["pM"][:, (fo % 4) * 128:(fo % 4) * 128 + 128] if fo < 4
                                    else g["pO"][:, (fo - 4) * 128:(fo - 4) * 128 + 128])
                            r = e.matmul(pdst, g["s_scr"][:, fo * 128: fo * 128 + 128],
                                         g["ident_b"][:], start=True, stop=True)
                        return r
                    p.op("tensor", strn, wait=[sfm_done], inc="T")
                    strn_d = p.now("T")
                    p.op("vector", lambda e: e.tensor_copy(g["s_scr"][:, 0:512], g["pM"][:]),
                         wait=[strn_d])
                    p.op("vector", lambda e: e.tensor_copy(g["s_scr"][:, 512:1024], g["pO"][:]))
                    stm_done = p.now("V")
                    PS["pM"] = [stm_done]
                    PS["pO"] = [stm_done]
                    PS["sstore"] = p.op("sync", lambda e, c=c: e.dma_start(
                        out=g["s_tm_d"][(c - 1) * 64:(c + 1) * 64, :], in_=g["s_scr"][:]),
                        wait=[stm_done], dma_="st")

            prep_main(0)
            prep_main(1)
            prep_gtrvk(0)
            for cc in range(8):
                c = grp * 8 + cc
                ser1(cc, c)
                ser2(cc)
                if cc + 2 < 8:
                    prep_main(cc + 2)
                ser3(cc, c)
                ser4(cc, c)
                if cc + 1 < 8:
                    prep_gtrvk(cc + 1)
                tail(cc, c)
            s_store_prev = PS["sstore"]
            pZ_free = [PS["pZ"][-1], PS["pZ"][-1]]
            pW_free = PS["pW"][-1]
            pT_free = PS["pT"][-1]
            pM_free = PS["pM"][-1]
            pO_free = PS["pO"][-1]
            O_free = PS["O_free"]
            prev_group_read = p.now("T")
            prev_conv_v = p.now("V")
        attn_done_list = p.dma_all("st")
        attn_T = p.now("T")
        attn_V = p.now("V")
        attn_S = p.now("S")
        attn_P = p.now("P")
        attn_end = [attn_T, attn_V, attn_S, attn_P] + attn_done_list
        es_attn.close()

        es_ffn = contextlib.ExitStack()

        def sb(name, shape, dtype, _es=es_ffn):
            g[name] = _es.enter_context(nc.sbuf_tensor(name, list(shape), dtype))
            return g[name]

        def ps(name, shape, dtype, _es=es_ffn):
            g[name] = _es.enter_context(nc.psum_tensor(name, list(shape), dtype))
            return g[name]

        sb("gate_s", (128, ND * 16), BF16)
        sb("b1_s", (128, 256), F32)
        sb("b2_s", (16, 1024), BF16)
        sb("b2c", (128, 128), F32)
        sb("gcols", (128, 48), F32)
        sb("fcol_sb", (128, 48), U32)
        sb("ecl_scr", (128, 4), F32)
        sb("iotaF", (128, 16), F32)
        sb("gstat", (128, 64), F32)
        sb("nrm2", (128, 48), F32)
        sb("xt_s", (128, 2048), F32)
        sb("sin_s", (128, 1024), BF16)
        sb("x1_s", (128, 2048), F32)
        sb("sg_s", (128, 1024), BF16)
        sb("gtile", (128, 1024), BF16)
        sb("xn1t", (128, 1024), BF16)
        ps("pH0", (128, 512), F32)
        ps("pH1", (128, 512), F32)
        ps("pOu0", (128, 512), F32)
        ps("pOu1", (128, 512), F32)
        ps("pAux", (128, 512), F32)
        ps("pTr2", (128, 1024), BF16)
        dram("x1_d", (L, D), F32)
        dram("xn1_d", (L, D), BF16)
        dram("rout0_d", (3, L), F32)
        dram("idxmap0", (16 * 1024, 1), U32)
        dram("eo0_d", (16 * 1024, D), BF16)

        for kc in range(ND):
            p.op("gpsimd", lambda e, kc=kc: e.dma_start(
                out=g["gate_s"][:, kc * 16:(kc + 1) * 16],
                in_=g["gate0"][kc * 128:(kc + 1) * 128, :]), wait=attn_end, dma_="wf")
        p.op("sync", lambda e: e.dma_start(out=g["b1_s"][:], in_=g["b1s"]),
             wait=attn_end, dma_="ld")
        p.op("gpsimd", lambda e: e.dma_start(out=g["b2_s"][:], in_=g["b2m"]),
             wait=attn_end, dma_="wf")
        p.op("sync", lambda e: e.dma_start(out=g["iotaF"][:], in_=g["iota16"]),
             wait=attn_end, dma_="ld")
        ffn_ld = p.dma_all("wf") + p.dma_all("ld")
        for dt in range(ND):
            p.op("tensor", lambda e, dt=dt: e.transpose(
                g["pTr2"][:, 0:16], g["b2_s"][0:16, dt * 128:(dt + 1) * 128],
                g["ident_b"][0:16, 0:16]), wait=ffn_ld + [p.now("V")], inc="T")
            p.op("vector", lambda e, dt=dt: e.tensor_copy(
                g["b2c"][:, dt * 16:(dt + 1) * 16], g["pTr2"][:, 0:16]),
                wait=[p.now("T")])
        b2c_done = p.now("V")

        rt0_st = []
        for tt in range(NT):
            def glog(e, tt=tt):
                r = None
                for kc in range(ND):
                    r = e.matmul(g["pAux"][:, 0:16],
                                 g["xnFMb"][:, kc * 2048 + tt * 128: kc * 2048 + tt * 128 + 128],
                                 g["gate_s"][:, kc * 16:(kc + 1) * 16],
                                 start=(kc == 0), stop=(kc == ND - 1))
                return r
            p.op("tensor", glog, wait=ffn_ld + attn_end + [("V", p.scount["V"])], inc="T")
            glog_d = p.now("T")
            GS = g["gstat"]
            p.op("vector", lambda e: e.tensor_reduce(
                GS[:, 0:1], g["pAux"][:, 0:16], AX.X, ALU.max),
                wait=[glog_d] + (rt0_st[-3:] if tt > 0 else []))
            p.op("vector", lambda e: e.tensor_scalar(
                g["nrm2"][:, 0:16], g["pAux"][:, 0:16], GS[:, 0:1], None, ALU.subtract))
            p.op("scalar", lambda e: e.activation(
                g["nrm2"][:, 16:32], g["nrm2"][:, 0:16], AF.Exp), wait=[p.now("V")])
            p.op("vector", lambda e: e.tensor_reduce(
                GS[:, 1:2], g["nrm2"][:, 16:32], AX.X, ALU.add), wait=[p.now("S")])
            p.op("vector", lambda e: e.reciprocal(GS[:, 2:3], GS[:, 1:2]))
            p.op("vector", lambda e: e.tensor_scalar(
                g["nrm2"][:, 0:16], g["nrm2"][:, 16:32], GS[:, 2:3], None, ALU.mult))
            p.op("vector", lambda e: e.tensor_copy(g["nrm2"][:, 16:32], g["nrm2"][:, 0:16]))
            for kk in range(3):
                p.op("vector", lambda e, kk=kk: e.tensor_reduce(
                    GS[:, 3 + kk: 4 + kk], g["nrm2"][:, 16:32], AX.X, ALU.max))
                p.op("vector", lambda e, kk=kk: e.tensor_scalar(
                    g["nrm2"][:, 32:48], g["nrm2"][:, 16:32], GS[:, 3 + kk: 4 + kk],
                    None, ALU.is_ge))
                p.op("vector", lambda e: e.tensor_scalar(
                    g["nrm2"][:, 32:48], g["nrm2"][:, 32:48], -99.0, 99.0,
                    ALU.mult, ALU.add))
                p.op("vector", lambda e: e.tensor_tensor(
                    g["nrm2"][:, 32:48], g["nrm2"][:, 32:48], g["iotaF"][:], ALU.add))
                p.op("vector", lambda e, kk=kk: e.tensor_reduce(
                    g["ecl_scr"][:, kk:kk + 1], g["nrm2"][:, 32:48], AX.X, ALU.min))
                p.op("vector", lambda e, kk=kk: e.tensor_scalar(
                    g["nrm2"][:, 32:48], g["iotaF"][:], g["ecl_scr"][:, kk:kk + 1],
                    None, ALU.is_equal))
                p.op("vector", lambda e: e.tensor_scalar(
                    g["nrm2"][:, 32:48], g["nrm2"][:, 32:48], 9.0, None, ALU.mult))
                p.op("vector", lambda e: e.tensor_tensor(
                    g["nrm2"][:, 16:32], g["nrm2"][:, 16:32], g["nrm2"][:, 32:48],
                    ALU.subtract))
            p.op("vector", lambda e: e.tensor_tensor(
                GS[:, 6:7], GS[:, 3:4], GS[:, 4:5], ALU.add))
            p.op("vector", lambda e: e.tensor_tensor(
                GS[:, 6:7], GS[:, 6:7], GS[:, 5:6], ALU.add))
            p.op("vector", lambda e: e.tensor_scalar(
                GS[:, 6:7], GS[:, 6:7], 1e-5, None, ALU.max))
            p.op("vector", lambda e: e.reciprocal(GS[:, 7:8], GS[:, 6:7]))
            p.op("vector", lambda e, tt=tt: e.tensor_scalar(
                g["gcols"][:, tt * 3: tt * 3 + 3], GS[:, 3:6], GS[:, 7:8],
                None, ALU.mult))
            rpk = p.now("V")
            for kk in range(3):
                rt0_st.append(p.op("sync", lambda e, tt=tt, kk=kk: e.dma_start(
                    out=g["rout0_d"][kk, tt * 128:(tt + 1) * 128].rearrange(
                        "(p one) -> p one", one=1),
                    in_=g["ecl_scr"][:, kk:kk + 1]), wait=[rpk], dma_="fst"))
        gate_done = p.now("V")

        es_rt0 = contextlib.ExitStack()

        def sbr(name, shape, dtype, _es=es_rt0):
            g[name] = _es.enter_context(nc.sbuf_tensor(name, list(shape), dtype, side="right"))

        for nm in ("R1", "R2", "R3", "R4", "R5"):
            sbr(nm, (48, 2048), F32)
        sbr("r3b", (3, 2048), BF16)
        sbr("ex48sum_s", (48, 48), F32)
        sbr("exgt48_s", (48, 48), F32)
        sbr("exblk48_s", (48, 48), F32)
        sbr("ecol48_s", (48, 1), F32)
        sbr("acol48_s", (48, 1), F32)
        sbr("sel3_s", (3, 48), BF16)
        sbr("selx_s", (48, 3), F32)
        sbr("zinitF", (128, 128), U32)
        sbr("tokids0", (128, 16), U32)
        sbr("c3s", (128, 4), F32)
        for nm in ("ex48sum", "exgt48", "exblk48", "ecol48", "acol48", "sel3", "selx"):
            p.op("sync", lambda e, nm=nm: e.dma_start(out=g[nm + "_s"][:], in_=g[nm]),
                 wait=attn_end, dma_="ld")
        rc_ld = p.dma_all("ld")
        bcF = [None, None]

        def mkregsF(e):
            bcF[0] = e.alloc_register(name="bf0")
            e.reg_mov(bcF[0], L - 1)
            bcF[1] = e.alloc_register(name="bf1")
            e.reg_mov(bcF[1], 16 * 1024 - 1)
            return None
        p.op("gpsimd", mkregsF)
        p.op("gpsimd", lambda e: e.memset(g["zinitF"][:], 10000000), wait=attn_end)
        p.op("gpsimd", lambda e: e.iota(
            g["tokids0"][:], [[128, 16]], base=0, channel_multiplier=1))
        ztok_d = p.now("P")
        i0init = [p.op("sync", lambda e: e.dma_start(
            out=g["idxmap0"].rearrange("(p t) one -> p (t one)", p=128),
            in_=g["zinitF"][:]), wait=[ztok_d], dma_="fst")]
        p.op("gpsimd", lambda e: e.dma_start(out=g["r3b"][0:3, :], in_=g["rout0_d"]),
             wait=rt0_st, dma_="gld")
        r3_ld = p.dma_all("gld") + p.dma_all("ld")
        for q in range(4):
            p.op("tensor", lambda e, q=q: e.matmul(
                g["pH0"][0:48, 0:512], g["sel3_s"][:],
                g["r3b"][0:3, q * 512:(q + 1) * 512],
                start=True, stop=True), wait=r3_ld + [p.now("V")], inc="T")
            p.op("vector", lambda e, q=q: e.tensor_copy(
                g["R1"][:, q * 512:(q + 1) * 512], g["pH0"][0:48, 0:512]),
                wait=[p.now("T")])
        p.op("vector", lambda e: e.tensor_scalar(
            g["R2"][:], g["R1"][:], g["ecol48_s"][:, 0:1], None, ALU.is_equal),
            wait=rc_ld)
        p.op("vector", lambda e: e.tensor_copy(g["R3"][:], g["R2"][:]))
        cur, nxt = "R3", "R4"
        for kk in range(11):
            sh = 1 << kk
            p.op("vector", lambda e, cur=cur, nxt=nxt, sh=sh: e.tensor_copy(
                g[nxt][:, 0:sh], g[cur][:, 0:sh]))
            p.op("vector", lambda e, cur=cur, nxt=nxt, sh=sh: e.tensor_tensor(
                g[nxt][:, sh:2048], g[cur][:, sh:2048], g[cur][:, 0:2048 - sh], ALU.add))
            cur, nxt = nxt, cur
        if cur != "R3":
            p.op("vector", lambda e: e.tensor_copy(g["R3"][:], g["R4"][:]))
        cs0_d = p.now("V")
        for srcn, dstn, mat in (("R3", "R5", "ex48sum_s"), ("R2", "R4", "exgt48_s")):
            for q in range(4):
                p.op("tensor", lambda e, srcn=srcn, mat=mat, q=q: e.matmul(
                    g["pH0"][0:48, 0:512], g[mat][:],
                    g[srcn][:, q * 512:(q + 1) * 512],
                    start=True, stop=True), wait=[cs0_d, p.now("V")], inc="T")
                p.op("vector", lambda e, dstn=dstn, q=q: e.tensor_copy(
                    g[dstn][:, q * 512:(q + 1) * 512], g["pH0"][0:48, 0:512]),
                    wait=[p.now("T")])
        p.op("vector", lambda e: e.tensor_tensor(g["R5"][:], g["R5"][:], g["R4"][:], ALU.subtract))
        p.op("vector", lambda e: e.tensor_tensor(g["R5"][:], g["R5"][:], g["R2"][:], ALU.mult))
        p.op("vector", lambda e: e.tensor_scalar(
            g["R4"][:], g["R2"][:], g["acol48_s"][:, 0:1], None, ALU.mult))
        nn_d = p.now("V")
        for srcn, dstn in (("R5", "R1"), ("R4", "R4")):
            for q in range(4):
                p.op("tensor", lambda e, srcn=srcn, q=q: e.matmul(
                    g["pH0"][0:48, 0:512], g["exblk48_s"][:],
                    g[srcn][:, q * 512:(q + 1) * 512],
                    start=True, stop=True), wait=[nn_d, p.now("V")], inc="T")
                p.op("vector", lambda e, dstn=dstn, q=q: e.tensor_copy(
                    g[dstn][:, q * 512:(q + 1) * 512], g["pH0"][0:48, 0:512]),
                    wait=[p.now("T")])
        p.op("vector", lambda e: e.tensor_scalar(
            g["R3"][:], g["R1"][:], 1024.5, None, ALU.is_gt))
        p.op("vector", lambda e: e.tensor_scalar(
            g["R3"][:], g["R3"][:], 1.0e7, None, ALU.mult))
        p.op("vector", lambda e: e.tensor_tensor(g["R5"][:], g["R1"][:], g["R4"][:], ALU.add))
        p.op("vector", lambda e: e.tensor_tensor(g["R5"][:], g["R5"][:], g["R3"][:], ALU.add))
        p.op("vector", lambda e: e.tensor_scalar(
            g["R5"][:], g["R5"][:], -1.0, None, ALU.add))
        flat0_d = p.now("V")
        for tr in range(NT):
            p.op("tensor", lambda e, tr=tr: e.matmul(
                g["pH0"][:, 0:3],
                g["R5"][:, tr * 128:(tr + 1) * 128],
                g["selx_s"][:],
                start=True, stop=True), wait=[flat0_d, p.now("V")], inc="T")
            p.op("vector", lambda e, tr=tr: e.tensor_copy(
                g["c3s"][:, 0:3], g["pH0"][:, 0:3]), wait=[p.now("T")])
            p.op("vector", lambda e, tr=tr: e.tensor_copy(
                g["fcol_sb"][:, tr * 3: tr * 3 + 3], g["c3s"][:, 0:3]))
            sc_v = p.now("V")
            for kk in range(3):
                p.op("gpsimd", lambda e, tr=tr, kk=kk: e.indirect_dma_start(
                    out=g["idxmap0"],
                    out_offset=IndirectOffsetOnAxis(
                        ap=g["fcol_sb"][:, tr * 3 + kk: tr * 3 + kk + 1], axis=0),
                    in_=g["tokids0"][:, tr: tr + 1],
                    in_offset=None,
                    bounds_check=bcF[1], oob_is_err=False),
                    wait=[sc_v, ztok_d] + i0init, dma_="msc")
        sc0_done = p.dma_all("msc") + i0init
        rt0_T = p.now("T")
        rt0_V = p.now("V")
        es_rt0.close()

        sb("xgt0", (128, 2048), BF16)
        sb("xgfA", (128, ND * 1024), BF16)
        sb("xgfB", (128, ND * 1024), BF16)
        sb("w_ring", (128, 4 * 8192), BF16)
        sb("h_buf", (128, 16 * 512), BF16)
        sb("eofm0", (128, ND * 512), BF16)
        sb("eot0", (128, 1024), BF16)
        sb("obufF", (128, 8), U32)
        sb("obufG", (128, 8), U32)

        EE = n_exp
        slot_free = [[rt0_T], [rt0_T], [rt0_T], [rt0_T]]
        xgf_free = {0: [rt0_T, rt0_V], 1: [rt0_T, rt0_V]}
        xgf_ready = {0: None, 1: None}
        eo_st = []
        prev_h_read = rt0_T
        gat_prev = []
        gat_d = []
        tr_prev = []
        for rt in range(8):
            ob = "obufF" if rt % 2 == 0 else "obufG"
            xo = (rt % 2) * 1024
            ld = p.op("sync", lambda e, rt=rt, ob=ob: e.dma_start(
                out=g[ob][:, 0:1],
                in_=g["idxmap0"][rt * 128: rt * 128 + 128, :]),
                wait=sc0_done + (gat_d[-2:-1] if rt >= 2 else []), dma_="ld")
            p.op("vector", lambda e, xo=xo: e.memset(g["xgt0"][:, xo:xo + 1024], 0.0),
                 wait=[ld] + (xgf_free[0] if rt == 0 else []) + tr_prev)
            z_d = p.now("V")
            gt = p.op("gpsimd", lambda e, ob=ob, xo=xo: e.indirect_dma_start(
                out=g["xgt0"][:, xo:xo + 1024], out_offset=None,
                in_=g["xn_d"],
                in_offset=IndirectOffsetOnAxis(ap=g[ob][:, 0:1], axis=0),
                bounds_check=bcF[0], oob_is_err=False),
                wait=[z_d], dma_="gld")
            def xtr0(e, xo=xo):
                r = None
                for fo in range(ND):
                    r = e.transpose(g["pTr2"][:, fo * 128: fo * 128 + 128],
                                    g["xgt0"][:, xo + fo * 128: xo + fo * 128 + 128],
                                    g["ident_b"][:])
                return r
            p.op("tensor", xtr0, wait=[gt, p.now("V")], inc="T")
            tr_prev = [p.now("T")]
            def xev0(e, rt=rt):
                dst = g["xgfA"][:].rearrange("p (f t) -> p f t", t=1024)
                return e.tensor_copy(dst[:, :, rt * 128:(rt + 1) * 128],
                                     g["pTr2"][:].rearrange("p (f t) -> p f t", t=128))
            p.op("vector", xev0, wait=[p.now("T")])
            gat_d.append(p.now("V"))
        gat_prev = gat_d[-2:]
        xgf_ready[0] = p.now("V")
        for ex in range(EE):
            par = ex % 2
            xgf = "xgfA" if par == 0 else "xgfB"
            gex = ex + 1
            if gex < EE:
                gpar = gex % 2
                gxgf = "xgfA" if gpar == 0 else "xgfB"
                gat_d = []
                tr_prev = []
                for rt in range(8):
                    ob = "obufF" if rt % 2 == 0 else "obufG"
                    xo = (rt % 2) * 1024
                    ld = p.op("scalar", lambda e, gex=gex, rt=rt, ob=ob: e.dma_start(
                        out=g[ob][:, 0:1],
                        in_=g["idxmap0"][gex * 1024 + rt * 128: gex * 1024 + rt * 128 + 128, :]),
                        wait=sc0_done + (gat_d[-2:-1] if rt >= 2 else gat_prev), dma_="ld")
                    p.op("vector", lambda e, xo=xo: e.memset(g["xgt0"][:, xo:xo + 1024], 0.0),
                         wait=[ld] + (xgf_free[gpar] if rt == 0 else []) + tr_prev)
                    z_d = p.now("V")
                    gt = p.op("gpsimd", lambda e, ob=ob, xo=xo: e.indirect_dma_start(
                        out=g["xgt0"][:, xo:xo + 1024], out_offset=None,
                        in_=g["xn_d"],
                        in_offset=IndirectOffsetOnAxis(ap=g[ob][:, 0:1], axis=0),
                        bounds_check=bcF[0], oob_is_err=False),
                        wait=[z_d], dma_="gld")
                    def xtr0(e, xo=xo):
                        r = None
                        for fo in range(ND):
                            r = e.transpose(g["pTr2"][:, fo * 128: fo * 128 + 128],
                                            g["xgt0"][:, xo + fo * 128: xo + fo * 128 + 128],
                                            g["ident_b"][:])
                        return r
                    p.op("tensor", xtr0, wait=[gt, p.now("V")], inc="T")
                    tr_prev = [p.now("T")]
                    def xev0(e, rt=rt, gxgf=gxgf):
                        dst = g[gxgf][:].rearrange("p (f t) -> p f t", t=1024)
                        return e.tensor_copy(dst[:, :, rt * 128:(rt + 1) * 128],
                                             g["pTr2"][:].rearrange("p (f t) -> p f t", t=128))
                    p.op("vector", xev0, wait=[p.now("T")])
                    gat_d.append(p.now("V"))
                gat_prev = gat_d[-2:]
                xgf_ready[gpar] = p.now("V")
            xgf_done = xgf_ready[par]
            slab_ld = []
            for si, (wt_name, r0, c0) in enumerate((
                    ("w1b_d", 0, 0), ("w1b_d", 0, 1024),
                    ("w2b_d", 0, 0), ("w2b_d", 1024, 0))):
                ent = []
                for kc in range(ND):
                    ent.append(p.op("sync", lambda e, ex=ex, si=si, kc=kc, wt_name=wt_name,
                                    r0=r0, c0=c0: e.dma_start(
                        out=g["w_ring"][:, si * 8192 + kc * 1024: si * 8192 + kc * 1024 + 1024],
                        in_=g[wt_name][ex, r0 + kc * 128: r0 + kc * 128 + 128, c0:c0 + 1024]),
                        wait=slot_free[si] + cvt_done[ex], dma_="wfs"))
                slab_ld.append(ent)
            for tc in range(2):
                for ht in range(16):
                    si = 0 if ht < 8 else 1
                    pH = "pH0" if ht % 2 == 0 else "pH1"
                    def w1mm(e, tc=tc, ht=ht, si=si, pH=pH, xgf=xgf):
                        r = None
                        for kc in range(ND):
                            r = e.matmul(
                                g[pH][:],
                                g["w_ring"][:, si * 8192 + kc * 1024 + (ht % 8) * 128:
                                            si * 8192 + kc * 1024 + (ht % 8) * 128 + 128],
                                g[xgf][:, kc * 1024 + tc * 512: kc * 1024 + tc * 512 + 512],
                                start=(kc == 0), stop=(kc == ND - 1))
                        return r
                    wt = list(slab_ld[si]) + [xgf_done, prev_h_read]
                    if ht >= 2:
                        wt.append(hrelu[ht - 2])
                    p.op("tensor", w1mm, wait=wt, inc="T")
                    w1_d = p.now("T")
                    p.op("scalar", lambda e, ex=ex, ht=ht, pH=pH: e.activation(
                        g["h_buf"][:, ht * 512:(ht + 1) * 512], g[pH][:], AF.Relu,
                        bias=g["b1_s"][:, ex * 16 + ht: ex * 16 + ht + 1]),
                        wait=[w1_d], inc="S")
                    if ht == 0:
                        hrelu = {}
                    hrelu[ht] = p.now("S")
                hdone = p.now("S")
                if tc == 1:
                    w1_lastuse = p.now("T")
                for dt in range(ND):
                    pOu = "pOu0" if dt % 2 == 0 else "pOu1"
                    def w2mm(e, dt=dt, pOu=pOu):
                        r = None
                        for hc in range(16):
                            si = 2 if hc < 8 else 3
                            r = e.matmul(
                                g[pOu][:],
                                g["w_ring"][:, si * 8192 + (hc % 8) * 1024 + dt * 128:
                                            si * 8192 + (hc % 8) * 1024 + dt * 128 + 128],
                                g["h_buf"][:, hc * 512:(hc + 1) * 512],
                                start=(hc == 0), stop=(hc == 15))
                        return r
                    wt = list(slab_ld[2]) + list(slab_ld[3]) + [hdone]
                    if dt >= 2:
                        wt.append(eoev[dt - 2])
                    p.op("tensor", w2mm, wait=wt, inc="T")
                    w2_d = p.now("T")
                    p.op("scalar", lambda e, ex=ex, dt=dt, pOu=pOu: e.activation(
                        g["eofm0"][:, dt * 512:(dt + 1) * 512], g[pOu][:], AF.Identity,
                        bias=g["b2c"][:, dt * 16 + ex: dt * 16 + ex + 1]),
                        wait=[w2_d, b2c_done], inc="S")
                    if dt == 0:
                        eoev = {}
                    eoev[dt] = p.now("S")
                eofm_done = p.now("S")
                prev_h_read = p.now("T")
                for st4 in range(4):
                    def etr0(e, st4=st4):
                        r = None
                        for dt in range(ND):
                            r = e.transpose(g["pTr2"][:, dt * 128: dt * 128 + 128],
                                            g["eofm0"][:, dt * 512 + st4 * 128: dt * 512 + st4 * 128 + 128],
                                            g["ident_b"][:])
                        return r
                    p.op("tensor", etr0, wait=[eofm_done, p.now("V")], inc="T")
                    p.op("vector", lambda e: e.tensor_copy(g["eot0"][:], g["pTr2"][:]),
                         wait=[p.now("T")] + eo_st[-1:])
                    ev_d = p.now("V")
                    eo_st.append(p.op("sync", lambda e, ex=ex, tc=tc, st4=st4: e.dma_start(
                        out=g["eo0_d"][ex * 1024 + tc * 512 + st4 * 128:
                                       ex * 1024 + tc * 512 + st4 * 128 + 128, :],
                        in_=g["eot0"][:]), wait=[ev_d], dma_="fst"))
            xgf_free[par] = [p.now("T")]
            slot_free = [[w1_lastuse], [w1_lastuse], [p.now("T")], [p.now("T")]]
        eo0_all = p.dma_all("fst")

        sb("grw0", (128, 1024), BF16)
        sb("grw1", (128, 1024), BF16)
        sb("grw2", (128, 1024), BF16)
        sb("combF", (128, 1024), F32)
        SUM, SQ, MEAN1, RSTD1, TMP1 = 0, NT, 2 * NT, 3 * NT, 4 * NT
        xin_prev = {}
        sinrd = {}
        x1store = {}
        xn1store = {}
        for tt in range(NT):
            sl = tt % 2
            wt_in = [gate_done] if tt < 2 else [xin_prev[tt - 2]]
            p.op("sync", lambda e, tt=tt, sl=sl: e.dma_start(
                out=g["xt_s"][:, sl * 1024:(sl + 1) * 1024],
                in_=g["x"][tt * 128:(tt + 1) * 128, :]), wait=wt_in, dma_="ld")
            sld = p.op("sync", lambda e, tt=tt: e.dma_start(
                out=g["sin_s"][:],
                in_=g["s_tm_d"][tt * 128:(tt + 1) * 128, :]),
                wait=wt_in + attn_done_list +
                     ([sinrd[tt - 1]] if tt >= 1 else []), dma_="ld")
            tile_ld = p.dma_all("ld")
            gts = []
            for kk in range(3):
                p.op("vector", lambda e, kk=kk: e.memset(g["grw" + str(kk)][:], 0.0),
                     wait=(eo0_all + [rt0_V] if tt == 0 and kk == 0 else []))
                z_d = p.now("V")
                gts.append(p.op("gpsimd", lambda e, tt=tt, kk=kk: e.indirect_dma_start(
                    out=g["grw" + str(kk)][:], out_offset=None,
                    in_=g["eo0_d"],
                    in_offset=IndirectOffsetOnAxis(
                        ap=g["fcol_sb"][:, tt * 3 + kk: tt * 3 + kk + 1], axis=0),
                    bounds_check=bcF[1], oob_is_err=False),
                    wait=[z_d] + eo0_all, dma_="gld"))
            p.op("vector", lambda e, tt=tt: e.tensor_scalar(
                g["combF"][:], g["grw0"][:], g["gcols"][:, tt * 3: tt * 3 + 1],
                None, ALU.mult), wait=gts + [gate_done])
            p.op("vector", lambda e, tt=tt: e.scalar_tensor_tensor(
                g["combF"][:], g["grw1"][:], g["gcols"][:, tt * 3 + 1: tt * 3 + 2],
                g["combF"][:], ALU.mult, ALU.add))
            p.op("vector", lambda e, tt=tt: e.scalar_tensor_tensor(
                g["combF"][:], g["grw2"][:], g["gcols"][:, tt * 3 + 2: tt * 3 + 3],
                g["combF"][:], ALU.mult, ALU.add))
            cmb0_d = p.now("V")
            p.op("scalar", lambda e: e.activation(
                g["gtile"][:], g["combF"][:], AF.Silu), wait=[cmb0_d], inc="S")
            gt_d = p.now("S")
            p.op("vector", lambda e: e.tensor_tensor(
                g["sg_s"][:], g["sin_s"][:], g["gtile"][:],
                ALU.mult), wait=[gt_d, sld] + tile_ld)
            sinrd[tt] = p.now("V")
            p.op("vector", lambda e, sl=sl: e.tensor_tensor(
                g["x1_s"][:, sl * 1024:(sl + 1) * 1024], g["xt_s"][:, sl * 1024:(sl + 1) * 1024],
                g["sg_s"][:], ALU.add),
                wait=[x1store[tt - 2]] if tt >= 2 else ())
            x1_d_v = p.now("V")
            p.op("scalar", lambda e, tt=tt, sl=sl: e.activation(
                g["sq_scr"][:, 0:1024], g["x1_s"][:, sl * 1024:(sl + 1) * 1024],
                AF.Square, accum_out=g["stat"][:, SQ + tt: SQ + tt + 1]), wait=[x1_d_v])
            p.op("scalar", lambda e, tt=tt, sl=sl: e.activation(
                g["sq_scr"][:, 0:1024], g["x1_s"][:, sl * 1024:(sl + 1) * 1024],
                AF.Copy, accum_out=g["stat"][:, SUM + tt: SUM + tt + 1]), inc="S")
            st_d = p.now("S")
            p.op("vector", lambda e, tt=tt: e.tensor_scalar(
                g["stat"][:, MEAN1 + tt: MEAN1 + tt + 1], g["stat"][:, SUM + tt: SUM + tt + 1],
                1.0 / D, None, ALU.mult), wait=[st_d])
            p.op("vector", lambda e, tt=tt: e.tensor_tensor(
                g["stat"][:, TMP1 + tt: TMP1 + tt + 1], g["stat"][:, MEAN1 + tt: MEAN1 + tt + 1],
                g["stat"][:, MEAN1 + tt: MEAN1 + tt + 1], ALU.mult))
            p.op("vector", lambda e, tt=tt: e.tensor_scalar(
                g["stat"][:, RSTD1 + tt: RSTD1 + tt + 1], g["stat"][:, SQ + tt: SQ + tt + 1],
                1.0 / D, None, ALU.mult))
            p.op("vector", lambda e, tt=tt: e.tensor_tensor(
                g["stat"][:, RSTD1 + tt: RSTD1 + tt + 1], g["stat"][:, RSTD1 + tt: RSTD1 + tt + 1],
                g["stat"][:, TMP1 + tt: TMP1 + tt + 1], ALU.subtract))
            p.op("vector", lambda e, tt=tt: e.tensor_scalar(
                g["stat"][:, RSTD1 + tt: RSTD1 + tt + 1], g["stat"][:, RSTD1 + tt: RSTD1 + tt + 1],
                LN_EPS, None, ALU.add))
            p.op("scalar", lambda e, tt=tt: e.activation(
                g["stat"][:, TMP1 + tt: TMP1 + tt + 1], g["stat"][:, RSTD1 + tt: RSTD1 + tt + 1],
                AF.Ln), wait=[p.now("V")])
            p.op("vector", lambda e, tt=tt: e.tensor_scalar(
                g["stat"][:, TMP1 + tt: TMP1 + tt + 1], g["stat"][:, TMP1 + tt: TMP1 + tt + 1],
                -0.5, None, ALU.mult), wait=[p.now("S")])
            p.op("scalar", lambda e, tt=tt: e.activation(
                g["stat"][:, RSTD1 + tt: RSTD1 + tt + 1], g["stat"][:, TMP1 + tt: TMP1 + tt + 1],
                AF.Exp), wait=[p.now("V")], inc="S")
            p.op("vector", lambda e, tt=tt, sl=sl: e.tensor_scalar(
                g["xn1t"][:], g["x1_s"][:, sl * 1024:(sl + 1) * 1024],
                g["stat"][:, MEAN1 + tt: MEAN1 + tt + 1],
                g["stat"][:, RSTD1 + tt: RSTD1 + tt + 1],
                ALU.subtract, ALU.mult),
                wait=[p.now("S")] + ([xn1store[tt - 1]] if tt >= 1 else []))
            xn1_v = p.now("V")
            x1store[tt] = p.op("sync", lambda e, tt=tt, sl=sl: e.dma_start(
                out=g["x1_d"][tt * 128:(tt + 1) * 128, :],
                in_=g["x1_s"][:, sl * 1024:(sl + 1) * 1024]), wait=[xn1_v], dma_="st")
            xn1store[tt] = p.op("sync", lambda e, tt=tt: e.dma_start(
                out=g["xn1_d"][tt * 128:(tt + 1) * 128, :], in_=g["xn1t"][:]),
                wait=[xn1_v], dma_="st")
            xin_prev[tt] = xn1store[tt]
        l0_done = p.dma_all("st")
        l0_T = p.now("T")
        l0_V = p.now("V")
        l0_S = p.now("S")
        l0_end = [l0_T, l0_V, l0_S] + l0_done
        es_ffn.close()
        es_xnf.close()

        es_l1 = contextlib.ExitStack()

        def sb(name, shape, dtype, _es=es_l1):
            g[name] = _es.enter_context(nc.sbuf_tensor(name, list(shape), dtype))
            return g[name]

        def ps(name, shape, dtype, _es=es_l1):
            g[name] = _es.enter_context(nc.psum_tensor(name, list(shape), dtype))
            return g[name]

        sb("xn1fm", (128, ND * 2056), BF16)
        sb("cnv1", (128, ND * 2048), BF16)
        sb("wl1", (128, 8 * 128 + 8 * 64 + 8 * 1024 + 8 * 16), BF16)
        sb("cwb", (128, ND * 4 + 3 * ND), F32)
        sb("colv", (64, 16), F32)
        sb("dpow", (64, 16), F32)
        sb("BCb", (64, 2 * 2048), BF16)
        sb("u_f", (64, 2048), F32)
        sb("w_f", (64, 2048), F32)
        sb("hp0", (64, 2048), F32)
        sb("hp1", (64, 2048), F32)
        sb("y_b", (64, 2048), BF16)
        sb("scr1", (64, 2048), F32)
        sb("s1scr", (128, 1024), BF16)
        sb("s1fm", (128, ND * 512), BF16)
        sb("xt1", (128, 2048), BF16)
        sb("xnf_t", (128, 1024), BF16)
        sb("nwb", (128, D), BF16)
        sb("rb16", (128, 16), F32)
        sb("iota_s", (128, 16), F32)
        sb("rstat", (128, 64), F32)
        sb("rcl", (128, 128), F32)
        sb("rlg", (128, 256), F32)
        sb("rsel", (128, 256), F32)
        sb("ones_s", (1, 128), BF16)
        ps("q1", (128, 512), F32)
        ps("q2", (128, 512), F32)
        ps("q3", (128, 512), F32)
        ps("qTr", (128, 1024), BF16)
        dram("s1_d", (L, D), BF16)
        dram("xnf_d", (L, D), BF16)
        dram("rout_d", (4, L), F32)

        WPo, WSo, WOo, RWo = 0, 8 * 128, 8 * 128 + 8 * 64, 8 * 128 + 8 * 64 + 8 * 1024
        for kc in range(ND):
            p.op("gpsimd", lambda e, kc=kc: e.dma_start(
                out=g["wl1"][:, WPo + kc * 128: WPo + kc * 128 + 128],
                in_=g["WpN"][kc * 128:(kc + 1) * 128, :]), wait=l0_end, dma_="wf")
            p.op("gpsimd", lambda e, kc=kc: e.dma_start(
                out=g["wl1"][:, WSo + kc * 64: WSo + kc * 64 + 64],
                in_=g["WsN"][kc * 128:(kc + 1) * 128, :]), wait=l0_end, dma_="wf")
            p.op("gpsimd", lambda e, kc=kc: e.dma_start(
                out=g["wl1"][:, RWo + kc * 16: RWo + kc * 16 + 16],
                in_=g["rwT"][kc * 128:(kc + 1) * 128, :]), wait=l0_end, dma_="wf")
        p.op("gpsimd", lambda e: e.dma_start(
            out=g["wl1"][0:64, WOo: WOo + 1024], in_=g["Wo1"]), wait=l0_end, dma_="wf")
        p.op("sync", lambda e: e.dma_start(
            out=g["cwb"][:, 0:ND * 4], in_=g["cw1"]), wait=l0_end, dma_="ld")
        p.op("sync", lambda e: e.dma_start(
            out=g["cwb"][:, ND * 4:ND * 5], in_=g["cb1"]), wait=l0_end, dma_="ld")
        p.op("sync", lambda e: e.dma_start(
            out=g["cwb"][:, ND * 5:ND * 6], in_=g["bo1"]), wait=l0_end, dma_="ld")
        p.op("sync", lambda e: e.dma_start(
            out=g["cwb"][:, ND * 6:ND * 7], in_=g["nw128"]), wait=l0_end, dma_="ld")
        p.op("sync", lambda e: e.dma_start(out=g["colv"][:, 0:1], in_=g["bpB"]),
             wait=l0_end, dma_="ld")
        p.op("sync", lambda e: e.dma_start(out=g["colv"][:, 1:2], in_=g["bpC"]),
             wait=l0_end, dma_="ld")
        p.op("sync", lambda e: e.dma_start(out=g["colv"][:, 2:3], in_=g["bs1"]),
             wait=l0_end, dma_="ld")
        p.op("sync", lambda e: e.dma_start(out=g["colv"][:, 3:5], in_=g["adt"]),
             wait=l0_end, dma_="ld")
        p.op("sync", lambda e: e.dma_start(out=g["iota_s"][:], in_=g["iota16"]),
             wait=l0_end, dma_="ld")
        p.op("gpsimd", lambda e: e.dma_start(out=g["ones_s"][:], in_=g["ones1"]),
             wait=l0_end, dma_="wf")
        l1_ld = p.dma_all("wf") + p.dma_all("ld")
        p.op("gpsimd", lambda e: e.memset(g["xn1fm"][:], 0.0), wait=l0_end)
        pad1_z = p.now("P")

        for kc in range(ND):
            p.op("vector", lambda e, kc=kc: e.tensor_scalar(
                g["wl1"][:, RWo + kc * 16: RWo + kc * 16 + 16],
                g["wl1"][:, RWo + kc * 16: RWo + kc * 16 + 16],
                g["cwb"][:, ND * 6 + kc: ND * 6 + kc + 1], None, ALU.mult),
                wait=l1_ld if kc == 0 else ())
        wrnw_done = p.now("V")

        p.op("gpsimd", lambda e: e.dma_start(out=g["xnf_t"][0:1, 0:1024], in_=g["normw1"]),
             wait=l0_end, dma_="wf")
        nw_ld = p.dma_all("wf")
        for hb in range(2):
            p.op("tensor", lambda e, hb=hb: e.matmul(
                g["q1"][:], g["ones_s"][:],
                g["xnf_t"][0:1, hb * 512:(hb + 1) * 512].bitcast(BF16) if False else
                g["xnf_t"][0:1, hb * 512:(hb + 1) * 512],
                start=True, stop=True), wait=nw_ld + [pad1_z, p.now("V")], inc="T")
            p.op("vector", lambda e, hb=hb: e.tensor_copy(
                g["nwb"][:, hb * 512:(hb + 1) * 512], g["q1"][:]), wait=[p.now("T")])
        p.op("gpsimd", lambda e: e.dma_start(out=g["rb16"][0:1, 0:16], in_=g["rbias"]),
             wait=l0_end, dma_="wf")
        rb_ld = p.dma_all("wf")
        p.op("vector", lambda e: e.tensor_copy(g["xnf_t"][0:1, 0:16], g["rb16"][0:1, 0:16]),
             wait=rb_ld)
        rbcast_src = p.now("V")
        p.op("tensor", lambda e: e.matmul(
            g["q1"][:, 0:16], g["ones_s"][:], g["xnf_t"][0:1, 0:16],
            start=True, stop=True), wait=[rbcast_src], inc="T")
        p.op("vector", lambda e: e.tensor_copy(g["rb16"][:], g["q1"][:, 0:16]),
             wait=[p.now("T")])
        nwb_done = p.now("V")

        p.op("scalar", lambda e: e.activation(
            g["colv"][:, 7:9], g["colv"][:, 3:5], AF.Abs), wait=l1_ld + [p.now("V")])
        p.op("vector", lambda e: e.tensor_scalar(
            g["colv"][:, 7:9], g["colv"][:, 7:9], -1.0, None, ALU.mult), wait=[p.now("S")])
        p.op("scalar", lambda e: e.activation(
            g["colv"][:, 7:9], g["colv"][:, 7:9], AF.Exp), wait=[p.now("V")])
        p.op("scalar", lambda e: e.activation(
            g["colv"][:, 7:9], g["colv"][:, 7:9], AF.Ln, bias=1.0))
        p.op("vector", lambda e: e.tensor_scalar(
            g["colv"][:, 9:11], g["colv"][:, 3:5], 0.0, None, ALU.max), wait=[p.now("S")])
        p.op("vector", lambda e: e.tensor_tensor(
            g["colv"][:, 9:11], g["colv"][:, 9:11], g["colv"][:, 7:9], ALU.add))
        p.op("vector", lambda e: e.tensor_tensor(
            g["colv"][:, 11:12], g["colv"][:, 9:10], g["colv"][:, 10:11], ALU.mult))
        p.op("vector", lambda e: e.tensor_scalar(
            g["colv"][:, 11:12], g["colv"][:, 11:12], -1.0, 1.0, ALU.mult, ALU.add))
        p.op("vector", lambda e: e.tensor_copy(g["dpow"][:, 0:1], g["colv"][:, 11:12]))
        for kk in range(1, 12):
            p.op("vector", lambda e, kk=kk: e.tensor_tensor(
                g["dpow"][:, kk:kk + 1], g["dpow"][:, kk - 1:kk], g["dpow"][:, kk - 1:kk],
                ALU.mult))
        dpow_done = p.now("V")

        xt1_prev = {}
        for tt in range(NT):
            sl = tt % 2
            wt_in = list(l0_done) + (xt1_prev[tt - 2] if tt >= 2 else [])
            ld_ent = p.op("sync", lambda e, tt=tt, sl=sl: e.dma_start(
                out=g["xt1"][:, sl * 1024:(sl + 1) * 1024],
                in_=g["xn1_d"][tt * 128:(tt + 1) * 128, :]), wait=wt_in, dma_="ld")
            def x1tr(e, tt=tt, sl=sl):
                r = None
                for fo in range(ND):
                    r = e.transpose(g["qTr"][:, fo * 128: fo * 128 + 128],
                                    g["xt1"][:, sl * 1024 + fo * 128: sl * 1024 + fo * 128 + 128],
                                    g["ident_b"][:])
                return r
            p.op("tensor", x1tr, wait=[ld_ent, pad1_z, p.now("V")], inc="T")
            def x1ev(e, tt=tt):
                dst = g["xn1fm"][:].rearrange("p (f c) -> p f c", c=2056)
                return e.tensor_copy(dst[:, :, 8 + tt * 128: 8 + tt * 128 + 128],
                                     g["qTr"][:].rearrange("p (f c) -> p f c", c=128))
            p.op("vector", x1ev, wait=[p.now("T")])
            x1fm_v = p.now("V")
            p.op("scalar", lambda e, tt=tt, sl=sl: e.activation(
                g["sq_scr"][:, 0:1024], g["xt1"][:, sl * 1024:(sl + 1) * 1024],
                AF.Square, accum_out=g["rstat"][:, tt:tt + 1]), wait=[ld_ent], inc="S")
            def rlog(e, tt=tt):
                r = None
                for kc in range(ND):
                    r = e.matmul(g["q2"][:, tt * 16: tt * 16 + 16],
                                 g["xn1fm"][:, kc * 2056 + 8 + tt * 128: kc * 2056 + 8 + tt * 128 + 128],
                                 g["wl1"][:, RWo + kc * 16: RWo + kc * 16 + 16],
                                 start=(kc == 0), stop=(kc == ND - 1))
                return r
            p.op("tensor", rlog, wait=[x1fm_v, wrnw_done, p.now("V")], inc="T")
            xt1_prev[tt] = [p.now("T"), p.now("S")]
        xn1fm_done_all = p.now("V")
        pass1_T = p.now("T")
        pass1_S = p.now("S")

        RS = g["rstat"]
        p.op("vector", lambda e: e.tensor_scalar(
            RS[:, 16:32], RS[:, 0:16], 1.0 / D, 1e-5, ALU.mult, ALU.add),
            wait=[pass1_S])
        p.op("scalar", lambda e: e.activation(
            RS[:, 16:32], RS[:, 16:32], AF.Ln), wait=[p.now("V")])
        p.op("vector", lambda e: e.tensor_scalar(
            RS[:, 16:32], RS[:, 16:32], -0.5, None, ALU.mult), wait=[p.now("S")])
        p.op("scalar", lambda e: e.activation(
            RS[:, 16:32], RS[:, 16:32], AF.Exp), wait=[p.now("V")], inc="S")
        rstd_all = p.now("S")

        xnfst = {}
        for tt in range(NT):
            sl = tt % 2
            ld2 = p.op("sync", lambda e, tt=tt, sl=sl: e.dma_start(
                out=g["xt1"][:, sl * 1024:(sl + 1) * 1024],
                in_=g["xn1_d"][tt * 128:(tt + 1) * 128, :]),
                wait=[pass1_T, pass1_S] + (xt1_prev[tt - 2][0:1] if False else []) +
                     ([xnfst[tt - 2]] if tt >= 2 else []), dma_="ld")
            p.op("vector", lambda e, tt=tt, sl=sl: e.tensor_scalar(
                g["xnf_t"][:], g["xt1"][:, sl * 1024:(sl + 1) * 1024],
                RS[:, 16 + tt: 17 + tt], None, ALU.mult),
                wait=[ld2, rstd_all, nwb_done] + ([xnfst[tt - 1]] if tt >= 1 else []))
            p.op("vector", lambda e: e.tensor_tensor(
                g["xnf_t"][:], g["xnf_t"][:], g["nwb"][:], ALU.mult))
            xnf_done = p.now("V")
            xnfst[tt] = p.op("sync", lambda e, tt=tt: e.dma_start(
                out=g["xnf_d"][tt * 128:(tt + 1) * 128, :], in_=g["xnf_t"][:]),
                wait=[xnf_done], dma_="st")

        RC = g["rcl"]
        p.op("vector", lambda e: e.tensor_copy(g["rlg"][:], g["q2"][:, 0:256]),
             wait=[pass1_T])
        for tt in range(NT):
            p.op("vector", lambda e, tt=tt: e.tensor_scalar(
                g["rlg"][:, tt * 16:(tt + 1) * 16], g["rlg"][:, tt * 16:(tt + 1) * 16],
                RS[:, 16 + tt: 17 + tt], None, ALU.mult), wait=[rstd_all] if tt == 0 else ())
        for tt in range(NT):
            p.op("vector", lambda e, tt=tt: e.tensor_tensor(
                g["rlg"][:, tt * 16:(tt + 1) * 16], g["rlg"][:, tt * 16:(tt + 1) * 16],
                g["rb16"][:], ALU.add))
        for tt in range(NT):
            p.op("vector", lambda e, tt=tt: e.tensor_reduce(
                RC[:, tt:tt + 1], g["rlg"][:, tt * 16:(tt + 1) * 16], AX.X, ALU.max))
        for tt in range(NT):
            p.op("vector", lambda e, tt=tt: e.tensor_scalar(
                g["rlg"][:, tt * 16:(tt + 1) * 16], g["rlg"][:, tt * 16:(tt + 1) * 16],
                RC[:, tt:tt + 1], None, ALU.subtract))
        p.op("scalar", lambda e: e.activation(
            g["rlg"][:], g["rlg"][:], AF.Exp), wait=[p.now("V")], inc="S")
        p.op("vector", lambda e: e.tensor_copy(g["rsel"][:], g["rlg"][:]),
             wait=[p.now("S")])
        for tt in range(NT):
            p.op("vector", lambda e, tt=tt: e.tensor_reduce(
                RC[:, 16 + tt: 17 + tt], g["rlg"][:, tt * 16:(tt + 1) * 16], AX.X, ALU.add))
        p.op("vector", lambda e: e.reciprocal(RC[:, 16:32], RC[:, 16:32]))
        for tt in range(NT):
            p.op("vector", lambda e, tt=tt: e.tensor_scalar(
                g["rlg"][:, tt * 16:(tt + 1) * 16], g["rlg"][:, tt * 16:(tt + 1) * 16],
                RC[:, 16 + tt: 17 + tt], None, ALU.mult))
        for tt in range(NT):
            p.op("vector", lambda e, tt=tt: e.tensor_reduce(
                RC[:, 32 + tt: 33 + tt], g["rlg"][:, tt * 16:(tt + 1) * 16], AX.X, ALU.max))
        for tt in range(NT):
            p.op("vector", lambda e, tt=tt: e.tensor_scalar(
                g["rsel"][:, tt * 16:(tt + 1) * 16], g["rlg"][:, tt * 16:(tt + 1) * 16],
                RC[:, 32 + tt: 33 + tt], None, ALU.is_ge))
        p.op("vector", lambda e: e.tensor_scalar(
            g["rsel"][:], g["rsel"][:], -99.0, 99.0, ALU.mult, ALU.add))
        for tt in range(NT):
            p.op("vector", lambda e, tt=tt: e.tensor_tensor(
                g["rsel"][:, tt * 16:(tt + 1) * 16], g["rsel"][:, tt * 16:(tt + 1) * 16],
                g["iota_s"][:], ALU.add))
        for tt in range(NT):
            p.op("vector", lambda e, tt=tt: e.tensor_reduce(
                RC[:, 48 + tt: 49 + tt], g["rsel"][:, tt * 16:(tt + 1) * 16], AX.X, ALU.min))
        for tt in range(NT):
            p.op("vector", lambda e, tt=tt: e.tensor_scalar(
                g["rsel"][:, tt * 16:(tt + 1) * 16], g["iota_s"][:],
                RC[:, 48 + tt: 49 + tt], None, ALU.is_equal))
        p.op("vector", lambda e: e.tensor_scalar(
            g["rsel"][:], g["rsel"][:], 2.0, None, ALU.mult))
        p.op("vector", lambda e: e.tensor_tensor(
            g["rlg"][:], g["rlg"][:], g["rsel"][:], ALU.subtract))
        for tt in range(NT):
            p.op("vector", lambda e, tt=tt: e.tensor_reduce(
                RC[:, 64 + tt: 65 + tt], g["rlg"][:, tt * 16:(tt + 1) * 16], AX.X, ALU.max))
        for tt in range(NT):
            p.op("vector", lambda e, tt=tt: e.tensor_scalar(
                g["rsel"][:, tt * 16:(tt + 1) * 16], g["rlg"][:, tt * 16:(tt + 1) * 16],
                RC[:, 64 + tt: 65 + tt], None, ALU.is_ge))
        p.op("vector", lambda e: e.tensor_scalar(
            g["rsel"][:], g["rsel"][:], -99.0, 99.0, ALU.mult, ALU.add))
        for tt in range(NT):
            p.op("vector", lambda e, tt=tt: e.tensor_tensor(
                g["rsel"][:, tt * 16:(tt + 1) * 16], g["rsel"][:, tt * 16:(tt + 1) * 16],
                g["iota_s"][:], ALU.add))
        for tt in range(NT):
            p.op("vector", lambda e, tt=tt: e.tensor_reduce(
                RC[:, 80 + tt: 81 + tt], g["rsel"][:, tt * 16:(tt + 1) * 16], AX.X, ALU.min))
        rout_done = p.now("V")
        for j, c0 in enumerate((48, 80, 32, 64)):
            for tt in range(NT):
                p.op("sync", lambda e, j=j, tt=tt, c0=c0: e.dma_start(
                    out=g["rout_d"][j, tt * 128:(tt + 1) * 128].rearrange(
                        "(p one) -> p one", one=1),
                    in_=RC[:, c0 + tt: c0 + tt + 1]), wait=[rout_done], dma_="st")
        l1r_st = p.dma_all("st")

        for ft in range(ND):
            p.op("vector", lambda e, ft=ft: e.tensor_scalar(
                g["cnv1"][:, ft * 2048:(ft + 1) * 2048],
                g["xn1fm"][:, ft * 2056 + 8: ft * 2056 + 2056],
                g["cwb"][:, ft * 4 + 3: ft * 4 + 4],
                g["cwb"][:, ND * 4 + ft: ND * 4 + ft + 1],
                ALU.mult, ALU.add), wait=[xn1fm_done_all] if ft == 0 else ())
            for tap in (2, 1, 0):
                p.op("vector", lambda e, ft=ft, tap=tap: e.scalar_tensor_tensor(
                    g["cnv1"][:, ft * 2048:(ft + 1) * 2048],
                    g["xn1fm"][:, ft * 2056 + 5 + tap: ft * 2056 + 5 + tap + 2048],
                    g["cwb"][:, ft * 4 + tap: ft * 4 + tap + 1],
                    g["cnv1"][:, ft * 2048:(ft + 1) * 2048],
                    ALU.mult, ALU.add))
        cnv_done = p.now("V")

        for tc in range(4):
            for which, col0, bcol, dst0 in (("B", WPo, 0, 0), ("C", WPo + 64, 1, 2048)):
                def pmm(e, tc=tc, col0=col0):
                    r = None
                    for kc in range(ND):
                        r = e.matmul(g["q1"][0:64, 0:512],
                                     g["wl1"][:, col0 + kc * 128: col0 + kc * 128 + 64],
                                     g["cnv1"][:, kc * 2048 + tc * 512: kc * 2048 + tc * 512 + 512],
                                     start=(kc == 0), stop=(kc == ND - 1))
                    return r
                p.op("tensor", pmm, wait=[cnv_done, p.now("V"), p.now("S")], inc="T")
                pm_d = p.now("T")
                if which == "B":
                    p.op("vector", lambda e: e.tensor_scalar(
                        g["scr1"][:, 512:1024], g["q1"][0:64, 0:512], g["colv"][:, 0:1],
                        None, ALU.add), wait=[pm_d])
                    p.op("scalar", lambda e: e.activation(
                        g["scr1"][:, 0:512], g["scr1"][:, 512:1024], AF.Abs),
                        wait=[p.now("V")])
                    p.op("vector", lambda e: e.tensor_scalar(
                        g["scr1"][:, 0:512], g["scr1"][:, 0:512], -1.0, None, ALU.mult),
                        wait=[p.now("S")])
                    p.op("scalar", lambda e: e.activation(
                        g["scr1"][:, 0:512], g["scr1"][:, 0:512], AF.Exp), wait=[p.now("V")])
                    p.op("scalar", lambda e: e.activation(
                        g["scr1"][:, 0:512], g["scr1"][:, 0:512], AF.Ln, bias=1.0))
                    p.op("vector", lambda e: e.tensor_scalar(
                        g["scr1"][:, 512:1024], g["scr1"][:, 512:1024], 0.0, None, ALU.max),
                        wait=[p.now("S")])
                    p.op("vector", lambda e, tc=tc: e.tensor_tensor(
                        g["BCb"][0:64, tc * 512:(tc + 1) * 512], g["scr1"][:, 512:1024],
                        g["scr1"][:, 0:512], ALU.add))
                else:
                    p.op("scalar", lambda e, tc=tc: e.activation(
                        g["BCb"][0:64, 2048 + tc * 512: 2048 + tc * 512 + 512],
                        g["q1"][0:64, 0:512], AF.Tanh,
                        bias=g["colv"][:, 1:2]), wait=[pm_d], inc="S")
            def umm(e, tc=tc):
                r = None
                for kc in range(ND):
                    r = e.matmul(g["q2"][0:64, 0:512],
                                 g["wl1"][:, WSo + kc * 64: WSo + kc * 64 + 64],
                                 g["xn1fm"][:, kc * 2056 + 8 + tc * 512: kc * 2056 + 8 + tc * 512 + 512],
                                 start=(kc == 0), stop=(kc == ND - 1))
                return r
            p.op("tensor", umm, wait=[xn1fm_done_all, p.now("V")], inc="T")
            p.op("vector", lambda e, tc=tc: e.tensor_scalar(
                g["u_f"][0:64, tc * 512:(tc + 1) * 512], g["q2"][0:64, 0:512],
                g["colv"][:, 2:3], None, ALU.add), wait=[p.now("T")])
        bcu_done = p.now("V")
        bcu_done_s = p.now("S")

        p.op("vector", lambda e: e.tensor_scalar(
            g["w_f"][0:64, :], g["u_f"][0:64, :], g["colv"][:, 10:11], None, ALU.mult),
            wait=[bcu_done, bcu_done_s, dpow_done])
        p.op("vector", lambda e: e.tensor_tensor(
            g["w_f"][0:64, :], g["w_f"][0:64, :], g["BCb"][0:64, 0:2048], ALU.mult))
        p.op("vector", lambda e: e.tensor_copy(g["hp0"][0:64, :], g["w_f"][0:64, :]))
        cur, nxt = "hp0", "hp1"
        for kk in range(11):
            sh = 1 << kk
            p.op("vector", lambda e, cur=cur, nxt=nxt, sh=sh: e.tensor_copy(
                g[nxt][0:64, 0:sh], g[cur][0:64, 0:sh]))
            p.op("vector", lambda e, cur=cur, nxt=nxt, sh=sh, kk=kk: e.scalar_tensor_tensor(
                g[nxt][0:64, sh:2048], g[cur][0:64, 0:2048 - sh],
                g["dpow"][:, kk:kk + 1], g[cur][0:64, sh:2048], ALU.mult, ALU.add))
            cur, nxt = nxt, cur
        p.op("vector", lambda e, cur=cur: e.tensor_tensor(
            g["y_b"][0:64, :], g[cur][0:64, :], g["BCb"][0:64, 2048:4096], ALU.mult))
        y_done = p.now("V")

        for tc in range(4):
            s1ev = {}
            for dt in range(ND):
                pq = "q1" if dt % 2 == 0 else "q2"
                wt = [y_done] + l1_ld
                if dt >= 2:
                    wt.append(s1ev[dt - 2])
                elif tc > 0:
                    wt.append(s1tr_done)
                p.op("tensor", lambda e, tc=tc, dt=dt, pq=pq: e.matmul(
                    g[pq][:, 0:512],
                    g["wl1"][0:64, WOo + dt * 128: WOo + dt * 128 + 128],
                    g["y_b"][0:64, tc * 512:(tc + 1) * 512],
                    start=True, stop=True), wait=wt, inc="T")
                p.op("scalar", lambda e, dt=dt, pq=pq: e.activation(
                    g["s1fm"][:, dt * 512:(dt + 1) * 512], g[pq][:, 0:512], AF.Identity,
                    bias=g["cwb"][:, ND * 5 + dt: ND * 5 + dt + 1]),
                    wait=[p.now("T")], inc="S")
                s1ev[dt] = p.now("S")
            s1fm_done = p.now("S")
            for ts4 in range(4):
                def s1tr(e, tc=tc, ts4=ts4):
                    r = None
                    for dt in range(ND):
                        r = e.transpose(g["qTr"][:, dt * 128: dt * 128 + 128],
                                        g["s1fm"][:, dt * 512 + ts4 * 128: dt * 512 + ts4 * 128 + 128],
                                        g["ident_b"][:])
                    return r
                p.op("tensor", s1tr, wait=[s1fm_done, p.now("V")], inc="T")
                p.op("vector", lambda e: e.tensor_copy(g["s1scr"][:], g["qTr"][:]),
                     wait=[p.now("T")] + ([s1st_prev] if (tc + ts4) > 0 else []))
                s1v = p.now("V")
                s1st_prev = p.op("sync", lambda e, tc=tc, ts4=ts4: e.dma_start(
                    out=g["s1_d"][tc * 512 + ts4 * 128: tc * 512 + ts4 * 128 + 128, :],
                    in_=g["s1scr"][:]), wait=[s1v], dma_="st")
            s1tr_done = p.now("T")
        l1a_done = p.dma_all("st")
        es_l1.close()

        if "stat" in taps:
            dout("tap_stat", (128, 6 * NT), F32)
            p.op("sync", lambda e: e.dma_start(out=g["tap_stat"], in_=g["stat"][:]),
                 wait=[rstd_done], dma_="st")
        if "xn_fm" in taps:
            dout("tap_xn_fm", (128, ND * 2048), BF16)
            p.op("sync", lambda e: e.dma_start(out=g["tap_xn_fm"], in_=g["xnFMb"][:]),
                 wait=[xn_fm_done], dma_="st")
        if "x1" in taps:
            dout("tap_x1", (L, D), F32)
            p.op("sync", lambda e: e.dma_start(out=g["tap_x1"], in_=g["x1_d"]),
                 wait=l0_done, dma_="st")
        if "xn1" in taps:
            dout("tap_xn1", (L, D), BF16)
            p.op("sync", lambda e: e.dma_start(out=g["tap_xn1"], in_=g["xn1_d"]),
                 wait=l0_done, dma_="st")
        if "s1" in taps:
            dout("tap_s1", (L, D), BF16)
            p.op("sync", lambda e: e.dma_start(out=g["tap_s1"], in_=g["s1_d"]),
                 wait=l1a_done, dma_="st")
        if "xnf" in taps:
            dout("tap_xnf", (L, D), BF16)
            p.op("sync", lambda e: e.dma_start(out=g["tap_xnf"], in_=g["xnf_d"]),
                 wait=l1a_done, dma_="st")
        if "rout" in taps:
            dout("tap_rout", (4, L), F32)
            p.op("sync", lambda e: e.dma_start(out=g["tap_rout"], in_=g["rout_d"]),
                 wait=l1a_done, dma_="st")
        if "s_tm" in taps:
            dout("tap_s_tm", (L, D), BF16)
            p.op("sync", lambda e: e.dma_start(out=g["tap_s_tm"], in_=g["s_tm_d"]),
                 wait=attn_done_list, dma_="st")

        if fused:
            din("ohc", (128, 1), F32)
            din("rshift", (128, 1), F32)
            for nm in ("fc1A", "fc1B"):
                din(nm, (D, 4096), F32)
            for nm in ("fc2A", "fc2B"):
                din(nm, (4096, D), F32)
            din("b1A", (128, 32), F32)
            din("b1B", (128, 32), F32)
            din("b2A", (128, 8), F32)
            din("b2B", (128, 8), F32)
            din("cvm", (2, D), F32)
            dram("xnf_all", (NCORES * L, D), BF16)
            dram("rout_all", (32, L), F32)
            dram("eo_sh", (3072, D), BF16)
            dram("eo_all", (NCORES * 3072, D), BF16)
            dram("idxmap", (3072, 1), U32)
            dram("ext_scr", (6, L), F32)
            dout("x2_sh", (L, D), F32)
            g["x1own"] = g["x1_d"]
            g["s1own"] = g["s1_d"]
            g["xnfown"] = g["xnf_d"]

            allst = (p.dma_all("st") + p.dma_all("wf") + p.dma_all("ld") +
                     p.dma_all("wld"))
            p.op("gpsimd", lambda e: e.collective_compute(
                "AllGather", ALU.bypass, replica_groups=[list(range(NCORES))],
                ins=[g["rout_d"]], outs=[g["rout_all"]]), wait=l1r_st)
            cg_rout = p.now("P")
            p.op("gpsimd", lambda e: e.collective_compute(
                "AllGather", ALU.bypass, replica_groups=[list(range(NCORES))],
                ins=[g["xnf_d"]], outs=[g["xnf_all"]]))
            cg1 = p.now("P")

            es_moe = contextlib.ExitStack()
            es_rt = contextlib.ExitStack()

            def sbm(name, shape, dtype, _es=es_moe):
                g[name] = _es.enter_context(nc.sbuf_tensor(name, list(shape), dtype))

            def psm(name, shape, dtype, _es=es_moe):
                g[name] = _es.enter_context(nc.psum_tensor(name, list(shape), dtype))

            def sb_arr(name, shape, dtype, _es=es_rt):
                g[name] = _es.enter_context(
                    nc.sbuf_tensor(name, list(shape), dtype, side="right"))

            routing_buffers(sbm, sb_arr, din, want_w=True, skip_ident=True)
            psm("pR", (128, 512), F32)
            sbm("ohc_s", (128, 1), F32)
            sbm("rshift_s", (128, 1), F32)
            sbm("obuf", (128, 8), U32)
            sbm("zinit", (128, 24), U32)
            sbm("tokids", (128, 128), U32)
            sbm("scr32", (128, 64), F32)
            sbm("idxc32", (128, 32), U32)
            sbm("b1A_s", (128, 32), F32)
            sbm("b1B_s", (128, 32), F32)
            sbm("b2A_s", (128, 8), F32)
            sbm("b2B_s", (128, 8), F32)
            emit_routing_loads(p, g, skip_ident=True)
            p.op("sync", lambda e: e.dma_start(out=g["ohc_s"][:], in_=g["ohc"]),
                 dma_="mld")
            p.op("sync", lambda e: e.dma_start(out=g["rshift_s"][:], in_=g["rshift"]),
                 dma_="mld")
            for nm in ("b1A", "b1B", "b2A", "b2B"):
                p.op("sync", lambda e, nm=nm: e.dma_start(out=g[nm + "_s"][:], in_=g[nm]),
                     dma_="mld")
            bias_ld = p.dma_all("mld")
            bcreg = [None, None, None, None]

            def mkregs(e):
                for i, v in enumerate((3071, NCORES * L - 1, NCORES * 3072 - 1, 1)):
                    bcreg[i] = e.alloc_register(name="bc" + str(i))
                    e.reg_mov(bcreg[i], v)
                return None
            p.op("gpsimd", mkregs)
            rt_done = emit_routing(p, g, nc, want_flat_rows=True, want_w=True,
                                   pre=[cg_rout])
            ex_st = emit_own_extract(nc, p, g, rt_done)
            sc_done = emit_moe_scatter(nc, p, g, rt_done, bcreg, 0, pre=ex_st)
            rt_T = p.now("T")
            rt_V = p.now("V")
            es_rt.close()
            sbm("xgt", (128, 1024), BF16)
            sbm("xgf", (128, ND * 3072), BF16)
            sbm("wslab", (128, 2 * 8192), BF16)
            sbm("hbuf", (128, 32 * 512), BF16)
            sbm("eot", (128, 1024), BF16)
            sbm("eofm", (128, ND * 512), BF16)
            psm("pF0", (128, 512), F32)
            psm("pF1", (128, 512), F32)
            psm("pG", (128, 1024), BF16)
            psm("pO0", (128, 512), F32)
            psm("pO1", (128, 512), F32)
            psm("pO2", (128, 512), F32)
            psm("pO3", (128, 512), F32)
            sbm("cols", (128, 16), F32)
            sbm("idx4", (128, 4), U32)
            sbm("grow1", (128, D), BF16)
            sbm("grow2", (128, D), BF16)
            sbm("cvrow1", (128, D), BF16)
            sbm("cvrow2", (128, D), BF16)
            sbm("xnfo_t", (128, D), BF16)
            sbm("x1t", (128, D), F32)
            sbm("s1t", (128, D), BF16)
            sbm("comb", (128, D), F32)
            sbm("x2t", (128, 2 * D), F32)
            emit_mlp_compute(nc, p, g, bcreg, 1, sc_done, [rt_T, rt_V, cg1], bias_ld)
            p.op("gpsimd", lambda e: e.collective_compute(
                "AllGather", ALU.bypass, replica_groups=[list(range(NCORES))],
                ins=[g["eo_sh"]], outs=[g["eo_all"]]), wait=p.dma_all("mst"))
            cg2 = p.now("P")
            emit_combine(nc, p, g, bcreg, 2, 3, ex_st, [rt_T, rt_V] + allst, [cg2])
            es_moe.close()

        fin = p.dma_all("st") + p.dma_all("wf")
        if fused:
            fin += p.dma_all("mst") + p.dma_all("mld") + p.dma_all("mwf")
        p.op("sync", lambda e: None, wait=fin)

        p.flush(block)
    return nc



def stage_full(inputs):
    in_maps = stage_inputs(inputs)
    f = {k: np.asarray(v) for k, v in inputs.items()}
    FULLE = [4, 6, 7, 9, 10, 12, 13, 15]
    HALFE = [5, 8, 11, 14]
    fc1 = f["L1_ffn_fc1_weight"]; fc2 = f["L1_ffn_fc2_weight"]
    b1 = f["L1_ffn_fc1_bias"]; b2 = f["L1_ffn_fc2_bias"]
    for c in range(NCORES):
        m = in_maps[c]
        eA = FULLE[c]
        eB = HALFE[c // 2]
        m["fc1A"] = np.ascontiguousarray(fc1[eA - 4].T)
        m["fc2A"] = np.ascontiguousarray(fc2[eA - 4].T)
        m["fc1B"] = np.ascontiguousarray(fc1[eB - 4].T)
        m["fc2B"] = np.ascontiguousarray(fc2[eB - 4].T)
        m["b1A"] = np.ascontiguousarray(b1[eA - 4].reshape(32, 128).T)
        m["b1B"] = np.ascontiguousarray(b1[eB - 4].reshape(32, 128).T)
        m["b2A"] = np.ascontiguousarray(b2[eA - 4].reshape(8, 128).T)
        m["b2B"] = np.ascontiguousarray(b2[eB - 4].reshape(8, 128).T)
        m["rshift"] = np.full((128, 1), 3072.0 * c, np.float32)
        m["ohc"] = np.zeros((128, 1), np.float32)
        m["ohc"][c * 16, 0] = 1.0
    return in_maps


def kernel(**inputs):
    in_maps = stage_full(inputs)
    nc = build_program(detect_races=False, fused=True)
    res = run_bass_kernel_spmd(nc, in_maps, core_ids=list(range(NCORES)))
    out = np.zeros((B, L, D), np.float32)
    for c in range(NCORES):
        out[c] = np.asarray(res.results[c]["x2_sh"])
    return out


def kernel_unfused(**inputs):
    in_maps = stage_inputs(inputs)
    nc = build_program(taps=("x1", "s1", "xnf", "rout"), detect_races=False)
    res = run_bass_kernel_spmd(nc, in_maps, core_ids=list(range(NCORES)))
    xnf_all = np.zeros((NCORES * L, D), dtype=ml_dtypes.bfloat16)
    rout_all = np.zeros((32, L), np.float32)
    x1_sh = np.zeros((NCORES, L, D), np.float32)
    s1_sh = np.zeros((NCORES, L, D), dtype=ml_dtypes.bfloat16)
    for c in range(NCORES):
        r = res.results[c]
        xnf_all[c * L:(c + 1) * L] = np.asarray(r["tap_xnf"])
        rout_all[4 * c:4 * c + 4] = np.asarray(r["tap_rout"])
        x1_sh[c] = np.asarray(r["tap_x1"])
        s1_sh[c] = np.asarray(r["tap_s1"])
    in_maps2 = stage_mlp(inputs, xnf_all, rout_all)
    nc2 = build_mlp(detect_races=False)
    res2 = run_bass_kernel_spmd(nc2, in_maps2, core_ids=list(range(NCORES)))
    eo_all = np.zeros((24576, D), dtype=ml_dtypes.bfloat16)
    for c in range(NCORES):
        eo_all[c * 3072:(c + 1) * 3072] = np.asarray(res2.results[c]["eo_sh"])
    in_maps3 = stage_cmb(inputs, xnf_all, rout_all, eo_all, x1_sh, s1_sh)
    nc3 = build_cmb(detect_races=False)
    res3 = run_bass_kernel_spmd(nc3, in_maps3, core_ids=list(range(NCORES)))
    out = np.zeros((B, L, D), np.float32)
    for c in range(NCORES):
        out[c] = np.asarray(res3.results[c]["x2_sh"])
    return out


def run_dev(taps, inputs):
    in_maps = stage_inputs(inputs)
    nc = build_program(taps=taps)
    res = run_bass_kernel_spmd(nc, in_maps, core_ids=list(range(NCORES)))
    return res.results



def stage_mlp(inputs, xnf_all, rout_all):
    f = {k: np.asarray(v) for k, v in inputs.items()}
    FULLE = [4, 6, 7, 9, 10, 12, 13, 15]
    HALFE = [5, 8, 11, 14]
    ii16 = np.arange(128)
    Arow = np.full(16, 1.0e7, np.float32)
    Brow = np.zeros(16, np.float32)
    for c, e in enumerate(FULLE):
        Arow[e] = 3072.0 * c
    for i, e in enumerate(HALFE):
        Arow[e] = 3072.0 * (2 * i) + 2048.0
        Brow[e] = 2048.0
    common = {
        "xnf_all": np.ascontiguousarray(xnf_all),
        "rout_all": np.ascontiguousarray(rout_all),
        "ecol": (ii16 % 16).astype(np.float32).reshape(128, 1),
        "Acol": Arow[(ii16 % 16)].reshape(128, 1),
        "Bcol": Brow[(ii16 % 16)].reshape(128, 1),
        "Mfix": ((ii16[:, None] % 16 == ii16[None, :] % 16) &
                 (ii16[:, None] // 16 < ii16[None, :] // 16)).astype(np.float32),
        "gsum": (ii16[:, None] // 16 == ii16[None, :] // 16).astype(np.float32),
        "sel8": np.zeros((8, 128), dtype=ml_dtypes.bfloat16),
        "identb": np.eye(128, dtype=ml_dtypes.bfloat16),
        "ohch": np.zeros((128, 8), np.float32),
    }
    for ch in range(8):
        common["sel8"][ch, ch * 16:(ch + 1) * 16] = 1
        common["ohch"][ch * 16, ch] = 1
    fc1 = f["L1_ffn_fc1_weight"]; fc2 = f["L1_ffn_fc2_weight"]
    b1 = f["L1_ffn_fc1_bias"]; b2 = f["L1_ffn_fc2_bias"]
    in_maps = []
    for c in range(NCORES):
        m = dict(common)
        eA = FULLE[c]
        eB = HALFE[c // 2]
        m["fc1A"] = np.ascontiguousarray(fc1[eA - 4].T)
        m["fc2A"] = np.ascontiguousarray(fc2[eA - 4].T)
        m["fc1B"] = np.ascontiguousarray(fc1[eB - 4].T)
        m["fc2B"] = np.ascontiguousarray(fc2[eB - 4].T)
        m["b1A"] = np.ascontiguousarray(b1[eA - 4].reshape(32, 128).T)
        m["b1B"] = np.ascontiguousarray(b1[eB - 4].reshape(32, 128).T)
        m["b2A"] = np.ascontiguousarray(b2[eA - 4].reshape(8, 128).T)
        m["b2B"] = np.ascontiguousarray(b2[eB - 4].reshape(8, 128).T)
        m["rshift"] = np.full((128, 1), 3072.0 * c, np.float32)
        in_maps.append(m)
    return in_maps


def emit_routing(p, g, nc, want_flat_rows, want_w, pre=()):
    ALUx = ALU

    def arr(name):
        return g[name]

    for rr in range(4):
        p.op("gpsimd", lambda e, rr=rr: e.dma_start(
            out=g["r8b"][0:8, rr * 2048:(rr + 1) * 2048],
            in_=g["rout_all"].rearrange("(c r) t -> c r t", r=4)[:, rr, :]),
            wait=list(pre), dma_="mld")
    r8b_d = p.dma_all("mld")
    names = ["ids1", "ids2"]
    for j, nm in enumerate(names):
        for q in range(4):
            p.op("tensor", lambda e, j=j, q=q: e.matmul(
                g["pR"][:, 0:512], g["sel8_s"][:, 0:128],
                g["r8b"][0:8, j * 2048 + q * 512: j * 2048 + q * 512 + 512],
                start=True, stop=True), wait=[p.now("V")] + p.dma_all("mld"), inc="T")
            p.op("vector", lambda e, nm=nm, q=q: e.tensor_copy(
                g[nm][:, q * 512:(q + 1) * 512], g["pR"][:, 0:512]), wait=[p.now("T")])
    bcast_d = p.now("V")
    p.op("vector", lambda e: e.tensor_scalar(
        g["oh1"][:], g["ids1"][:], g["ecol_s"][:, 0:1], None, ALUx.is_equal),
        wait=[bcast_d] + p.dma_all("mld"))
    p.op("vector", lambda e: e.tensor_scalar(
        g["oh2"][:], g["ids2"][:], g["ecol_s"][:, 0:1], None, ALUx.is_equal))
    for srcn, dstn in (("oh1", "C1"), ("oh2", "C2")):
        p.op("vector", lambda e, srcn=srcn, dstn=dstn: e.tensor_copy(
            g[dstn][:], g[srcn][:]))
        cur, nxt = dstn, "Cp"
        for kk in range(11):
            sh = 1 << kk
            p.op("vector", lambda e, cur=cur, nxt=nxt, sh=sh: e.tensor_copy(
                g[nxt][:, 0:sh], g[cur][:, 0:sh]))
            p.op("vector", lambda e, cur=cur, nxt=nxt, sh=sh: e.tensor_tensor(
                g[nxt][:, sh:2048], g[cur][:, sh:2048], g[cur][:, 0:2048 - sh], ALUx.add))
            cur, nxt = nxt, cur
        if cur != dstn:
            p.op("vector", lambda e, cur=cur, dstn=dstn: e.tensor_copy(
                g[dstn][:], g[cur][:]))
    cs_d = p.now("V")
    for nmc in ("C1", "C2"):
        p.op("vector", lambda e, nmc=nmc: e.tensor_copy(
            g["tcol"][:, 0:1], g[nmc][:, 2047:2048]), wait=[cs_d])
        p.op("tensor", lambda e: e.matmul(
            g["pR"][:, 0:1], g["Mfix_s"][:], g["tcol"][:, 0:1],
            start=True, stop=True), wait=[p.now("V")] + p.dma_all("mld"), inc="T")
        p.op("vector", lambda e: e.tensor_copy(g["tcol"][:, 1:2], g["pR"][:, 0:1]),
             wait=[p.now("T")])
        p.op("vector", lambda e, nmc=nmc: e.tensor_scalar(
            g[nmc][:], g[nmc][:], g["tcol"][:, 1:2], None, ALUx.add))
    cfix_d = p.now("V")
    p.op("vector", lambda e: e.tensor_tensor(g["Cp"][:], g["C1"][:], g["C2"][:], ALUx.add),
         wait=[cfix_d])
    p.op("vector", lambda e: e.tensor_tensor(g["t1"][:], g["Cp"][:], g["oh2"][:], ALUx.subtract))
    p.op("vector", lambda e: e.tensor_tensor(g["t1"][:], g["t1"][:], g["oh1"][:], ALUx.mult))
    p.op("vector", lambda e: e.tensor_tensor(g["t2"][:], g["Cp"][:], g["oh2"][:], ALUx.mult))
    p.op("vector", lambda e: e.tensor_scalar(
        g["a1"][:], g["oh1"][:], g["Acol_s"][:, 0:1], None, ALUx.mult))
    p.op("vector", lambda e: e.tensor_scalar(
        g["b1x"][:], g["oh1"][:], g["Bcol_s"][:, 0:1], None, ALUx.mult))
    p.op("vector", lambda e: e.tensor_scalar(
        g["a2"][:], g["oh2"][:], g["Acol_s"][:, 0:1], None, ALUx.mult))
    p.op("vector", lambda e: e.tensor_scalar(
        g["b2x"][:], g["oh2"][:], g["Bcol_s"][:, 0:1], None, ALUx.mult))
    pre_d = p.now("V")
    for nmc in ("t1", "t2", "a1", "b1x", "a2", "b2x"):
        for q in range(4):
            p.op("tensor", lambda e, nmc=nmc, q=q: e.matmul(
                g["pR"][:, 0:512], g["gsum_s"][:], g[nmc][:, q * 512:(q + 1) * 512],
                start=True, stop=True), wait=[pre_d, p.now("V")] + p.dma_all("mld"), inc="T")
            p.op("vector", lambda e, nmc=nmc, q=q: e.tensor_copy(
                g[nmc][:, q * 512:(q + 1) * 512], g["pR"][:, 0:512]), wait=[p.now("T")])
    gs_d = p.now("V")
    for kq, (ncount, aa, bb, flat) in enumerate(
            (("t1", "a1", "b1x", "flat1"), ("t2", "a2", "b2x", "flat2"))):
        p.op("vector", lambda e, ncount=ncount: e.tensor_scalar(
            g["Cp"][:], g[ncount][:], -1.0, None, ALUx.add), wait=[gs_d])
        p.op("vector", lambda e: e.tensor_scalar(
            g["t3"][:], g["Cp"][:], 1023.5, None, ALUx.is_ge))
        p.op("vector", lambda e, bb=bb: e.tensor_tensor(
            g["t3"][:], g["t3"][:], g[bb][:], ALUx.mult))
        p.op("vector", lambda e, aa=aa: e.tensor_tensor(
            g["Cp"][:], g["Cp"][:], g[aa][:], ALUx.add))
        p.op("vector", lambda e, flat=flat: e.tensor_tensor(
            g[flat][:], g["Cp"][:], g["t3"][:], ALUx.add))
        p.op("vector", lambda e, ncount=ncount: e.tensor_scalar(
            g["t3"][:], g[ncount][:], 2048.5, None, ALUx.is_gt))
        p.op("vector", lambda e: e.tensor_scalar(
            g["t3"][:], g["t3"][:], 1.0e7, None, ALUx.mult))
        p.op("vector", lambda e, flat=flat: e.tensor_tensor(
            g[flat][:], g[flat][:], g["t3"][:], ALUx.add))
    if want_w:
        for kq, wnm in enumerate(("w1a", "w2a")):
            for q in range(4):
                p.op("tensor", lambda e, kq=kq, q=q: e.matmul(
                    g["pR"][:, 0:512], g["sel8_s"][:, 0:128],
                    g["r8b"][0:8, (2 + kq) * 2048 + q * 512: (2 + kq) * 2048 + q * 512 + 512],
                    start=True, stop=True), wait=[p.now("V")] + p.dma_all("mld"), inc="T")
                p.op("vector", lambda e, wnm=wnm, q=q: e.tensor_copy(
                    g[wnm][:, q * 512:(q + 1) * 512], g["pR"][:, 0:512]), wait=[p.now("T")])
        for kq, (ncount, wnm) in enumerate((("t1", "w1a"), ("t2", "w2a"))):
            p.op("vector", lambda e, ncount=ncount: e.tensor_scalar(
                g["t3"][:], g[ncount][:], 2048.5, None, ALUx.is_le))
            p.op("vector", lambda e, wnm=wnm: e.tensor_tensor(
                g[wnm][:], g[wnm][:], g["t3"][:], ALUx.mult))
    return p.now("V")


def routing_buffers(sb_const, sb_arr, din, want_w, skip_ident=False):
    din("ecol", (128, 1), F32)
    din("Acol", (128, 1), F32)
    din("Bcol", (128, 1), F32)
    din("Mfix", (128, 128), F32)
    din("gsum", (128, 128), F32)
    din("sel8", (8, 128), BF16)
    if not skip_ident:
        din("identb", (128, 128), BF16)
    din("ohch", (128, 8), F32)
    sb_arr("r8b", (8, 8192), BF16)
    sb_arr("rowx", (1, 2048), F32)
    names = ["ids1", "ids2", "oh1", "oh2", "C1", "C2", "Cp",
             "t1", "t2", "t3", "a1", "b1x", "a2", "b2x", "flat1", "flat2"]
    if want_w:
        names += ["w1a", "w2a"]
    for nm in names:
        sb_arr(nm, (128, 2048), F32)
    sb_const("tcol", (128, 4), F32)
    sb_const("ecol_s", (128, 1), F32)
    sb_const("Acol_s", (128, 1), F32)
    sb_const("Bcol_s", (128, 1), F32)
    sb_const("Mfix_s", (128, 128), F32)
    sb_const("gsum_s", (128, 128), F32)
    sb_const("sel8_s", (8, 128), BF16)
    if not skip_ident:
        sb_const("ident_b", (128, 128), BF16)
    sb_const("ohch_s", (128, 8), F32)


def emit_routing_loads(p, g, skip_ident=False):
    for nm in ("ecol", "Acol", "Bcol", "Mfix", "gsum", "ohch"):
        p.op("sync", lambda e, nm=nm: e.dma_start(out=g[nm + "_s"][:], in_=g[nm]), dma_="mld")
    p.op("gpsimd", lambda e: e.dma_start(out=g["sel8_s"][:], in_=g["sel8"]), dma_="mld")
    if not skip_ident:
        p.op("gpsimd", lambda e: e.dma_start(out=g["ident_b"][:], in_=g["identb"]), dma_="mld")


def emit_moe_scatter(nc, p, g, rt_done, bcreg, bx, pre=()):
    p.op("gpsimd", lambda e: e.memset(g["zinit"][:], 10000000))
    zi_d = p.now("P")
    idx_init = [p.op("sync", lambda e: e.dma_start(
        out=g["idxmap"].rearrange("(p t) one -> p (t one)", p=128),
        in_=g["zinit"][:]), wait=[zi_d], dma_="mst")]
    for ch in range(8):
        p.op("gpsimd", lambda e, ch=ch: e.iota(
            g["tokids"][:, ch * 16:(ch + 1) * 16], [[128, 16]],
            base=2048 * ch, channel_multiplier=1))
    tok_d = p.now("P")
    sc_prev = {0: [], 1: []}
    for tr in range(16):
        pp = tr % 2
        c0 = pp * 16
        for kq, arrn in enumerate(("flat1", "flat2")):
            p.op("tensor", lambda e, arrn=arrn, tr=tr, kq=kq, c0=c0: e.matmul(
                g["pR"][:, c0 + kq * 8: c0 + kq * 8 + 8],
                g[arrn][:, tr * 128:(tr + 1) * 128],
                g["ohch_s"][:, 0:8],
                start=True, stop=True),
                wait=[rt_done, p.now("V")] + (list(pre) if tr == 0 else []), inc="T")
        mm_d = p.now("T")
        p.op("vector", lambda e, c0=c0: e.tensor_copy(
            g["scr32"][:, c0:c0 + 16], g["pR"][:, c0:c0 + 16]),
            wait=[mm_d] + sc_prev[pp])
        p.op("vector", lambda e, c0=c0: e.tensor_scalar(
            g["scr32"][:, c0:c0 + 16], g["scr32"][:, c0:c0 + 16],
            g["rshift_s"][:, 0:1], None, ALU.subtract))
        p.op("vector", lambda e, c0=c0: e.tensor_scalar(
            g["scr32"][:, 32 + c0:32 + c0 + 16], g["scr32"][:, c0:c0 + 16],
            0.0, 1.0e7, ALU.is_lt, ALU.mult))
        p.op("vector", lambda e, c0=c0: e.tensor_tensor(
            g["scr32"][:, c0:c0 + 16], g["scr32"][:, c0:c0 + 16],
            g["scr32"][:, 32 + c0:32 + c0 + 16], ALU.add))
        p.op("vector", lambda e, c0=c0: e.tensor_copy(
            g["idxc32"][:, c0:c0 + 16], g["scr32"][:, c0:c0 + 16]))
        v_d = p.now("V")
        ent = []
        for kq in range(2):
            for ch in range(8):
                ent.append(p.op("gpsimd",
                    lambda e, tr=tr, kq=kq, ch=ch, c0=c0: e.indirect_dma_start(
                        out=g["idxmap"],
                        out_offset=IndirectOffsetOnAxis(
                            ap=g["idxc32"][:, c0 + kq * 8 + ch: c0 + kq * 8 + ch + 1],
                            axis=0),
                        in_=g["tokids"][:, ch * 16 + tr: ch * 16 + tr + 1],
                        in_offset=None,
                        bounds_check=bcreg[bx], oob_is_err=False),
                    wait=[v_d, tok_d] + idx_init, dma_="msc"))
        sc_prev[pp] = ent
    return p.dma_all("msc") + idx_init


def emit_mlp_compute(nc, p, g, bcreg, bx, sc_done, pre, bias_ld):
    gat_d = []
    for rt in range(24):
        ld = p.op("sync", lambda e, rt=rt: e.dma_start(
            out=g["obuf"][:, 0:1],
            in_=g["idxmap"][rt * 128:(rt + 1) * 128, :]),
            wait=sc_done + gat_d, dma_="mld")
        p.op("vector", lambda e, rt=rt: e.memset(g["xgt"][:], 0.0),
             wait=[ld] + (list(pre) if rt == 0 else []))
        z_d = p.now("V")
        gt = p.op("gpsimd", lambda e: e.indirect_dma_start(
            out=g["xgt"][:], out_offset=None,
            in_=g["xnf_all"],
            in_offset=IndirectOffsetOnAxis(ap=g["obuf"][:, 0:1], axis=0),
            bounds_check=bcreg[bx], oob_is_err=False),
            wait=[z_d], dma_="mld")
        def xtr(e, rt=rt):
            r = None
            for fo in range(ND):
                r = e.transpose(g["pG"][:, fo * 128: fo * 128 + 128],
                                g["xgt"][:, fo * 128: fo * 128 + 128],
                                g["ident_b"][:])
            return r
        p.op("tensor", xtr, wait=[gt, p.now("V")] + p.dma_all("mld"), inc="T")
        def xev(e, rt=rt):
            dst = g["xgf"][:].rearrange("p (f t) -> p f t", t=3072)
            return e.tensor_copy(dst[:, :, rt * 128:(rt + 1) * 128],
                                 g["pG"][:].rearrange("p (f t) -> p f t", t=128))
        p.op("vector", xev, wait=[p.now("T")])
        gat_d = [p.now("V")]
    xgf_done = p.now("V")

    eo_st = []
    slab_last = [xgf_done, xgf_done]
    eo_evac_prev = xgf_done
    f2_lastT = xgf_done
    for tci in range(6):
        part = "A" if tci < 4 else "B"
        sc0 = tci * 512
        b1n, b2n = "b1" + part, "b2" + part
        for qq in range(4):
            sl = qq % 2
            slab = []
            for kc in range(ND):
                slab.append(p.op("gpsimd", lambda e, part=part, qq=qq, kc=kc, sl=sl:
                    e.dma_start(
                        out=g["wslab"][:, sl * 8192 + kc * 1024: sl * 8192 + kc * 1024 + 1024],
                        in_=g["fc1" + part][kc * 128:(kc + 1) * 128,
                                            qq * 1024:(qq + 1) * 1024]),
                    wait=[slab_last[sl]], dma_="mwf"))
            for hq in range(8):
                ht = qq * 8 + hq
                pF = "pF0" if ht % 2 == 0 else "pF1"
                def f1mm(e, sc0=sc0, hq=hq, sl=sl, pF=pF):
                    r = None
                    for kc in range(ND):
                        r = e.matmul(
                            g[pF][:],
                            g["wslab"][:, sl * 8192 + kc * 1024 + hq * 128:
                                       sl * 8192 + kc * 1024 + hq * 128 + 128],
                            g["xgf"][:, kc * 3072 + sc0: kc * 3072 + sc0 + 512],
                            start=(kc == 0), stop=(kc == ND - 1))
                    return r
                wt = slab + [xgf_done]
                if ht >= 2:
                    wt.append(hged[ht - 2])
                elif tci > 0:
                    wt.append(f2_lastT)
                p.op("tensor", f1mm, wait=wt, inc="T")
                p.op("scalar", lambda e, ht=ht, pF=pF, b1n=b1n: e.activation(
                    g["hbuf"][:, ht * 512:(ht + 1) * 512], g[pF][:], AF.Gelu,
                    bias=g[b1n + "_s"][:, ht:ht + 1]), wait=[p.now("T")] + bias_ld, inc="S")
                if ht == 0:
                    hged = {}
                hged[ht] = p.now("S")
            slab_last[sl] = p.now("T")
        h_done = p.now("S")
        for dh in range(2):
            for qq in range(4):
                sl = qq % 2
                slab = []
                for hc8 in range(ND):
                    slab.append(p.op("gpsimd",
                        lambda e, part=part, qq=qq, hc8=hc8, sl=sl, dh=dh: e.dma_start(
                            out=g["wslab"][:, sl * 8192 + hc8 * 512:
                                           sl * 8192 + hc8 * 512 + 512],
                            in_=g["fc2" + part][qq * 1024 + hc8 * 128:
                                                qq * 1024 + hc8 * 128 + 128,
                                                dh * 512:(dh + 1) * 512]),
                        wait=[slab_last[sl]], dma_="mwf"))
                def f2mm(e, qq=qq, sl=sl, dh=dh):
                    r = None
                    for hc8 in range(ND):
                        hc = qq * 8 + hc8
                        for d4 in range(4):
                            r = e.matmul(
                                g["pO" + str(d4)][:],
                                g["wslab"][:, sl * 8192 + hc8 * 512 + d4 * 128:
                                           sl * 8192 + hc8 * 512 + d4 * 128 + 128],
                                g["hbuf"][:, hc * 512:(hc + 1) * 512],
                                start=(qq == 0 and hc8 == 0),
                                stop=(qq == 3 and hc8 == ND - 1))
                    return r
                wt = slab + [h_done]
                if qq == 0:
                    wt.append(eo_evac_prev)
                p.op("tensor", f2mm, wait=wt, inc="T")
                slab_last[sl] = p.now("T")
            f2_lastT = p.now("T")
            for d4 in range(4):
                dt = dh * 4 + d4
                p.op("scalar", lambda e, d4=d4, dt=dt, b2n=b2n: e.activation(
                    g["eofm"][:, dt * 512:(dt + 1) * 512], g["pO" + str(d4)][:],
                    AF.Identity, bias=g[b2n + "_s"][:, dt:dt + 1]),
                    wait=[f2_lastT] + bias_ld, inc="S")
            eo_evac_prev = p.now("S")
        for st4 in range(4):
            def etr(e, st4=st4):
                r = None
                for dt in range(ND):
                    r = e.transpose(g["pG"][:, dt * 128: dt * 128 + 128],
                                    g["eofm"][:, dt * 512 + st4 * 128: dt * 512 + st4 * 128 + 128],
                                    g["ident_b"][:])
                return r
            p.op("tensor", etr, wait=[eo_evac_prev, p.now("V")], inc="T")
            p.op("vector", lambda e: e.tensor_copy(g["eot"][:], g["pG"][:]),
                 wait=[p.now("T")] + eo_st[-1:])
            ev_d = p.now("V")
            eo_st.append(p.op("sync", lambda e, sc0=sc0, st4=st4: e.dma_start(
                out=g["eo_sh"][sc0 + st4 * 128: sc0 + st4 * 128 + 128, :],
                in_=g["eot"][:]), wait=[ev_d], dma_="mst"))
    return eo_st


def build_mlp(detect_races=True, taps=()):
    nc = bass.Bass("TRN2", target_bir_lowering=False, detect_race_conditions=detect_races)
    es = contextlib.ExitStack()
    g = {}

    def din(name, shape, dtype):
        g[name] = nc.dram_tensor(name, list(shape), dtype, kind="ExternalInput").ap()

    def dout(name, shape, dtype):
        g[name] = nc.dram_tensor(name, list(shape), dtype, kind="ExternalOutput").ap()

    def dram(name, shape, dtype):
        g[name] = nc.dram_tensor(name, list(shape), dtype, kind="Internal").ap()

    def sb(name, shape, dtype):
        g[name] = es.enter_context(nc.sbuf_tensor(name, list(shape), dtype))

    def ps(name, shape, dtype):
        g[name] = es.enter_context(nc.psum_tensor(name, list(shape), dtype))

    din("xnf_all", (NCORES * L, D), BF16)
    din("rout_all", (32, L), F32)
    for nm in ("fc1A", "fc1B"):
        din(nm, (D, 4096), F32)
    for nm in ("fc2A", "fc2B"):
        din(nm, (4096, D), F32)
    din("b1A", (128, 32), F32)
    din("b1B", (128, 32), F32)
    din("b2A", (128, 8), F32)
    din("b2B", (128, 8), F32)
    din("rshift", (128, 1), F32)
    dout("eo_sh", (3072, D), BF16)
    dram("idxmap", (3072, 1), U32)

    with es:
        p = Prog(nc, es)
        block = es.enter_context(nc.Block())
        es_rt = contextlib.ExitStack()

        def sb_arr(name, shape, dtype):
            g[name] = es_rt.enter_context(nc.sbuf_tensor(name, list(shape), dtype, side="right"))

        routing_buffers(sb, sb_arr, din, want_w=False)
        ps("pR", (128, 512), F32)
        sb("rshift_s", (128, 1), F32)
        sb("obuf", (128, 8), U32)
        sb("zinit", (128, 24), U32)
        sb("tokids", (128, 128), U32)
        sb("scr32", (128, 64), F32)
        sb("idxc32", (128, 32), U32)
        ps("pF0", (128, 512), F32)
        ps("pF1", (128, 512), F32)
        ps("pG", (128, 1024), BF16)
        ps("pO0", (128, 512), F32)
        ps("pO1", (128, 512), F32)
        ps("pO2", (128, 512), F32)
        ps("pO3", (128, 512), F32)
        sb("b1A_s", (128, 32), F32)
        sb("b1B_s", (128, 32), F32)
        sb("b2A_s", (128, 8), F32)
        sb("b2B_s", (128, 8), F32)

        emit_routing_loads(p, g)
        for nm in ("b1A", "b1B", "b2A", "b2B"):
            p.op("sync", lambda e, nm=nm: e.dma_start(out=g[nm + "_s"][:], in_=g[nm]),
                 dma_="mld")
        bias_ld = p.dma_all("mld")
        bcreg = [None, None]

        def mkregs(e):
            bcreg[0] = e.alloc_register(name="bc0")
            e.reg_mov(bcreg[0], 3071)
            bcreg[1] = e.alloc_register(name="bc1")
            e.reg_mov(bcreg[1], NCORES * L - 1)
            return None
        p.op("gpsimd", mkregs)
        p.op("sync", lambda e: e.dma_start(out=g["rshift_s"][:], in_=g["rshift"]), dma_="mld")
        rt_done = emit_routing(p, g, nc, want_flat_rows=True, want_w=False)
        sc_done = emit_moe_scatter(nc, p, g, rt_done, bcreg, 0)
        rt_T = p.now("T")
        rt_V = p.now("V")
        es_rt.close()
        sb("xgt", (128, 1024), BF16)
        sb("xgf", (128, ND * 3072), BF16)
        sb("wslab", (128, 2 * 8192), BF16)
        sb("hbuf", (128, 32 * 512), BF16)
        sb("eot", (128, 1024), BF16)
        sb("eofm", (128, ND * 512), BF16)
        emit_mlp_compute(nc, p, g, bcreg, 1, sc_done, [rt_T, rt_V], bias_ld)
        p.op("sync", lambda e: None, wait=p.dma_all("mst") + p.dma_all("mwf"))
        p.flush(block)
    return nc



def stage_cmb(inputs, xnf_all, rout_all, eo_all, x1_sh, s1_sh):
    f = {k: np.asarray(v) for k, v in inputs.items()}
    ii16 = np.arange(128)
    FULLE = [4, 6, 7, 9, 10, 12, 13, 15]
    HALFE = [5, 8, 11, 14]
    Arow = np.full(16, 1.0e7, np.float32)
    Brow = np.zeros(16, np.float32)
    for c, e in enumerate(FULLE):
        Arow[e] = 3072.0 * c
    for i, e in enumerate(HALFE):
        Arow[e] = 3072.0 * (2 * i) + 2048.0
        Brow[e] = 2048.0
    common = {
        "xnf_all": np.ascontiguousarray(xnf_all),
        "rout_all": np.ascontiguousarray(rout_all),
        "eo_all": np.ascontiguousarray(eo_all),
        "cvm": np.ascontiguousarray(f["L1_ffn_constant_vectors"]),
        "ecol": (ii16 % 16).astype(np.float32).reshape(128, 1),
        "Acol": Arow[(ii16 % 16)].reshape(128, 1),
        "Bcol": Brow[(ii16 % 16)].reshape(128, 1),
        "Mfix": ((ii16[:, None] % 16 == ii16[None, :] % 16) &
                 (ii16[:, None] // 16 < ii16[None, :] // 16)).astype(np.float32),
        "gsum": (ii16[:, None] // 16 == ii16[None, :] // 16).astype(np.float32),
        "sel8": np.zeros((8, 128), dtype=ml_dtypes.bfloat16),
        "identb": np.eye(128, dtype=ml_dtypes.bfloat16),
        "ohch": np.zeros((128, 8), np.float32),
    }
    for ch in range(8):
        common["sel8"][ch, ch * 16:(ch + 1) * 16] = 1
        common["ohch"][ch * 16, ch] = 1
    in_maps = []
    for c in range(NCORES):
        m = dict(common)
        m["ohc"] = np.zeros((128, 1), np.float32)
        m["ohc"][c * 16, 0] = 1.0
        m["x1own"] = np.ascontiguousarray(x1_sh[c])
        m["s1own"] = np.ascontiguousarray(s1_sh[c])
        m["xnfown"] = np.ascontiguousarray(xnf_all[c * L:(c + 1) * L])
        in_maps.append(m)
    return in_maps


def emit_own_extract(nc, p, g, rt_done):
    ex_st = []
    for ai, arrn in enumerate(("flat1", "flat2", "w1a", "w2a", "ids1", "ids2")):
        for q in range(4):
            p.op("tensor", lambda e, arrn=arrn, q=q: e.matmul(
                g["pR"][0:1, 0:512], g["ohc_s"][:, 0:1],
                g[arrn][:, q * 512:(q + 1) * 512],
                start=True, stop=True), wait=[rt_done, p.now("V")] + p.dma_all("mld"),
                inc="T")
            p.op("vector", lambda e, q=q: e.tensor_copy(
                g["rowx"][0:1, q * 512:(q + 1) * 512], g["pR"][0:1, 0:512]),
                wait=[p.now("T")])
        rv = p.now("V")
        ex_st.append(p.op("sync", lambda e, ai=ai: e.dma_start(
            out=g["ext_scr"][ai, :].rearrange("(one t) -> one t", one=1),
            in_=g["rowx"][0:1, 0:2048]), wait=[rv], dma_="mst"))
    return ex_st


def emit_combine(nc, p, g, bcreg, beo, bcv, ex_st, first_extra, gath_extra):
    prev_x2 = []
    for tt in range(NT):
        sl = tt % 2
        colld = []
        for ai in range(6):
            colld.append(p.op("sync", lambda e, ai=ai, tt=tt: e.dma_start(
                out=g["cols"][:, ai:ai + 1],
                in_=g["ext_scr"][ai, tt * 128:(tt + 1) * 128].rearrange(
                    "(p one) -> p one", one=1)),
                wait=ex_st + (list(first_extra) if tt == 0 else []), dma_="mld"))
        p.op("vector", lambda e: e.tensor_copy(g["idx4"][:, 0:1], g["cols"][:, 0:1]),
             wait=colld + prev_x2)
        p.op("vector", lambda e: e.tensor_copy(g["idx4"][:, 1:2], g["cols"][:, 1:2]))
        for kq in range(2):
            p.op("vector", lambda e, kq=kq: e.tensor_scalar(
                g["cols"][:, 6 + kq: 7 + kq], g["cols"][:, 4 + kq: 5 + kq],
                -2.0, None, ALU.add))
            p.op("vector", lambda e, kq=kq: e.tensor_scalar(
                g["cols"][:, 8 + kq: 9 + kq], g["cols"][:, 6 + kq: 7 + kq],
                -0.5, None, ALU.is_ge))
            p.op("vector", lambda e, kq=kq: e.tensor_scalar(
                g["cols"][:, 10 + kq: 11 + kq], g["cols"][:, 6 + kq: 7 + kq],
                1.5, None, ALU.is_le))
            p.op("vector", lambda e, kq=kq: e.tensor_tensor(
                g["cols"][:, 8 + kq: 9 + kq], g["cols"][:, 8 + kq: 9 + kq],
                g["cols"][:, 10 + kq: 11 + kq], ALU.mult))
            p.op("vector", lambda e, kq=kq: e.tensor_scalar(
                g["cols"][:, 10 + kq: 11 + kq], g["cols"][:, 8 + kq: 9 + kq],
                -1.0e7, 1.0e7, ALU.mult, ALU.add))
            p.op("vector", lambda e, kq=kq: e.tensor_tensor(
                g["cols"][:, 6 + kq: 7 + kq], g["cols"][:, 6 + kq: 7 + kq],
                g["cols"][:, 10 + kq: 11 + kq], ALU.add))
            p.op("vector", lambda e, kq=kq: e.tensor_copy(
                g["idx4"][:, 2 + kq: 3 + kq], g["cols"][:, 6 + kq: 7 + kq]))
        p.op("vector", lambda e: e.tensor_scalar(
            g["cols"][:, 12:13], g["cols"][:, 4:5], 1.0, None, ALU.is_equal))
        p.op("vector", lambda e: e.tensor_tensor(
            g["cols"][:, 12:13], g["cols"][:, 12:13], g["cols"][:, 2:3], ALU.mult))
        p.op("vector", lambda e: e.tensor_scalar(
            g["cols"][:, 13:14], g["cols"][:, 5:6], 1.0, None, ALU.is_equal))
        p.op("vector", lambda e: e.tensor_tensor(
            g["cols"][:, 13:14], g["cols"][:, 13:14], g["cols"][:, 3:4], ALU.mult))
        p.op("vector", lambda e: e.tensor_tensor(
            g["cols"][:, 12:13], g["cols"][:, 12:13], g["cols"][:, 13:14], ALU.add))
        p.op("vector", lambda e: e.memset(g["grow1"][:], 0.0))
        p.op("vector", lambda e: e.memset(g["grow2"][:], 0.0))
        p.op("vector", lambda e: e.memset(g["cvrow1"][:], 0.0))
        p.op("vector", lambda e: e.memset(g["cvrow2"][:], 0.0))
        idx_d = p.now("V")
        gts = []
        for kq, (dst, icol, srcn, breg) in enumerate((
                ("grow1", 0, "eo_all", beo), ("grow2", 1, "eo_all", beo),
                ("cvrow1", 2, "cvm", bcv), ("cvrow2", 3, "cvm", bcv))):
            gts.append(p.op("gpsimd", lambda e, dst=dst, icol=icol, srcn=srcn, breg=breg:
                e.indirect_dma_start(
                    out=g[dst][:], out_offset=None,
                    in_=g[srcn],
                    in_offset=IndirectOffsetOnAxis(ap=g["idx4"][:, icol:icol + 1], axis=0),
                    bounds_check=bcreg[breg], oob_is_err=False),
                wait=[idx_d] + list(gath_extra), dma_="mld"))
        ld0 = list(first_extra) if tt == 0 else []
        xld = p.op("sync", lambda e, tt=tt: e.dma_start(
            out=g["xnfo_t"][:], in_=g["xnfown"][tt * 128:(tt + 1) * 128, :]),
            wait=prev_x2 + ld0, dma_="mld")
        x1ld = p.op("sync", lambda e, tt=tt: e.dma_start(
            out=g["x1t"][:], in_=g["x1own"][tt * 128:(tt + 1) * 128, :]),
            wait=prev_x2 + ld0, dma_="mld")
        s1ld = p.op("sync", lambda e, tt=tt: e.dma_start(
            out=g["s1t"][:], in_=g["s1own"][tt * 128:(tt + 1) * 128, :]),
            wait=prev_x2 + ld0, dma_="mld")
        p.op("vector", lambda e: e.tensor_tensor(
            g["grow1"][:], g["grow1"][:], g["cvrow1"][:], ALU.add), wait=gts)
        p.op("vector", lambda e: e.tensor_tensor(
            g["grow2"][:], g["grow2"][:], g["cvrow2"][:], ALU.add))
        p.op("vector", lambda e: e.tensor_scalar(
            g["comb"][:], g["grow1"][:], g["cols"][:, 2:3], None, ALU.mult))
        p.op("vector", lambda e: e.tensor_scalar(
            g["grow2"][:], g["grow2"][:], g["cols"][:, 3:4], None, ALU.mult))
        p.op("vector", lambda e: e.tensor_tensor(
            g["comb"][:], g["comb"][:], g["grow2"][:], ALU.add))
        p.op("vector", lambda e: e.tensor_scalar(
            g["xnfo_t"][:], g["xnfo_t"][:], g["cols"][:, 12:13], None, ALU.mult),
            wait=[xld])
        p.op("vector", lambda e: e.tensor_tensor(
            g["comb"][:], g["comb"][:], g["xnfo_t"][:], ALU.add))
        p.op("vector", lambda e: e.tensor_tensor(
            g["cols"][:, 14:15], g["cols"][:, 2:3], g["cols"][:, 3:4], ALU.add))
        p.op("vector", lambda e: e.tensor_scalar(
            g["cols"][:, 15:16], g["cols"][:, 14:15], 0.0, None, ALU.is_gt))
        p.op("vector", lambda e: e.tensor_tensor(
            g["cols"][:, 14:15], g["cols"][:, 14:15], g["cols"][:, 15:16], ALU.mult))
        p.op("vector", lambda e: e.tensor_scalar(
            g["cols"][:, 15:16], g["cols"][:, 15:16], -1.0, 1.0, ALU.mult, ALU.add))
        p.op("vector", lambda e: e.tensor_tensor(
            g["cols"][:, 14:15], g["cols"][:, 14:15], g["cols"][:, 15:16], ALU.add))
        p.op("vector", lambda e: e.reciprocal(g["cols"][:, 15:16], g["cols"][:, 14:15]))
        p.op("vector", lambda e: e.tensor_scalar(
            g["comb"][:], g["comb"][:], g["cols"][:, 15:16], None, ALU.mult))
        cmb_d = p.now("V")
        p.op("scalar", lambda e: e.activation(
            g["comb"][:], g["comb"][:], AF.Silu), wait=[cmb_d], inc="S")
        p.op("vector", lambda e, sl=sl: e.tensor_copy(
            g["x2t"][:, sl * D:(sl + 1) * D], g["comb"][:]), wait=[p.now("S")])
        p.op("vector", lambda e, sl=sl: e.tensor_tensor(
            g["x2t"][:, sl * D:(sl + 1) * D], g["x2t"][:, sl * D:(sl + 1) * D],
            g["s1t"][:], ALU.mult), wait=[s1ld])
        p.op("vector", lambda e, sl=sl: e.tensor_tensor(
            g["x2t"][:, sl * D:(sl + 1) * D], g["x2t"][:, sl * D:(sl + 1) * D],
            g["x1t"][:], ALU.add), wait=[x1ld])
        x2_d = p.now("V")
        prev_x2 = [p.op("sync", lambda e, tt=tt, sl=sl: e.dma_start(
            out=g["x2_sh"][tt * 128:(tt + 1) * 128, :],
            in_=g["x2t"][:, sl * D:(sl + 1) * D]), wait=[x2_d], dma_="mst")]
    return prev_x2


def build_cmb(detect_races=True):
    nc = bass.Bass("TRN2", target_bir_lowering=False, detect_race_conditions=detect_races)
    es = contextlib.ExitStack()
    g = {}

    def din(name, shape, dtype):
        g[name] = nc.dram_tensor(name, list(shape), dtype, kind="ExternalInput").ap()

    def dout(name, shape, dtype):
        g[name] = nc.dram_tensor(name, list(shape), dtype, kind="ExternalOutput").ap()

    def dram(name, shape, dtype):
        g[name] = nc.dram_tensor(name, list(shape), dtype, kind="Internal").ap()

    def sb(name, shape, dtype):
        g[name] = es.enter_context(nc.sbuf_tensor(name, list(shape), dtype))

    def ps(name, shape, dtype):
        g[name] = es.enter_context(nc.psum_tensor(name, list(shape), dtype))

    din("xnf_all", (NCORES * L, D), BF16)
    din("rout_all", (32, L), F32)
    din("eo_all", (24576, D), BF16)
    din("cvm", (2, D), F32)
    din("x1own", (L, D), F32)
    din("s1own", (L, D), BF16)
    din("xnfown", (L, D), BF16)
    din("ohc", (128, 1), F32)
    dout("x2_sh", (L, D), F32)
    dram("ext_scr", (6, L), F32)

    with es:
        p = Prog(nc, es)
        block = es.enter_context(nc.Block())
        es_rt = contextlib.ExitStack()

        def sb_arr(name, shape, dtype):
            g[name] = es_rt.enter_context(nc.sbuf_tensor(name, list(shape), dtype, side="right"))

        routing_buffers(sb, sb_arr, din, want_w=True)
        ps("pR", (128, 512), F32)
        sb("ohc_s", (128, 1), F32)
        sb("cols", (128, 16), F32)
        sb("idx4", (128, 4), U32)
        sb("grow1", (128, D), BF16)
        sb("grow2", (128, D), BF16)
        sb("cvrow1", (128, D), BF16)
        sb("cvrow2", (128, D), BF16)
        sb("xnfo_t", (128, D), BF16)
        sb("x1t", (128, D), F32)
        sb("s1t", (128, D), BF16)
        sb("comb", (128, D), F32)
        sb("x2t", (128, 2 * D), F32)

        emit_routing_loads(p, g)
        p.op("sync", lambda e: e.dma_start(out=g["ohc_s"][:], in_=g["ohc"]), dma_="mld")
        rt_done = emit_routing(p, g, nc, want_flat_rows=True, want_w=True)

        bcreg = [None, None]

        def mkregs(e):
            bcreg[0] = e.alloc_register(name="bc0")
            e.reg_mov(bcreg[0], 24575)
            bcreg[1] = e.alloc_register(name="bc1")
            e.reg_mov(bcreg[1], 1)
            return None
        p.op("gpsimd", mkregs)

        ex_st = emit_own_extract(nc, p, g, rt_done)
        rtT = p.now("T")
        rtV = p.now("V")
        es_rt.close()

        emit_combine(nc, p, g, bcreg, 0, 1, ex_st, [rtT, rtV], [])
        p.op("sync", lambda e: None, wait=p.dma_all("mst") + p.dma_all("mld"))
        p.flush(block)
    return nc


# version 17
# speedup vs baseline: 1.1355x; 1.1355x over previous
import contextlib
import numpy as np
import ml_dtypes
import concourse.bass as bass
import concourse.mybir as mybir
from concourse.bass import IndirectOffsetOnAxis
from concourse.bass_utils import run_bass_kernel_spmd

F32 = mybir.dt.float32
BF16 = mybir.dt.bfloat16
I32 = mybir.dt.int32
U32 = mybir.dt.uint32
AF = mybir.ActivationFunctionType
ALU = mybir.AluOpType
AX = mybir.AxisListType

B, L, D, H, HD = 8, 2048, 1024, 16, 64
NT = L // 128
ND = D // 128
NCORES = 8
LN_EPS = 1e-5
ATTN_EPS = 1e-6

DEBUG_TAPS = ()



class Prog:
    ENGINES = ("tensor", "vector", "scalar", "gpsimd", "sync")

    def __init__(self, nc, es):
        self.nc = nc
        self.es = es
        self.q = {e: [] for e in self.ENGINES}
        self.sems = {}
        self.scount = {}

    def sem(self, name):
        if name not in self.sems:
            self.sems[name] = self.es.enter_context(self.nc.semaphore(name))
            self.scount[name] = 0
        return self.sems[name]

    ENGSEM = {"vector": "V", "scalar": "S", "gpsimd": "P", "tensor": "T"}

    def op(self, eng, fn, wait=(), inc=None, inc_amt=1, dma_=None):
        if dma_ is not None:
            return self.dma(eng, fn, dma_, wait=wait)
        wait = list(wait)
        inc2 = None
        if eng in self.ENGSEM and inc_amt != 16:
            sname = self.ENGSEM[eng]
            assert inc is None or inc == sname, (eng, inc)
            self.sem(sname)
            if eng != "tensor":
                wait.append((sname, self.scount[sname]))
            self.scount[sname] += 1
            inc2 = (sname, 1)
        elif inc is not None:
            self.sem(inc)
            self.scount[inc] += inc_amt
            inc2 = (inc, inc_amt)
        for sn, _ in wait:
            self.sem(sn)
        wcopy = tuple(wait)

        def run(e, wcopy=wcopy, fn=fn, inc2=inc2):
            for (sn, cnt) in wcopy:
                e.wait_ge(self.sems[sn], cnt)
            r = fn(e)
            if inc2 is not None:
                if r is None:
                    r = e.nop()
                r.then_inc(self.sems[inc2[0]], inc2[1])
        self.q[eng].append(run)

    NDMACH = 8

    def dma(self, eng, fn, base, wait=()):
        idx = self.scount.get("#" + base, 0)
        self.scount["#" + base] = idx + 1
        ch = base + str(idx % self.NDMACH)
        self.sem(ch)
        wait = list(wait)
        if idx >= self.NDMACH:
            wait.append((ch, self.scount[ch]))
        self.op(eng, fn, wait=wait, inc=ch, inc_amt=16)
        return (ch, self.scount[ch])

    def dma_all(self, base):
        n = self.scount.get("#" + base, 0)
        out = []
        for c in range(min(n, self.NDMACH)):
            ch = base + str(c)
            out.append((ch, self.scount[ch]))
        return out

    def now(self, semname):
        self.sem(semname)
        return (semname, self.scount[semname])

    def after(self, semname):
        return self.now(semname)

    def flush(self, block):
        for ename in self.ENGINES:
            ops = self.q[ename]
            if not ops:
                continue

            def body(e, ops=tuple(ops)):
                for r in ops:
                    r(e)
            getattr(block, ename)(body)
            self.q[ename] = []



def _bf(x):
    return np.ascontiguousarray(x).astype(ml_dtypes.bfloat16)


def stage_inputs(inputs):
    f = {k: np.asarray(v) for k, v in inputs.items()}
    common = {}
    common["identb"] = np.eye(128, dtype=ml_dtypes.bfloat16)
    common["identf"] = np.eye(128, dtype=np.float32)
    common["WqT"] = np.ascontiguousarray(f["L0_attn_Wq"].T)
    common["WkT"] = np.ascontiguousarray(f["L0_attn_Wk"].T)
    common["WvT"] = np.ascontiguousarray(f["L0_attn_Wv"].T)
    common["WoT"] = np.ascontiguousarray(f["L0_attn_Wo"].T)
    common["WbetaT"] = np.ascontiguousarray(f["L0_attn_Wbeta"].T)
    for nm, key in (("bq", "L0_attn_bq"), ("bk", "L0_attn_bk"), ("bv", "L0_attn_bv"),
                    ("bo", "L0_attn_bo")):
        common[nm] = np.ascontiguousarray(f[key].reshape(ND, 128).T)
    common["bbeta"] = np.ascontiguousarray(f["L0_attn_bbeta"].reshape(16, 1))
    for nm, key in (("wqc", "L0_attn_wq_conv"), ("wkc", "L0_attn_wk_conv"),
                    ("wvc", "L0_attn_wv_conv")):
        common[nm] = np.ascontiguousarray(
            f[key].reshape(ND, 128, 4).transpose(1, 0, 2).reshape(128, ND * 4))
    for nm, key in (("bqc", "L0_attn_bq_conv"), ("bkc", "L0_attn_bk_conv"),
                    ("bvc", "L0_attn_bv_conv")):
        common[nm] = np.ascontiguousarray(f[key].reshape(ND, 128).T)
    common["normw0"] = np.ascontiguousarray(np.tile(f["L0_attn_norm_w"], 2).reshape(128, 1))
    ii = np.arange(128)[:, None]
    jj = np.arange(128)[None, :]
    sameblk = (ii // 64) == (jj // 64)
    common["maskX8"] = np.ascontiguousarray(np.tile(
        (-(((jj < ii) & sameblk).astype(np.float32))), (1, 8)))
    common["maskG8"] = np.ascontiguousarray(np.tile(
        ((ii <= jj) & sameblk).astype(np.float32), (1, 8)))
    common["maskXT8"] = np.ascontiguousarray(np.tile(
        (-(((jj > ii) & sameblk).astype(np.float32))), (1, 8)))
    bd = np.zeros((128, 128), dtype=ml_dtypes.bfloat16)
    bd[0:64, 0:64] = 1; bd[64:128, 64:128] = 1
    common["onesbd"] = bd
    sel = np.zeros((16, ND * 128), dtype=ml_dtypes.bfloat16)
    for ft in range(ND):
        for p in range(128):
            sel[2 * ft + p // 64, ft * 128 + p] = 1
    common["selbeta"] = sel

    common["gate0"] = np.ascontiguousarray(f["L0_ffn_gate"])
    common["W1e"] = np.ascontiguousarray(f["L0_ffn_W1"])
    common["W2e"] = np.ascontiguousarray(f["L0_ffn_W2"])
    common["b1s"] = np.ascontiguousarray(
        f["L0_ffn_b1"].reshape(16, 16, 128).transpose(2, 0, 1).reshape(128, 256))
    common["b2m"] = np.ascontiguousarray(f["L0_ffn_b2"])
    selE = np.zeros((16, 16 * 128), dtype=ml_dtypes.bfloat16)
    for e in range(16):
        selE[e, e * 128:(e + 1) * 128] = 1
    common["selE"] = selE

    common["cw1"] = np.ascontiguousarray(
        f["L1_attn_conv_w"][:, 0, :].reshape(4, ND, 128).transpose(2, 1, 0).reshape(128, ND * 4))
    common["cb1"] = np.ascontiguousarray(f["L1_attn_conv_b"].reshape(ND, 128).T)
    common["WpN"] = np.ascontiguousarray(f["L1_attn_Wp"])
    common["WsN"] = np.ascontiguousarray(f["L1_attn_Ws"])
    common["Wo1"] = np.ascontiguousarray(f["L1_attn_Wo"])
    common["bpB"] = np.ascontiguousarray(f["L1_attn_bp"][0:64].reshape(64, 1))
    common["bpC"] = np.ascontiguousarray(f["L1_attn_bp"][64:128].reshape(64, 1))
    common["bs1"] = np.ascontiguousarray(f["L1_attn_bs"].reshape(64, 1))
    common["adt"] = np.ascontiguousarray(
        np.stack([f["L1_attn_A_log"], f["L1_attn_dt_log"]], axis=1))
    common["bo1"] = np.ascontiguousarray(f["L1_attn_bo"].reshape(ND, 128).T)
    common["rwT"] = np.ascontiguousarray(f["L1_ffn_router_weight"].T)
    common["rbias"] = np.ascontiguousarray(f["L1_ffn_router_bias"].reshape(1, 16))
    common["normw1"] = np.ascontiguousarray(f["L1_ffn_norm_weight"].reshape(1, D))
    common["nw128"] = np.ascontiguousarray(f["L1_ffn_norm_weight"].reshape(ND, 128).T)
    common["iota16"] = np.ascontiguousarray(np.tile(np.arange(16, dtype=np.float32), (128, 1)))
    common["ones1"] = np.ones((1, 128), dtype=ml_dtypes.bfloat16)

    FULLE = [4, 6, 7, 9, 10, 12, 13, 15]
    HALFE = [5, 8, 11, 14]
    Arow = np.full(16, 1.0e7, np.float32)
    Brow = np.zeros(16, np.float32)
    for c, e in enumerate(FULLE):
        Arow[e] = 3072.0 * c
    for i, e in enumerate(HALFE):
        Arow[e] = 3072.0 * (2 * i) + 2048.0
        Brow[e] = 2048.0
    ii16 = np.arange(128)
    common["ecol"] = np.ascontiguousarray((ii16 % 16).astype(np.float32).reshape(128, 1))
    common["Acol"] = np.ascontiguousarray(Arow[(ii16 % 16)].reshape(128, 1))
    common["Bcol"] = np.ascontiguousarray(Brow[(ii16 % 16)].reshape(128, 1))
    Mfix = ((ii16[:, None] % 16 == ii16[None, :] % 16) &
            (ii16[:, None] // 16 < ii16[None, :] // 16)).astype(np.float32)
    common["Mfix"] = np.ascontiguousarray(Mfix)
    gsum = (ii16[:, None] // 16 == ii16[None, :] // 16).astype(np.float32)
    common["gsum"] = np.ascontiguousarray(gsum)
    sel8 = np.zeros((8, 128), dtype=ml_dtypes.bfloat16)
    for ch in range(8):
        sel8[ch, ch * 16:(ch + 1) * 16] = 1
    common["sel8"] = sel8
    ohch = np.zeros((128, 8), np.float32)
    for ch in range(8):
        ohch[ch * 16, ch] = 1
    common["ohch"] = ohch
    common["cvm"] = np.ascontiguousarray(f["L1_ffn_constant_vectors"])
    pp48 = np.arange(48)
    common["ex48sum"] = np.ascontiguousarray(
        (pp48[:, None] % 16 == pp48[None, :] % 16).astype(np.float32))
    common["exgt48"] = np.ascontiguousarray(
        ((pp48[:, None] % 16 == pp48[None, :] % 16) &
         (pp48[None, :] // 16 > pp48[:, None] // 16)).astype(np.float32))
    common["exblk48"] = np.ascontiguousarray(
        (pp48[:, None] // 16 == pp48[None, :] // 16).astype(np.float32))
    common["ecol48"] = np.ascontiguousarray((pp48 % 16).astype(np.float32).reshape(48, 1))
    common["acol48"] = np.ascontiguousarray(
        ((pp48 % 16) * 1024.0).astype(np.float32).reshape(48, 1))
    sel3 = np.zeros((3, 48), dtype=ml_dtypes.bfloat16)
    for kk3 in range(3):
        sel3[kk3, kk3 * 16:(kk3 + 1) * 16] = 1
    common["sel3"] = sel3
    selx = np.zeros((48, 3), dtype=np.float32)
    for kk3 in range(3):
        selx[kk3 * 16, kk3] = 1
    common["selx"] = selx

    shards = np.split(np.ascontiguousarray(f["L0_attn_x"]), NCORES, axis=0)
    in_maps = []
    for c in range(NCORES):
        m = dict(common)
        m["x"] = np.ascontiguousarray(shards[c][0])
        in_maps.append(m)
    return in_maps



def build_program(taps=(), detect_races=True, n_exp=16, fused=False):
    nc = bass.Bass("TRN2", target_bir_lowering=False, detect_race_conditions=detect_races)
    es = contextlib.ExitStack()
    g = {}

    def din(name, shape, dtype):
        g[name] = nc.dram_tensor(name, list(shape), dtype, kind="ExternalInput").ap()
        return g[name]

    def dout(name, shape, dtype):
        g[name] = nc.dram_tensor(name, list(shape), dtype, kind="ExternalOutput").ap()
        return g[name]

    def dram(name, shape, dtype):
        g[name] = nc.dram_tensor(name, list(shape), dtype, kind="Internal").ap()
        return g[name]

    def sb(name, shape, dtype):
        g[name] = es.enter_context(nc.sbuf_tensor(name, list(shape), dtype))
        return g[name]

    def ps(name, shape, dtype):
        g[name] = es.enter_context(nc.psum_tensor(name, list(shape), dtype))
        return g[name]

    din("x", (L, D), F32)
    din("identb", (128, 128), BF16)
    din("identf", (128, 128), F32)
    for nm in ("WqT", "WkT", "WvT", "WoT"):
        din(nm, (D, D), F32)
    din("WbetaT", (D, 16), F32)
    for nm in ("bq", "bk", "bv", "bo", "bqc", "bkc", "bvc"):
        din(nm, (128, ND), F32)
    din("bbeta", (16, 1), F32)
    for nm in ("wqc", "wkc", "wvc"):
        din(nm, (128, ND * 4), F32)
    din("normw0", (128, 1), F32)
    din("maskX8", (128, 1024), F32)
    din("maskG8", (128, 1024), F32)
    din("selbeta", (16, ND * 128), BF16)
    din("onesbd", (128, 128), BF16)
    din("gate0", (D, 16), F32)
    din("W1e", (16, D, 2048), F32)
    din("W2e", (16, 2048, D), F32)
    din("b1s", (128, 256), F32)
    din("b2m", (16, D), F32)
    din("selE", (16, 16 * 128), BF16)
    din("ex48sum", (48, 48), F32)
    din("exgt48", (48, 48), F32)
    din("exblk48", (48, 48), F32)
    din("ecol48", (48, 1), F32)
    din("acol48", (48, 1), F32)
    din("sel3", (3, 48), BF16)
    din("selx", (48, 3), F32)
    din("cw1", (128, ND * 4), F32)
    din("cb1", (128, ND), F32)
    din("WpN", (D, 128), F32)
    din("WsN", (D, 64), F32)
    din("Wo1", (64, D), F32)
    din("bpB", (64, 1), F32)
    din("bpC", (64, 1), F32)
    din("bs1", (64, 1), F32)
    din("adt", (64, 2), F32)
    din("bo1", (128, ND), F32)
    din("rwT", (D, 16), F32)
    din("rbias", (1, 16), F32)
    din("normw1", (1, D), F32)
    din("nw128", (128, ND), F32)
    din("iota16", (128, 16), F32)
    din("ones1", (1, 128), BF16)

    with es:
        p = Prog(nc, es)
        block = es.enter_context(nc.Block())

        es_xnf = contextlib.ExitStack()
        g["xnFMb"] = es_xnf.enter_context(
            nc.sbuf_tensor("xnFMb", [128, ND * 2048], BF16, side="right"))
        sb("ident_b", (128, 128), BF16)
        sb("ident_f", (128, 128), F32)
        sb("stat", (128, 6 * NT), F32)
        sb("sq_scr", (128, 1024), BF16)
        sb("bias_qkv", (128, ND * 8), F32)
        sb("wconv", (128, ND * 4 * 3), F32)
        sb("bbeta_s", (16, 1), F32)
        sb("normw_s", (128, 1), F32)

        es_p0 = contextlib.ExitStack()
        g["xTM"] = es_p0.enter_context(nc.sbuf_tensor("xTM", [128, NT * 1024], F32, side="right"))
        g["xnTMb"] = es_p0.enter_context(nc.sbuf_tensor("xnTMb", [128, NT * 1024], BF16, side="right"))

        es_attn = contextlib.ExitStack()

        def sba(name, shape, dtype):
            g[name] = es_attn.enter_context(nc.sbuf_tensor(name, list(shape), dtype))
            return g[name]

        def psa(name, shape, dtype):
            g[name] = es_attn.enter_context(nc.psum_tensor(name, list(shape), dtype))
            return g[name]

        sba("wq_bf", (128, ND * 1024), BF16)
        sba("wk_bf", (128, ND * 1024), BF16)
        sba("wv_bf", (128, ND * 1024), BF16)
        sba("wbeta_bf", (128, ND * 16), BF16)
        sba("selb_bf", (16, ND * 128), BF16)
        sba("onesbd_b", (128, 128), BF16)

        g["pT0"] = es_p0.enter_context(nc.psum_tensor("pT0", [128, 128], BF16, side="right"))
        g["pT1"] = es_p0.enter_context(nc.psum_tensor("pT1", [128, 128], BF16, side="right"))

        for tt in range(NT):
            p.op("sync", lambda e, tt=tt: e.dma_start(
                out=g["xTM"][:, tt * 1024:(tt + 1) * 1024],
                in_=g["x"][tt * 128:(tt + 1) * 128, :]), dma_="ld")
        p.op("sync", lambda e: e.dma_start(out=g["ident_b"][:], in_=g["identb"]), dma_="ld")
        p.op("sync", lambda e: e.dma_start(out=g["ident_f"][:], in_=g["identf"]), dma_="ld")
        for kc in range(ND):
            for wname, dname in (("wq_bf", "WqT"), ("wk_bf", "WkT"), ("wv_bf", "WvT")):
                p.op("gpsimd", lambda e, kc=kc, wname=wname, dname=dname: e.dma_start(
                    out=g[wname][:, kc * 1024:(kc + 1) * 1024],
                    in_=g[dname][kc * 128:(kc + 1) * 128, :]), dma_="wld")
            p.op("gpsimd", lambda e, kc=kc: e.dma_start(
                out=g["wbeta_bf"][:, kc * 16:(kc + 1) * 16],
                in_=g["WbetaT"][kc * 128:(kc + 1) * 128, :]), dma_="wld")
        p.op("gpsimd", lambda e: e.dma_start(out=g["selb_bf"][:], in_=g["selbeta"]), dma_="wld")
        p.op("gpsimd", lambda e: e.dma_start(out=g["onesbd_b"][:], in_=g["onesbd"]), dma_="wld")
        for j, nm in enumerate(("bq", "bk", "bv", "bo", "bqc", "bkc", "bvc")):
            p.op("gpsimd", lambda e, j=j, nm=nm: e.dma_start(
                out=g["bias_qkv"][:, j * ND:(j + 1) * ND], in_=g[nm]), dma_="wld")
        for j, nm in enumerate(("wqc", "wkc", "wvc")):
            p.op("gpsimd", lambda e, j=j, nm=nm: e.dma_start(
                out=g["wconv"][:, j * ND * 4:(j + 1) * ND * 4], in_=g[nm]), dma_="wld")
        p.op("gpsimd", lambda e: e.dma_start(out=g["bbeta_s"][:], in_=g["bbeta"]), dma_="wld")
        p.op("gpsimd", lambda e: e.dma_start(out=g["normw_s"][:], in_=g["normw0"]), dma_="wld")
        wld_all = None
        ld_all = None

        for tt in range(NT):
            wt = list(p.dma_all("ld"))[:0] or p.dma_all("ld")
            p.op("scalar", lambda e, tt=tt: e.activation(
                g["sq_scr"][:, 0:1024], g["xTM"][:, tt * 1024:(tt + 1) * 1024],
                AF.Square, accum_out=g["stat"][:, NT + tt:NT + tt + 1]), wait=wt)
            p.op("scalar", lambda e, tt=tt: e.activation(
                g["xnTMb"][:, tt * 1024:(tt + 1) * 1024], g["xTM"][:, tt * 1024:(tt + 1) * 1024],
                AF.Copy, accum_out=g["stat"][:, tt:tt + 1]), wait=wt)
        stats_done = p.now("S")
        MEAN, RSTD, TMP = 2 * NT, 3 * NT, 4 * NT
        p.op("vector", lambda e: e.tensor_scalar(
            g["stat"][:, MEAN:MEAN + NT], g["stat"][:, 0:NT], 1.0 / D, None, ALU.mult),
            wait=[stats_done])
        p.op("vector", lambda e: e.tensor_tensor(
            g["stat"][:, TMP:TMP + NT], g["stat"][:, MEAN:MEAN + NT],
            g["stat"][:, MEAN:MEAN + NT], ALU.mult))
        p.op("vector", lambda e: e.tensor_scalar(
            g["stat"][:, RSTD:RSTD + NT], g["stat"][:, NT:NT + NT], 1.0 / D, None, ALU.mult))
        p.op("vector", lambda e: e.tensor_tensor(
            g["stat"][:, RSTD:RSTD + NT], g["stat"][:, RSTD:RSTD + NT],
            g["stat"][:, TMP:TMP + NT], ALU.subtract))
        p.op("vector", lambda e: e.tensor_scalar(
            g["stat"][:, RSTD:RSTD + NT], g["stat"][:, RSTD:RSTD + NT], LN_EPS, None, ALU.add))
        p.op("scalar", lambda e: e.activation(
            g["stat"][:, TMP:TMP + NT], g["stat"][:, RSTD:RSTD + NT], AF.Ln),
            wait=[p.now("V")])
        p.op("vector", lambda e: e.tensor_scalar(
            g["stat"][:, TMP:TMP + NT], g["stat"][:, TMP:TMP + NT], -0.5, None, ALU.mult),
            wait=[p.now("S")])
        p.op("scalar", lambda e: e.activation(
            g["stat"][:, RSTD:RSTD + NT], g["stat"][:, TMP:TMP + NT], AF.Exp),
            wait=[p.now("V")])
        rstd_done = p.now("S")

        for tt in range(NT):
            p.op("vector", lambda e, tt=tt: e.tensor_scalar(
                g["xnTMb"][:, tt * 1024:(tt + 1) * 1024],
                g["xTM"][:, tt * 1024:(tt + 1) * 1024],
                g["stat"][:, MEAN + tt:MEAN + tt + 1],
                g["stat"][:, RSTD + tt:RSTD + tt + 1],
                ALU.subtract, ALU.mult), wait=[rstd_done])
        xn_done = p.now("V")
        dram("xn_d", (L, D), BF16)
        xn_st = []
        for tt in range(NT):
            xn_st.append(p.op("sync", lambda e, tt=tt: e.dma_start(
                out=g["xn_d"][tt * 128:(tt + 1) * 128, :],
                in_=g["xnTMb"][:, tt * 1024:(tt + 1) * 1024]), wait=[xn_done], dma_="st"))

        tr_cnt = 0
        evac_done = []
        for ft in range(ND):
            for tt in range(NT):
                buf = "pT0" if tr_cnt % 2 == 0 else "pT1"
                wt = [xn_done]
                if tr_cnt >= 2:
                    wt.append(evac_done[tr_cnt - 2])
                p.op("tensor", lambda e, buf=buf, tt=tt, ft=ft: e.transpose(
                    g[buf][:], g["xnTMb"][:, tt * 1024 + ft * 128: tt * 1024 + ft * 128 + 128],
                    g["ident_b"][:]), wait=wt)
                trd = p.now("T")
                eng = "vector" if tr_cnt % 2 == 0 else "scalar"
                off = ft * 2048 + tt * 128
                if eng == "vector":
                    p.op(eng, lambda e, buf=buf, off=off: e.tensor_copy(
                        g["xnFMb"][:, off:off + 128], g[buf][:]), wait=[trd])
                else:
                    p.op(eng, lambda e, buf=buf, off=off: e.copy(
                        g["xnFMb"][:, off:off + 128], g[buf][:]), wait=[trd])
                evac_done.append(p.now("V" if eng == "vector" else "S"))
                tr_cnt += 1
        xn_fm_done_v = p.now("V")
        xn_fm_done_s = p.now("S")
        xn_fm_done = xn_fm_done_v



        es_p0.close()

        sba("maskX_s", (128, 1024), F32)
        sba("maskG_s", (128, 1024), F32)
        sba("wo_bf", (128, ND * 1024), BF16)
        sba("raw", (128, ND * 520), BF16)
        sba("hist", (128, 72), BF16)
        sba("qn_b", (128, ND * 512), BF16)
        sba("kn_b", (128, ND * 512), BF16)
        sba("kb_b", (128, ND * 512), BF16)
        sba("vb_b", (128, ND * 512), BF16)
        sba("betaB_s", (128, 512), BF16)
        sba("beta16g", (16, 512), BF16)
        sba("Zb0", (128, 1024), F32)
        sba("Zb1", (128, 1024), F32)
        sba("Wb0", (128, 1024), F32)
        sba("Wb1", (128, 1024), F32)
        sba("Gb0", (128, 1024), BF16)
        sba("Gb1", (128, 1024), BF16)
        sba("Ub", (128, 512), BF16)
        sba("Yb", (128, 512), F32)
        sba("VKtm0", (128, 1024), BF16)
        sba("VKtm1", (128, 1024), BF16)
        sba("Tb", (128, 1024), F32)
        sba("Tt0", (128, 1024), F32)
        sba("Tt1", (128, 1024), F32)
        sba("S_f32", (128, 512), F32)
        sba("S_bf", (128, 512), BF16)
        sba("O_grp", (128, 512), F32)
        sba("o_n", (128, 512), BF16)
        sba("o_fm", (128, 1024), BF16)
        sba("s_scr", (128, 1024), BF16)
        sba("nrm_scr", (128, 1024), F32)
        sba("stat2", (128, 64), F32)
        sba("zeros_b", (64, 128), BF16)
        psa("pZa", (128, 512), F32)
        psa("pZb", (128, 512), F32)
        psa("pWa", (128, 512), F32)
        psa("pWb", (128, 512), F32)
        psa("pTa", (128, 512), F32)
        psa("pTb", (128, 512), F32)
        psa("pO", (128, 512), F32)
        psa("pM", (128, 512), F32)
        dram("s_tm_d", (L, D), BF16)

        p.op("sync", lambda e: e.dma_start(out=g["maskX_s"][:], in_=g["maskX8"]),
             wait=[xn_fm_done_v, xn_fm_done_s] + xn_st, dma_="ld")
        p.op("sync", lambda e: e.dma_start(out=g["maskG_s"][:], in_=g["maskG8"]),
             wait=[xn_fm_done_v, xn_fm_done_s] + xn_st, dma_="ld")
        masks_ld = None
        for kc in range(ND):
            p.op("gpsimd", lambda e, kc=kc: e.dma_start(
                out=g["wo_bf"][:, kc * 1024:(kc + 1) * 1024],
                in_=g["WoT"][kc * 128:(kc + 1) * 128, :]),
                wait=[xn_fm_done_v, xn_fm_done_s] + xn_st, dma_="wld")
        wo_ld = None
        p.op("vector", lambda e: e.tensor_scalar(
            g["wo_bf"][:], g["wo_bf"][:], g["normw_s"][:, 0:1], None, ALU.mult),
            wait=p.dma_all("wld"))
        wo_ready = p.now("V")
        p.op("gpsimd", lambda e: e.memset(g["raw"][:], 0.0),
             wait=[xn_fm_done_v, xn_fm_done_s] + xn_st)
        p.op("gpsimd", lambda e: e.memset(g["zeros_b"][:], 0.0),
             wait=[xn_fm_done_v, xn_fm_done_s] + xn_st)
        pads_z = p.now("P")
        dram("w1b_d", (16, 1024, 2048), BF16)
        dram("w2b_d", (16, 2048, 1024), BF16)
        cvt_done = {}
        for ex in range(16):
            e1 = p.op("gpsimd", lambda e, ex=ex: e.dma_start(
                out=g["w1b_d"][ex], in_=g["W1e"][ex]),
                wait=[xn_fm_done_v] if ex == 0 else [], dma_="mwf")
            e2 = p.op("gpsimd", lambda e, ex=ex: e.dma_start(
                out=g["w2b_d"][ex], in_=g["W2e"][ex]), dma_="mwf")
            cvt_done[ex] = [e1, e2]

        pZ_free = [pads_z, pads_z]
        pW_free = pads_z
        pT_free = pads_z
        pO_free = pads_z
        pM_free = pads_z
        O_free = pads_z
        Tt_ready = [pads_z, pads_z]
        VK_ready = [pads_z, pads_z]
        G_ready = [pads_z, pads_z]
        S_bf_ready = None
        s_written = []
        group_ready = None
        prev_group_read = pads_z
        prev_conv_v = pads_z

        s_store_prev = None
        CVJ = {"q": 0, "k": 1, "v": 2}
        BJ = {"q": 0, "k": 1, "v": 2}
        BCJ = {"q": 4, "k": 5, "v": 6}
        RAW = {"q": "raw_q", "k": "raw_k", "v": "raw_v"}
        DST = {"q": "qn_b", "k": "kn_b", "v": "vb_b"}

        for grp in range(4):
            tok0 = grp * 512
            proj_evac = {}
            half = 0
            for tname in ("q", "k", "v"):
                wname = {"q": "wq_bf", "k": "wk_bf", "v": "wv_bf"}[tname]
                dst = DST[tname]
                hco = {"q": 0, "k": 24, "v": 48}[tname]
                if grp > 0:
                    def hrest(e, hco=hco):
                        r3 = g["raw"][:].rearrange("p (f c) -> p f c", c=520)
                        h3 = g["hist"][:, hco:hco + 24].rearrange("p (f c) -> p f c", c=3)
                        return e.tensor_copy(r3[:, :, 5:8], h3)
                    p.op("vector", hrest, wait=[prev_conv_v])
                hist_restored = p.now("V")
                for fo in range(ND):
                    psA = g["pZa" if half == 0 else "pZb"][:]
                    def mmproj(e, wname=wname, fo=fo, tok0=tok0, psA=psA):
                        r = None
                        for kc in range(ND):
                            r = e.matmul(psA, g[wname][:, kc * 1024 + fo * 128: kc * 1024 + fo * 128 + 128],
                                         g["xnFMb"][:, kc * 2048 + tok0: kc * 2048 + tok0 + 512],
                                         start=(kc == 0), stop=(kc == ND - 1))
                        return r
                    p.op("tensor", mmproj, wait=p.dma_all("wld") + [xn_fm_done_v, xn_fm_done_s, pZ_free[half]], inc="T")
                    mmdone = p.now("T")
                    def evproj(e, tname=tname, fo=fo, psA=psA):
                        return e.activation(
                            g["raw"][:, fo * 520 + 8: fo * 520 + 520], psA, AF.Identity,
                            bias=g["bias_qkv"][:, BJ[tname] * ND + fo: BJ[tname] * ND + fo + 1])
                    p.op("scalar", evproj, wait=[mmdone, prev_conv_v, hist_restored], inc="S")
                    pZ_free[half] = p.now("S")
                    proj_evac[(tname, fo)] = pZ_free[half]
                    half ^= 1
                for ft in range(ND):
                    wbase = CVJ[tname] * ND * 4 + ft * 4
                    def cnv0(e, dst=dst, ft=ft, wbase=wbase, tname=tname):
                        return e.tensor_scalar(
                            g[dst][:, ft * 512:(ft + 1) * 512],
                            g["raw"][:, ft * 520 + 8: ft * 520 + 520],
                            g["wconv"][:, wbase + 3: wbase + 4],
                            g["bias_qkv"][:, BCJ[tname] * ND + ft: BCJ[tname] * ND + ft + 1],
                            ALU.mult, ALU.add)
                    p.op("vector", cnv0, wait=[proj_evac[(tname, ft)], prev_group_read, hist_restored])
                    for tap in (2, 1, 0):
                        def cnvi(e, dst=dst, ft=ft, wbase=wbase, tap=tap):
                            return e.scalar_tensor_tensor(
                                g[dst][:, ft * 512:(ft + 1) * 512],
                                g["raw"][:, ft * 520 + 5 + tap: ft * 520 + 5 + tap + 512],
                                g["wconv"][:, wbase + tap: wbase + tap + 1],
                                g[dst][:, ft * 512:(ft + 1) * 512],
                                ALU.mult, ALU.add)
                        p.op("vector", cnvi)
                def hsave(e, hco=hco):
                    r3 = g["raw"][:].rearrange("p (f c) -> p f c", c=520)
                    h3 = g["hist"][:, hco:hco + 24].rearrange("p (f c) -> p f c", c=3)
                    return e.tensor_copy(h3, r3[:, :, 517:520])
                p.op("vector", hsave)
                prev_conv_v = p.now("V")
            conv_all = p.now("V")

            def mmbeta(e, tok0=tok0):
                r = None
                for kc in range(ND):
                    r = e.matmul(g["pM"][0:16, 0:512], g["wbeta_bf"][:, kc * 16:(kc + 1) * 16],
                                 g["xnFMb"][:, kc * 2048 + tok0: kc * 2048 + tok0 + 512],
                                 start=(kc == 0), stop=(kc == ND - 1))
                return r
            p.op("tensor", mmbeta, wait=p.dma_all("wld") + [xn_fm_done_v, xn_fm_done_s, pM_free], inc="T")
            p.op("scalar", lambda e: e.activation(
                g["beta16g"][:], g["pM"][0:16, 0:512], AF.Sigmoid,
                bias=g["bbeta_s"][:, 0:1]), wait=[p.now("T")], inc="S")
            beta16_done = p.now("S")
            pM_free = beta16_done

            nrm_free = p.now("V")
            for ft in range(ND):
                conv_q = conv_all
                conv_v = conv_all
                p.op("scalar", lambda e, ft=ft: e.activation(
                    g["sq_scr"][:, 0:512], g["kn_b"][:, ft * 512:(ft + 1) * 512], AF.Square),
                    wait=[conv_q, p.now("T")], inc="S")
                p.op("tensor", lambda e, ft=ft: e.matmul(
                    g["pO"][:], g["onesbd_b"][:], g["sq_scr"][:, 0:512],
                    start=True, stop=True), wait=p.dma_all("wld") + [p.now("S"), pO_free], inc="T")
                p.op("scalar", lambda e: e.activation(
                    g["nrm_scr"][:, 0:512], g["pO"][:], AF.Sqrt),
                    wait=[p.now("T"), nrm_free], inc="S")
                pO_free = p.now("S")
                p.op("vector", lambda e: e.tensor_scalar(
                    g["nrm_scr"][:, 0:512], g["nrm_scr"][:, 0:512], ATTN_EPS, None, ALU.add),
                    wait=[pO_free], inc="V")
                p.op("vector", lambda e: e.reciprocal(
                    g["nrm_scr"][:, 0:512], g["nrm_scr"][:, 0:512]), inc="V")
                kinv_done = p.now("V")
                p.op("scalar", lambda e, ft=ft: e.activation(
                    g["sq_scr"][:, 512:1024], g["qn_b"][:, ft * 512:(ft + 1) * 512], AF.Square),
                    wait=[conv_q, p.now("T")], inc="S")
                p.op("tensor", lambda e, ft=ft: e.matmul(
                    g["pTa"][:], g["onesbd_b"][:], g["sq_scr"][:, 512:1024],
                    start=True, stop=True), wait=p.dma_all("wld") + [p.now("S"), pT_free], inc="T")
                p.op("scalar", lambda e: e.activation(
                    g["nrm_scr"][:, 512:1024], g["pTa"][:], AF.Sqrt),
                    wait=[p.now("T"), nrm_free], inc="S")
                pT_free = p.now("S")
                p.op("vector", lambda e: e.tensor_scalar(
                    g["nrm_scr"][:, 512:1024], g["nrm_scr"][:, 512:1024], ATTN_EPS, None, ALU.add),
                    wait=[pT_free], inc="V")
                p.op("vector", lambda e: e.reciprocal(
                    g["nrm_scr"][:, 512:1024], g["nrm_scr"][:, 512:1024]), inc="V")
                p.op("scalar", lambda e, ft=ft: e.activation(
                    g["vb_b"][:, ft * 512:(ft + 1) * 512], g["vb_b"][:, ft * 512:(ft + 1) * 512],
                    AF.Silu), wait=[conv_v], inc="S")
                silu_done = p.now("S")
                p.op("vector", lambda e, ft=ft: e.tensor_tensor(
                    g["kn_b"][:, ft * 512:(ft + 1) * 512], g["kn_b"][:, ft * 512:(ft + 1) * 512],
                    g["nrm_scr"][:, 0:512], ALU.mult), wait=[kinv_done], inc="V")
                p.op("vector", lambda e, ft=ft: e.tensor_tensor(
                    g["vb_b"][:, ft * 512:(ft + 1) * 512], g["vb_b"][:, ft * 512:(ft + 1) * 512],
                    g["nrm_scr"][:, 0:512], ALU.mult), wait=[silu_done], inc="V")
                p.op("vector", lambda e, ft=ft: e.tensor_tensor(
                    g["qn_b"][:, ft * 512:(ft + 1) * 512], g["qn_b"][:, ft * 512:(ft + 1) * 512],
                    g["nrm_scr"][:, 512:1024], ALU.mult), inc="V")
                kqv_done = p.now("V")
                p.op("tensor", lambda e, ft=ft: e.matmul(
                    g["pM"][:], g["selb_bf"][:, ft * 128:(ft + 1) * 128], g["beta16g"][:],
                    start=True, stop=True), wait=p.dma_all("wld") + [beta16_done, pM_free], inc="T")
                p.op("vector", lambda e: e.tensor_copy(
                    g["betaB_s"][:], g["pM"][:]), wait=[p.now("T")], inc="V")
                pM_free = p.now("V")
                p.op("vector", lambda e, ft=ft: e.tensor_tensor(
                    g["kb_b"][:, ft * 512:(ft + 1) * 512], g["kn_b"][:, ft * 512:(ft + 1) * 512],
                    g["betaB_s"][:], ALU.mult), wait=[kqv_done], inc="V")
                p.op("vector", lambda e, ft=ft: e.tensor_tensor(
                    g["vb_b"][:, ft * 512:(ft + 1) * 512], g["vb_b"][:, ft * 512:(ft + 1) * 512],
                    g["betaB_s"][:], ALU.mult), inc="V")
                nrm_free = p.now("V")
            group_ready = p.now("V")
            prev_group_read = group_ready

            def HSg(h, cc):
                return (h % 2) * 64, (h // 2) * 512 + cc * 64

            PS = {"pZ": list(pZ_free), "pW": [pW_free], "pT": [pT_free],
                  "pM": [pM_free], "pO": [pO_free], "O_free": O_free,
                  "Tb_rd": pads_z, "on_free": pads_z,
                  "Yb_free": pads_z, "Ub_free": pads_z,
                  "Ub_done": None, "Yb_done": None, "O_done": None,
                  "sstore": s_store_prev}

            def prep_main(cc):
                par = cc % 2
                def amm(e, cc=cc):
                    r = None
                    for pl in range(8):
                        pz = g["pZa" if pl < 4 else "pZb"]
                        c0 = (pl % 4) * 128
                        r = e.matmul(pz[:, c0: c0 + 128],
                                     g["zeros_b"][:], g["kn_b"][0:64, 0:128],
                                     start=True, stop=False, skip_group_check=True)
                        for sub in range(2):
                            h = 2 * pl + sub
                            r0, cs = HSg(h, cc)
                            r = e.matmul(pz[r0:r0 + 64, c0 + r0: c0 + r0 + 64],
                                         g["kb_b"][r0:r0 + 64, cs:cs + 64],
                                         g["kn_b"][r0:r0 + 64, cs:cs + 64],
                                         start=False, stop=(sub == 1), skip_group_check=True)
                    return r
                p.op("tensor", amm, wait=[group_ready] + PS["pZ"], inc="T")
                _t = p.now("T")
                p.op("vector", lambda e: e.tensor_tensor(
                    g["Zb0"][:, 0:512], g["pZa"][:], g["maskX_s"][:, 0:512], ALU.mult),
                    wait=[_t] + p.dma_all("ld"))
                p.op("vector", lambda e: e.tensor_tensor(
                    g["Zb0"][:, 512:1024], g["pZb"][:], g["maskX_s"][:, 512:1024], ALU.mult))
                Z_d = p.now("V")
                PS["pZ"] = [Z_d]
                def ammT(e):
                    r = None
                    for pl in range(8):
                        pw = g["pWa" if pl < 4 else "pWb"]
                        r = e.matmul(pw[:, (pl % 4) * 128: (pl % 4) * 128 + 128],
                                     g["Zb0"][:, pl * 128: pl * 128 + 128],
                                     g["ident_f"][:], start=True, stop=True)
                    return r
                p.op("tensor", ammT, wait=[Z_d] + PS["pW"], inc="T")
                _t = p.now("T")
                p.op("scalar", lambda e: e.copy(g["Wb0"][:, 0:512], g["pWa"][:]),
                     wait=[_t], inc="S")
                p.op("scalar", lambda e: e.copy(g["Wb0"][:, 512:1024], g["pWb"][:]), inc="S")
                W_d = p.now("S")
                PS["pW"] = [W_d]
                for pl in range(8):
                    p.op("vector", lambda e, pl=pl: e.tensor_tensor(
                        g["Tb"][:, pl * 128:(pl + 1) * 128], g["Zb0"][:, pl * 128:(pl + 1) * 128],
                        g["ident_f"][:], ALU.add),
                        wait=[Z_d, PS["Tb_rd"]] if pl == 0 else ())
                T_d = p.now("V")
                Zp, Wp = "Zb0", "Wb0"
                Zp_d, Wp_d, Tp_d = Z_d, W_d, T_d
                for j in range(1, 6):
                    Zc = "Zb1" if j % 2 == 1 else "Zb0"
                    Wc = "Wb1" if j % 2 == 1 else "Wb0"
                    def sqmm(e, Zp=Zp, Wp=Wp):
                        r = None
                        for pl in range(8):
                            pz = g["pZa" if pl < 4 else "pZb"]
                            r = e.matmul(pz[:, (pl % 4) * 128: (pl % 4) * 128 + 128],
                                         g[Wp][:, pl * 128: pl * 128 + 128],
                                         g[Zp][:, pl * 128: pl * 128 + 128],
                                         start=True, stop=True)
                        return r
                    p.op("tensor", sqmm, wait=[Zp_d, Wp_d] + PS["pZ"], inc="T")
                    _t1 = p.now("T")
                    def sqmmT(e, Zp=Zp, Wp=Wp):
                        r = None
                        for pl in range(8):
                            pw = g["pWa" if pl < 4 else "pWb"]
                            r = e.matmul(pw[:, (pl % 4) * 128: (pl % 4) * 128 + 128],
                                         g[Zp][:, pl * 128: pl * 128 + 128],
                                         g[Wp][:, pl * 128: pl * 128 + 128],
                                         start=True, stop=True)
                        return r
                    p.op("tensor", sqmmT, wait=PS["pW"], inc="T")
                    _t2 = p.now("T")
                    p.op("scalar", lambda e, Zc=Zc: e.copy(
                        g[Zc][:, 0:512], g["pZa"][:]), wait=[_t1], inc="S")
                    p.op("scalar", lambda e, Zc=Zc: e.copy(
                        g[Zc][:, 512:1024], g["pZb"][:]), inc="S")
                    Zc_d = p.now("S")
                    PS["pZ"] = [Zc_d]
                    p.op("scalar", lambda e, Wc=Wc: e.copy(
                        g[Wc][:, 0:512], g["pWa"][:]), wait=[_t2], inc="S")
                    p.op("scalar", lambda e, Wc=Wc: e.copy(
                        g[Wc][:, 512:1024], g["pWb"][:]), inc="S")
                    Wc_d = p.now("S")
                    PS["pW"] = [Wc_d]
                    def tmm(e, Wc=Wc):
                        r = None
                        for pl in range(8):
                            pt = g["pTa" if pl < 4 else "pTb"]
                            r = e.matmul(pt[:, (pl % 4) * 128: (pl % 4) * 128 + 128],
                                         g[Wc][:, pl * 128: pl * 128 + 128],
                                         g["Tb"][:, pl * 128: pl * 128 + 128],
                                         start=True, stop=True)
                        return r
                    p.op("tensor", tmm, wait=[Wc_d, Tp_d] + PS["pT"], inc="T")
                    _t3 = p.now("T")
                    p.op("vector", lambda e: e.tensor_tensor(
                        g["Tb"][:, 0:512], g["Tb"][:, 0:512], g["pTa"][:], ALU.add),
                        wait=[_t3])
                    p.op("vector", lambda e: e.tensor_tensor(
                        g["Tb"][:, 512:1024], g["Tb"][:, 512:1024], g["pTb"][:], ALU.add))
                    Tp_d = p.now("V")
                    PS["pT"] = [Tp_d]
                    Zp, Wp = Zc, Wc
                    Zp_d, Wp_d = Zc_d, Wc_d
                def ttr(e):
                    r = None
                    for pl in range(8):
                        pw = g["pWa" if pl < 4 else "pWb"]
                        r = e.matmul(pw[:, (pl % 4) * 128: (pl % 4) * 128 + 128],
                                     g["Tb"][:, pl * 128: pl * 128 + 128],
                                     g["ident_f"][:], start=True, stop=True)
                    return r
                p.op("tensor", ttr, wait=[Tp_d] + PS["pW"], inc="T")
                _t = p.now("T")
                PS["Tb_rd"] = _t
                p.op("scalar", lambda e, par=par: e.copy(
                    g["Tt" + str(par)][:, 0:512], g["pWa"][:]), wait=[_t], inc="S")
                p.op("scalar", lambda e, par=par: e.copy(
                    g["Tt" + str(par)][:, 512:1024], g["pWb"][:]), inc="S")
                Tt_ready[par] = p.now("S")
                PS["pW"] = [Tt_ready[par]]

            def prep_gtrvk(cc):
                par = cc % 2
                def gmm(e, cc=cc):
                    r = None
                    for pl in range(8):
                        pz = g["pZa" if pl < 4 else "pZb"]
                        c0 = (pl % 4) * 128
                        r = e.matmul(pz[:, c0: c0 + 128],
                                     g["zeros_b"][:], g["kn_b"][0:64, 0:128],
                                     start=True, stop=False, skip_group_check=True)
                        for sub in range(2):
                            h = 2 * pl + sub
                            r0, cs = HSg(h, cc)
                            r = e.matmul(pz[r0:r0 + 64, c0 + r0: c0 + r0 + 64],
                                         g["kn_b"][r0:r0 + 64, cs:cs + 64],
                                         g["qn_b"][r0:r0 + 64, cs:cs + 64],
                                         start=False, stop=(sub == 1), skip_group_check=True)
                    return r
                p.op("tensor", gmm, wait=[group_ready] + PS["pZ"], inc="T")
                _t = p.now("T")
                p.op("vector", lambda e, par=par: e.tensor_tensor(
                    g["Gb" + str(par)][:, 0:512], g["pZa"][:], g["maskG_s"][:, 0:512], ALU.mult),
                    wait=[_t] + p.dma_all("ld"))
                p.op("vector", lambda e, par=par: e.tensor_tensor(
                    g["Gb" + str(par)][:, 512:1024], g["pZb"][:], g["maskG_s"][:, 512:1024],
                    ALU.mult))
                G_ready[par] = p.now("V")
                PS["pZ"] = [G_ready[par]]
                def trvk(e, cc=cc):
                    r = None
                    for pl in range(8):
                        for sub in range(2):
                            h = 2 * pl + sub
                            r0, cs = HSg(h, cc)
                            r = e.matmul(g["pZa"][r0:r0 + 64, pl * 64: pl * 64 + 64],
                                         g["vb_b"][r0:r0 + 64, cs:cs + 64],
                                         g["ident_b"][r0:r0 + 64, r0:r0 + 64],
                                         start=True, stop=True, skip_group_check=True)
                            r = e.matmul(g["pZb"][r0:r0 + 64, pl * 64: pl * 64 + 64],
                                         g["kn_b"][r0:r0 + 64, cs:cs + 64],
                                         g["ident_b"][r0:r0 + 64, r0:r0 + 64],
                                         start=True, stop=True, skip_group_check=True)
                    return r
                p.op("tensor", trvk, wait=[group_ready] + PS["pZ"], inc="T")
                _t = p.now("T")
                p.op("scalar", lambda e, par=par: e.copy(
                    g["VKtm" + str(par)][:, 0:512], g["pZa"][:]), wait=[_t], inc="S")
                p.op("scalar", lambda e, par=par: e.copy(
                    g["VKtm" + str(par)][:, 512:1024], g["pZb"][:]), inc="S")
                VK_ready[par] = p.now("S")
                PS["pZ"] = [VK_ready[par]]

            def ser1(cc, c):
                nonlocal S_bf_ready
                par = cc % 2
                if c > 0:
                    def ks0(e, cc=cc):
                        r = None
                        for pl in range(8):
                            for sub in range(2):
                                h = 2 * pl + sub
                                r0, cs = HSg(h, cc)
                                r = e.matmul(g["pM"][r0:r0 + 64, pl * 64: pl * 64 + 64],
                                             g["kb_b"][r0:r0 + 64, cs:cs + 64],
                                             g["S_bf"][r0:r0 + 64, (h // 2) * 64:(h // 2) * 64 + 64],
                                             start=True, stop=True, skip_group_check=True)
                        return r
                    p.op("tensor", ks0, wait=[S_bf_ready, group_ready] + PS["pM"], inc="T")
                    ks0_d = p.now("T")
                    p.op("vector", lambda e, par=par: e.tensor_tensor(
                        g["Yb"][:], g["VKtm" + str(par)][:, 0:512], g["pM"][:], ALU.subtract),
                        wait=[ks0_d, VK_ready[par], PS["Yb_free"]])
                else:
                    p.op("vector", lambda e, par=par: e.tensor_copy(
                        g["Yb"][:], g["VKtm" + str(par)][:, 0:512]),
                        wait=[VK_ready[par], PS["Yb_free"]])
                PS["Yb_done"] = p.now("V")
                PS["pM"] = [PS["Yb_done"]]

            def ser2(cc):
                par = cc % 2
                def umm(e, par=par):
                    r = None
                    for pl in range(8):
                        r = e.matmul(g["pM"][:, pl * 64: pl * 64 + 64],
                                     g["Tt" + str(par)][:, pl * 128: pl * 128 + 128],
                                     g["Yb"][:, pl * 64: pl * 64 + 64],
                                     start=(pl == 0), stop=(pl == 7))
                    return r
                p.op("tensor", umm, wait=[PS["Yb_done"], Tt_ready[par]] + PS["pM"], inc="T")
                _t = p.now("T")
                PS["Yb_free"] = _t
                p.op("scalar", lambda e: e.copy(g["Ub"][:], g["pM"][:]),
                     wait=[_t, PS["Ub_free"]], inc="S")
                PS["Ub_done"] = p.now("S")
                PS["pM"] = [PS["Ub_done"]]

            def ser3(cc, c):
                par = cc % 2
                def omm(e, cc=cc, c=c, par=par):
                    r = None
                    for pl in range(8):
                        if c > 0:
                            for sub in range(2):
                                h = 2 * pl + sub
                                r0, cs = HSg(h, cc)
                                r = e.matmul(g["pO"][r0:r0 + 64, pl * 64: pl * 64 + 64],
                                             g["qn_b"][r0:r0 + 64, cs:cs + 64],
                                             g["S_bf"][r0:r0 + 64, (h // 2) * 64:(h // 2) * 64 + 64],
                                             start=True, stop=False, skip_group_check=True)
                        r = e.matmul(g["pO"][:, pl * 64: pl * 64 + 64],
                                     g["Gb" + str(par)][:, pl * 128: pl * 128 + 128],
                                     g["Ub"][:, pl * 64: pl * 64 + 64],
                                     start=(c == 0), stop=True, skip_group_check=True)
                    return r
                wt = [G_ready[par], PS["Ub_done"]] + PS["pO"]
                if c > 0:
                    wt.append(S_bf_ready)
                p.op("tensor", omm, wait=wt, inc="T")
                p.op("scalar", lambda e: e.copy(g["O_grp"][:], g["pO"][:]),
                     wait=[p.now("T"), PS["O_free"]], inc="S")
                PS["O_done"] = p.now("S")
                PS["pO"] = [PS["O_done"]]

            def ser4(cc, c):
                nonlocal S_bf_ready
                par = cc % 2
                def smm(e, cc=cc, par=par):
                    r = None
                    for pl in range(8):
                        for sub in range(2):
                            h = 2 * pl + sub
                            r0, _ = HSg(h, cc)
                            r = e.matmul(g["pM"][r0:r0 + 64, (h // 2) * 64 - (h // 2) // 8 * 512:
                                                 (h // 2) * 64 - (h // 2) // 8 * 512 + 64],
                                         g["VKtm" + str(par)][r0:r0 + 64, 512 + pl * 64: 512 + pl * 64 + 64],
                                         g["Ub"][r0:r0 + 64, pl * 64: pl * 64 + 64],
                                         start=True, stop=True, skip_group_check=True)
                    return r
                p.op("tensor", smm, wait=[PS["Ub_done"], VK_ready[par]] + PS["pM"], inc="T")
                smm_d = p.now("T")
                PS["Ub_free"] = smm_d
                if c == 0:
                    p.op("vector", lambda e: e.tensor_copy(
                        g["S_f32"][:], g["pM"][:]), wait=[smm_d])
                else:
                    p.op("vector", lambda e: e.tensor_tensor(
                        g["S_f32"][:], g["S_f32"][:], g["pM"][:], ALU.add),
                        wait=[smm_d])
                p.op("vector", lambda e: e.tensor_copy(g["S_bf"][:], g["S_f32"][:]))
                PS["pM"] = [p.now("V")]
                S_bf_ready = p.now("V")

            def tail(cc, c):
                for pl in range(8):
                    p.op("scalar", lambda e, pl=pl: e.activation(
                        g["sq_scr"][:, 0:64],
                        g["O_grp"][:, pl * 64: pl * 64 + 64], AF.Square,
                        accum_out=g["stat2"][:, pl: pl + 1]),
                        wait=[PS["O_done"]] if pl == 0 else (), inc="S")
                p.op("vector", lambda e: e.tensor_scalar(
                    g["stat2"][:, 8:16], g["stat2"][:, 0:8], 1.0 / 64, ATTN_EPS,
                    ALU.mult, ALU.add), wait=[p.now("S")])
                p.op("scalar", lambda e: e.activation(
                    g["stat2"][:, 16:24], g["stat2"][:, 8:16], AF.Ln),
                    wait=[p.now("V")], inc="S")
                p.op("vector", lambda e: e.tensor_scalar(
                    g["stat2"][:, 16:24], g["stat2"][:, 16:24], -0.5, None, ALU.mult),
                    wait=[p.now("S")])
                p.op("scalar", lambda e: e.activation(
                    g["stat2"][:, 24:32], g["stat2"][:, 16:24], AF.Exp),
                    wait=[p.now("V")], inc="S")
                rstdH_done = p.now("S")
                for pl in range(8):
                    p.op("vector", lambda e, pl=pl: e.tensor_scalar(
                        g["o_n"][:, pl * 64: pl * 64 + 64], g["O_grp"][:, pl * 64: pl * 64 + 64],
                        g["stat2"][:, 24 + pl: 25 + pl], None, ALU.mult),
                        wait=[rstdH_done, PS["on_free"]] if pl == 0 else [rstdH_done])
                on_done = p.now("V")
                PS["O_free"] = on_done
                def otr(e):
                    r = None
                    for pl in range(8):
                        for sub in range(2):
                            r0 = sub * 64
                            r = e.matmul(g["pO"][r0:r0 + 64, pl * 64: pl * 64 + 64],
                                         g["o_n"][r0:r0 + 64, pl * 64: pl * 64 + 64],
                                         g["ident_b"][r0:r0 + 64, r0:r0 + 64],
                                         start=True, stop=True, skip_group_check=True)
                    return r
                p.op("tensor", otr, wait=[on_done] + PS["pO"], inc="T")
                otr_d = p.now("T")
                PS["on_free"] = otr_d
                def oev(e, cc=cc):
                    dst = g["o_fm"][:].rearrange("p (pl t) -> p pl t", t=128)
                    return e.tensor_copy(dst[:, :, (cc % 2) * 64:(cc % 2) * 64 + 64],
                                         g["pO"][:, 0:512].rearrange("p (pl t) -> p pl t", t=64))
                p.op("vector", oev, wait=[otr_d])
                ofm_done = p.now("V")
                PS["pO"] = [ofm_done]

                if cc % 2 == 1:
                    def womm(e):
                        r = None
                        for fo in range(ND):
                            pdst = (g["pM"][:, (fo % 4) * 128:(fo % 4) * 128 + 128] if fo < 4
                                    else g["pO"][:, (fo - 4) * 128:(fo - 4) * 128 + 128])
                            for kc in range(ND):
                                r = e.matmul(pdst,
                                             g["wo_bf"][:, kc * 1024 + fo * 128: kc * 1024 + fo * 128 + 128],
                                             g["o_fm"][:, kc * 128: kc * 128 + 128],
                                             start=(kc == 0), stop=(kc == ND - 1))
                        return r
                    p.op("tensor", womm,
                         wait=p.dma_all("wld") + [ofm_done, wo_ready] + PS["pM"] + PS["pO"],
                         inc="T")
                    wodone = p.now("T")
                    def sev(e):
                        r = None
                        for fo in range(ND):
                            psrc = (g["pM"][:, (fo % 4) * 128:(fo % 4) * 128 + 128] if fo < 4
                                    else g["pO"][:, (fo - 4) * 128:(fo - 4) * 128 + 128])
                            r = e.activation(g["s_scr"][:, fo * 128: fo * 128 + 128], psrc,
                                             AF.Identity,
                                             bias=g["bias_qkv"][:, 3 * ND + fo: 3 * ND + fo + 1])
                        return r
                    p.op("scalar", sev,
                         wait=[wodone, PS["sstore"]] if PS["sstore"] else [wodone], inc="S")
                    sfm_done = p.now("S")
                    def strn(e):
                        r = None
                        for fo in range(ND):
                            pdst = (g["pM"][:, (fo % 4) * 128:(fo % 4) * 128 + 128] if fo < 4
                                    else g["pO"][:, (fo - 4) * 128:(fo - 4) * 128 + 128])
                            r = e.matmul(pdst, g["s_scr"][:, fo * 128: fo * 128 + 128],
                                         g["ident_b"][:], start=True, stop=True)
                        return r
                    p.op("tensor", strn, wait=[sfm_done], inc="T")
                    strn_d = p.now("T")
                    p.op("vector", lambda e: e.tensor_copy(g["s_scr"][:, 0:512], g["pM"][:]),
                         wait=[strn_d])
                    p.op("vector", lambda e: e.tensor_copy(g["s_scr"][:, 512:1024], g["pO"][:]))
                    stm_done = p.now("V")
                    PS["pM"] = [stm_done]
                    PS["pO"] = [stm_done]
                    PS["sstore"] = p.op("sync", lambda e, c=c: e.dma_start(
                        out=g["s_tm_d"][(c - 1) * 64:(c + 1) * 64, :], in_=g["s_scr"][:]),
                        wait=[stm_done], dma_="st")

            prep_main(0)
            prep_main(1)
            prep_gtrvk(0)
            for cc in range(8):
                c = grp * 8 + cc
                ser1(cc, c)
                ser2(cc)
                if cc + 2 < 8:
                    prep_main(cc + 2)
                ser3(cc, c)
                ser4(cc, c)
                if cc + 1 < 8:
                    prep_gtrvk(cc + 1)
                tail(cc, c)
            s_store_prev = PS["sstore"]
            pZ_free = [PS["pZ"][-1], PS["pZ"][-1]]
            pW_free = PS["pW"][-1]
            pT_free = PS["pT"][-1]
            pM_free = PS["pM"][-1]
            pO_free = PS["pO"][-1]
            O_free = PS["O_free"]
            prev_group_read = p.now("T")
            prev_conv_v = p.now("V")
        attn_done_list = p.dma_all("st")
        attn_T = p.now("T")
        attn_V = p.now("V")
        attn_S = p.now("S")
        attn_P = p.now("P")
        attn_end = [attn_T, attn_V, attn_S, attn_P] + attn_done_list
        es_attn.close()

        es_ffn = contextlib.ExitStack()

        def sb(name, shape, dtype, _es=es_ffn):
            g[name] = _es.enter_context(nc.sbuf_tensor(name, list(shape), dtype))
            return g[name]

        def ps(name, shape, dtype, _es=es_ffn):
            g[name] = _es.enter_context(nc.psum_tensor(name, list(shape), dtype))
            return g[name]

        sb("gate_s", (128, ND * 16), BF16)
        sb("b1_s", (128, 256), F32)
        sb("b2_s", (16, 1024), BF16)
        sb("b2c", (128, 128), F32)
        sb("gcols", (128, 48), F32)
        sb("fcol_sb", (128, 48), U32)
        sb("ecl_scr", (128, 4), F32)
        sb("iotaF", (128, 16), F32)
        sb("gstat", (128, 64), F32)
        sb("nrm2", (128, 48), F32)
        sb("xt_s", (128, 2048), F32)
        sb("sin_s", (128, 1024), BF16)
        sb("x1_s", (128, 2048), F32)
        sb("sg_s", (128, 1024), BF16)
        sb("gtile", (128, 1024), BF16)
        sb("xn1t", (128, 1024), BF16)
        ps("pH0", (128, 512), F32)
        ps("pH1", (128, 512), F32)
        ps("pOu0", (128, 512), F32)
        ps("pOu1", (128, 512), F32)
        ps("pAux", (128, 512), F32)
        ps("pTr2", (128, 1024), BF16)
        dram("x1_d", (L, D), F32)
        dram("xn1_d", (L, D), BF16)
        dram("rout0_d", (3, L), F32)
        dram("idxmap0", (16 * 1024, 1), U32)
        dram("eo0_d", (16 * 1024, D), BF16)

        for kc in range(ND):
            p.op("gpsimd", lambda e, kc=kc: e.dma_start(
                out=g["gate_s"][:, kc * 16:(kc + 1) * 16],
                in_=g["gate0"][kc * 128:(kc + 1) * 128, :]), wait=attn_end, dma_="wf")
        p.op("sync", lambda e: e.dma_start(out=g["b1_s"][:], in_=g["b1s"]),
             wait=attn_end, dma_="ld")
        p.op("gpsimd", lambda e: e.dma_start(out=g["b2_s"][:], in_=g["b2m"]),
             wait=attn_end, dma_="wf")
        p.op("sync", lambda e: e.dma_start(out=g["iotaF"][:], in_=g["iota16"]),
             wait=attn_end, dma_="ld")
        ffn_ld = p.dma_all("wf") + p.dma_all("ld")
        for dt in range(ND):
            p.op("tensor", lambda e, dt=dt: e.transpose(
                g["pTr2"][:, 0:16], g["b2_s"][0:16, dt * 128:(dt + 1) * 128],
                g["ident_b"][0:16, 0:16]), wait=ffn_ld + [p.now("V")], inc="T")
            p.op("vector", lambda e, dt=dt: e.tensor_copy(
                g["b2c"][:, dt * 16:(dt + 1) * 16], g["pTr2"][:, 0:16]),
                wait=[p.now("T")])
        b2c_done = p.now("V")

        rt0_st = []
        for tt in range(NT):
            def glog(e, tt=tt):
                r = None
                for kc in range(ND):
                    r = e.matmul(g["pAux"][:, 0:16],
                                 g["xnFMb"][:, kc * 2048 + tt * 128: kc * 2048 + tt * 128 + 128],
                                 g["gate_s"][:, kc * 16:(kc + 1) * 16],
                                 start=(kc == 0), stop=(kc == ND - 1))
                return r
            p.op("tensor", glog, wait=ffn_ld + attn_end + [("V", p.scount["V"])], inc="T")
            glog_d = p.now("T")
            GS = g["gstat"]
            p.op("vector", lambda e: e.tensor_reduce(
                GS[:, 0:1], g["pAux"][:, 0:16], AX.X, ALU.max),
                wait=[glog_d] + (rt0_st[-3:] if tt > 0 else []))
            p.op("vector", lambda e: e.tensor_scalar(
                g["nrm2"][:, 0:16], g["pAux"][:, 0:16], GS[:, 0:1], None, ALU.subtract))
            p.op("scalar", lambda e: e.activation(
                g["nrm2"][:, 16:32], g["nrm2"][:, 0:16], AF.Exp), wait=[p.now("V")])
            p.op("vector", lambda e: e.tensor_reduce(
                GS[:, 1:2], g["nrm2"][:, 16:32], AX.X, ALU.add), wait=[p.now("S")])
            p.op("vector", lambda e: e.reciprocal(GS[:, 2:3], GS[:, 1:2]))
            p.op("vector", lambda e: e.tensor_scalar(
                g["nrm2"][:, 0:16], g["nrm2"][:, 16:32], GS[:, 2:3], None, ALU.mult))
            p.op("vector", lambda e: e.tensor_copy(g["nrm2"][:, 16:32], g["nrm2"][:, 0:16]))
            for kk in range(3):
                p.op("vector", lambda e, kk=kk: e.tensor_reduce(
                    GS[:, 3 + kk: 4 + kk], g["nrm2"][:, 16:32], AX.X, ALU.max))
                p.op("vector", lambda e, kk=kk: e.tensor_scalar(
                    g["nrm2"][:, 32:48], g["nrm2"][:, 16:32], GS[:, 3 + kk: 4 + kk],
                    None, ALU.is_ge))
                p.op("vector", lambda e: e.tensor_scalar(
                    g["nrm2"][:, 32:48], g["nrm2"][:, 32:48], -99.0, 99.0,
                    ALU.mult, ALU.add))
                p.op("vector", lambda e: e.tensor_tensor(
                    g["nrm2"][:, 32:48], g["nrm2"][:, 32:48], g["iotaF"][:], ALU.add))
                p.op("vector", lambda e, kk=kk: e.tensor_reduce(
                    g["ecl_scr"][:, kk:kk + 1], g["nrm2"][:, 32:48], AX.X, ALU.min))
                p.op("vector", lambda e, kk=kk: e.tensor_scalar(
                    g["nrm2"][:, 32:48], g["iotaF"][:], g["ecl_scr"][:, kk:kk + 1],
                    None, ALU.is_equal))
                p.op("vector", lambda e: e.tensor_scalar(
                    g["nrm2"][:, 32:48], g["nrm2"][:, 32:48], 9.0, None, ALU.mult))
                p.op("vector", lambda e: e.tensor_tensor(
                    g["nrm2"][:, 16:32], g["nrm2"][:, 16:32], g["nrm2"][:, 32:48],
                    ALU.subtract))
            p.op("vector", lambda e: e.tensor_tensor(
                GS[:, 6:7], GS[:, 3:4], GS[:, 4:5], ALU.add))
            p.op("vector", lambda e: e.tensor_tensor(
                GS[:, 6:7], GS[:, 6:7], GS[:, 5:6], ALU.add))
            p.op("vector", lambda e: e.tensor_scalar(
                GS[:, 6:7], GS[:, 6:7], 1e-5, None, ALU.max))
            p.op("vector", lambda e: e.reciprocal(GS[:, 7:8], GS[:, 6:7]))
            p.op("vector", lambda e, tt=tt: e.tensor_scalar(
                g["gcols"][:, tt * 3: tt * 3 + 3], GS[:, 3:6], GS[:, 7:8],
                None, ALU.mult))
            rpk = p.now("V")
            for kk in range(3):
                rt0_st.append(p.op("sync", lambda e, tt=tt, kk=kk: e.dma_start(
                    out=g["rout0_d"][kk, tt * 128:(tt + 1) * 128].rearrange(
                        "(p one) -> p one", one=1),
                    in_=g["ecl_scr"][:, kk:kk + 1]), wait=[rpk], dma_="fst"))
        gate_done = p.now("V")

        es_rt0 = contextlib.ExitStack()

        def sbr(name, shape, dtype, _es=es_rt0):
            g[name] = _es.enter_context(nc.sbuf_tensor(name, list(shape), dtype, side="right"))

        for nm in ("R1", "R2", "R3", "R4", "R5"):
            sbr(nm, (48, 2048), F32)
        sbr("r3b", (3, 2048), BF16)
        sbr("ex48sum_s", (48, 48), F32)
        sbr("exgt48_s", (48, 48), F32)
        sbr("exblk48_s", (48, 48), F32)
        sbr("ecol48_s", (48, 1), F32)
        sbr("acol48_s", (48, 1), F32)
        sbr("sel3_s", (3, 48), BF16)
        sbr("selx_s", (48, 3), F32)
        sbr("zinitF", (128, 128), U32)
        sbr("tokids0", (128, 16), U32)
        sbr("c3s", (128, 4), F32)
        for nm in ("ex48sum", "exgt48", "exblk48", "ecol48", "acol48", "sel3", "selx"):
            p.op("sync", lambda e, nm=nm: e.dma_start(out=g[nm + "_s"][:], in_=g[nm]),
                 wait=attn_end, dma_="ld")
        rc_ld = p.dma_all("ld")
        bcF = [None, None]

        def mkregsF(e):
            bcF[0] = e.alloc_register(name="bf0")
            e.reg_mov(bcF[0], L - 1)
            bcF[1] = e.alloc_register(name="bf1")
            e.reg_mov(bcF[1], 16 * 1024 - 1)
            return None
        p.op("gpsimd", mkregsF)
        p.op("gpsimd", lambda e: e.memset(g["zinitF"][:], 10000000), wait=attn_end)
        p.op("gpsimd", lambda e: e.iota(
            g["tokids0"][:], [[128, 16]], base=0, channel_multiplier=1))
        ztok_d = p.now("P")
        i0init = [p.op("sync", lambda e: e.dma_start(
            out=g["idxmap0"].rearrange("(p t) one -> p (t one)", p=128),
            in_=g["zinitF"][:]), wait=[ztok_d], dma_="fst")]
        p.op("gpsimd", lambda e: e.dma_start(out=g["r3b"][0:3, :], in_=g["rout0_d"]),
             wait=rt0_st, dma_="gld")
        r3_ld = p.dma_all("gld") + p.dma_all("ld")
        for q in range(4):
            p.op("tensor", lambda e, q=q: e.matmul(
                g["pH0"][0:48, 0:512], g["sel3_s"][:],
                g["r3b"][0:3, q * 512:(q + 1) * 512],
                start=True, stop=True), wait=r3_ld + [p.now("V")], inc="T")
            p.op("vector", lambda e, q=q: e.tensor_copy(
                g["R1"][:, q * 512:(q + 1) * 512], g["pH0"][0:48, 0:512]),
                wait=[p.now("T")])
        p.op("vector", lambda e: e.tensor_scalar(
            g["R2"][:], g["R1"][:], g["ecol48_s"][:, 0:1], None, ALU.is_equal),
            wait=rc_ld)
        p.op("vector", lambda e: e.tensor_copy(g["R3"][:], g["R2"][:]))
        cur, nxt = "R3", "R4"
        for kk in range(11):
            sh = 1 << kk
            p.op("vector", lambda e, cur=cur, nxt=nxt, sh=sh: e.tensor_copy(
                g[nxt][:, 0:sh], g[cur][:, 0:sh]))
            p.op("vector", lambda e, cur=cur, nxt=nxt, sh=sh: e.tensor_tensor(
                g[nxt][:, sh:2048], g[cur][:, sh:2048], g[cur][:, 0:2048 - sh], ALU.add))
            cur, nxt = nxt, cur
        if cur != "R3":
            p.op("vector", lambda e: e.tensor_copy(g["R3"][:], g["R4"][:]))
        cs0_d = p.now("V")
        for srcn, dstn, mat in (("R3", "R5", "ex48sum_s"), ("R2", "R4", "exgt48_s")):
            for q in range(4):
                p.op("tensor", lambda e, srcn=srcn, mat=mat, q=q: e.matmul(
                    g["pH0"][0:48, 0:512], g[mat][:],
                    g[srcn][:, q * 512:(q + 1) * 512],
                    start=True, stop=True), wait=[cs0_d, p.now("V")], inc="T")
                p.op("vector", lambda e, dstn=dstn, q=q: e.tensor_copy(
                    g[dstn][:, q * 512:(q + 1) * 512], g["pH0"][0:48, 0:512]),
                    wait=[p.now("T")])
        p.op("vector", lambda e: e.tensor_tensor(g["R5"][:], g["R5"][:], g["R4"][:], ALU.subtract))
        p.op("vector", lambda e: e.tensor_tensor(g["R5"][:], g["R5"][:], g["R2"][:], ALU.mult))
        p.op("vector", lambda e: e.tensor_scalar(
            g["R4"][:], g["R2"][:], g["acol48_s"][:, 0:1], None, ALU.mult))
        nn_d = p.now("V")
        for srcn, dstn in (("R5", "R1"), ("R4", "R4")):
            for q in range(4):
                p.op("tensor", lambda e, srcn=srcn, q=q: e.matmul(
                    g["pH0"][0:48, 0:512], g["exblk48_s"][:],
                    g[srcn][:, q * 512:(q + 1) * 512],
                    start=True, stop=True), wait=[nn_d, p.now("V")], inc="T")
                p.op("vector", lambda e, dstn=dstn, q=q: e.tensor_copy(
                    g[dstn][:, q * 512:(q + 1) * 512], g["pH0"][0:48, 0:512]),
                    wait=[p.now("T")])
        p.op("vector", lambda e: e.tensor_scalar(
            g["R3"][:], g["R1"][:], 1024.5, None, ALU.is_gt))
        p.op("vector", lambda e: e.tensor_scalar(
            g["R3"][:], g["R3"][:], 1.0e7, None, ALU.mult))
        p.op("vector", lambda e: e.tensor_tensor(g["R5"][:], g["R1"][:], g["R4"][:], ALU.add))
        p.op("vector", lambda e: e.tensor_tensor(g["R5"][:], g["R5"][:], g["R3"][:], ALU.add))
        p.op("vector", lambda e: e.tensor_scalar(
            g["R5"][:], g["R5"][:], -1.0, None, ALU.add))
        flat0_d = p.now("V")
        for tr in range(NT):
            p.op("tensor", lambda e, tr=tr: e.matmul(
                g["pH0"][:, 0:3],
                g["R5"][:, tr * 128:(tr + 1) * 128],
                g["selx_s"][:],
                start=True, stop=True), wait=[flat0_d, p.now("V")], inc="T")
            p.op("vector", lambda e, tr=tr: e.tensor_copy(
                g["c3s"][:, 0:3], g["pH0"][:, 0:3]), wait=[p.now("T")])
            p.op("vector", lambda e, tr=tr: e.tensor_copy(
                g["fcol_sb"][:, tr * 3: tr * 3 + 3], g["c3s"][:, 0:3]))
            sc_v = p.now("V")
            for kk in range(3):
                p.op("gpsimd", lambda e, tr=tr, kk=kk: e.indirect_dma_start(
                    out=g["idxmap0"],
                    out_offset=IndirectOffsetOnAxis(
                        ap=g["fcol_sb"][:, tr * 3 + kk: tr * 3 + kk + 1], axis=0),
                    in_=g["tokids0"][:, tr: tr + 1],
                    in_offset=None,
                    bounds_check=bcF[1], oob_is_err=False),
                    wait=[sc_v, ztok_d] + i0init, dma_="msc")
        sc0_done = p.dma_all("msc") + i0init
        rt0_T = p.now("T")
        rt0_V = p.now("V")
        es_rt0.close()

        sb("xgt0", (128, 2048), BF16)
        sb("xgfA", (128, ND * 1024), BF16)
        sb("xgfB", (128, ND * 1024), BF16)
        sb("w_ring", (128, 4 * 8192), BF16)
        sb("h_buf", (128, 16 * 512), BF16)
        sb("eofm0", (128, ND * 512), BF16)
        sb("eot0", (128, 1024), BF16)
        sb("obufF", (128, 8), U32)
        sb("obufG", (128, 8), U32)

        EE = n_exp
        slot_free = [[rt0_T], [rt0_T], [rt0_T], [rt0_T]]
        xgf_free = {0: [rt0_T, rt0_V], 1: [rt0_T, rt0_V]}
        xgf_ready = {0: None, 1: None}
        eo_st = []
        prev_h_read = rt0_T
        gat_prev = []
        gat_d = []
        tr_prev = []
        for rt in range(8):
            ob = "obufF" if rt % 2 == 0 else "obufG"
            xo = (rt % 2) * 1024
            ld = p.op("sync", lambda e, rt=rt, ob=ob: e.dma_start(
                out=g[ob][:, 0:1],
                in_=g["idxmap0"][rt * 128: rt * 128 + 128, :]),
                wait=sc0_done + (gat_d[-2:-1] if rt >= 2 else []), dma_="ld")
            p.op("vector", lambda e, xo=xo: e.memset(g["xgt0"][:, xo:xo + 1024], 0.0),
                 wait=[ld] + (xgf_free[0] if rt == 0 else []) + tr_prev)
            z_d = p.now("V")
            gt = p.op("gpsimd", lambda e, ob=ob, xo=xo: e.indirect_dma_start(
                out=g["xgt0"][:, xo:xo + 1024], out_offset=None,
                in_=g["xn_d"],
                in_offset=IndirectOffsetOnAxis(ap=g[ob][:, 0:1], axis=0),
                bounds_check=bcF[0], oob_is_err=False),
                wait=[z_d], dma_="gld")
            def xtr0(e, xo=xo):
                r = None
                for fo in range(ND):
                    r = e.transpose(g["pTr2"][:, fo * 128: fo * 128 + 128],
                                    g["xgt0"][:, xo + fo * 128: xo + fo * 128 + 128],
                                    g["ident_b"][:])
                return r
            p.op("tensor", xtr0, wait=[gt, p.now("V")], inc="T")
            tr_prev = [p.now("T")]
            def xev0(e, rt=rt):
                dst = g["xgfA"][:].rearrange("p (f t) -> p f t", t=1024)
                return e.tensor_copy(dst[:, :, rt * 128:(rt + 1) * 128],
                                     g["pTr2"][:].rearrange("p (f t) -> p f t", t=128))
            p.op("vector", xev0, wait=[p.now("T")])
            gat_d.append(p.now("V"))
        gat_prev = gat_d[-2:]
        xgf_ready[0] = p.now("V")
        for ex in range(EE):
            par = ex % 2
            xgf = "xgfA" if par == 0 else "xgfB"
            gex = ex + 1
            if gex < EE:
                gpar = gex % 2
                gxgf = "xgfA" if gpar == 0 else "xgfB"
                gat_d = []
                tr_prev = []
                for rt in range(8):
                    ob = "obufF" if rt % 2 == 0 else "obufG"
                    xo = (rt % 2) * 1024
                    ld = p.op("scalar", lambda e, gex=gex, rt=rt, ob=ob: e.dma_start(
                        out=g[ob][:, 0:1],
                        in_=g["idxmap0"][gex * 1024 + rt * 128: gex * 1024 + rt * 128 + 128, :]),
                        wait=sc0_done + (gat_d[-2:-1] if rt >= 2 else gat_prev), dma_="ld")
                    p.op("vector", lambda e, xo=xo: e.memset(g["xgt0"][:, xo:xo + 1024], 0.0),
                         wait=[ld] + (xgf_free[gpar] if rt == 0 else []) + tr_prev)
                    z_d = p.now("V")
                    gt = p.op("gpsimd", lambda e, ob=ob, xo=xo: e.indirect_dma_start(
                        out=g["xgt0"][:, xo:xo + 1024], out_offset=None,
                        in_=g["xn_d"],
                        in_offset=IndirectOffsetOnAxis(ap=g[ob][:, 0:1], axis=0),
                        bounds_check=bcF[0], oob_is_err=False),
                        wait=[z_d], dma_="gld")
                    def xtr0(e, xo=xo):
                        r = None
                        for fo in range(ND):
                            r = e.transpose(g["pTr2"][:, fo * 128: fo * 128 + 128],
                                            g["xgt0"][:, xo + fo * 128: xo + fo * 128 + 128],
                                            g["ident_b"][:])
                        return r
                    p.op("tensor", xtr0, wait=[gt, p.now("V")], inc="T")
                    tr_prev = [p.now("T")]
                    def xev0(e, rt=rt, gxgf=gxgf):
                        dst = g[gxgf][:].rearrange("p (f t) -> p f t", t=1024)
                        return e.tensor_copy(dst[:, :, rt * 128:(rt + 1) * 128],
                                             g["pTr2"][:].rearrange("p (f t) -> p f t", t=128))
                    p.op("vector", xev0, wait=[p.now("T")])
                    gat_d.append(p.now("V"))
                gat_prev = gat_d[-2:]
                xgf_ready[gpar] = p.now("V")
            xgf_done = xgf_ready[par]
            slab_ld = []
            for si, (wt_name, r0, c0) in enumerate((
                    ("w1b_d", 0, 0), ("w1b_d", 0, 1024),
                    ("w2b_d", 0, 0), ("w2b_d", 1024, 0))):
                ent = []
                for kc in range(ND):
                    ent.append(p.op("sync", lambda e, ex=ex, si=si, kc=kc, wt_name=wt_name,
                                    r0=r0, c0=c0: e.dma_start(
                        out=g["w_ring"][:, si * 8192 + kc * 1024: si * 8192 + kc * 1024 + 1024],
                        in_=g[wt_name][ex, r0 + kc * 128: r0 + kc * 128 + 128, c0:c0 + 1024]),
                        wait=slot_free[si] + cvt_done[ex], dma_="wfs"))
                slab_ld.append(ent)
            for tc in range(2):
                for ht in range(16):
                    si = 0 if ht < 8 else 1
                    pH = "pH0" if ht % 2 == 0 else "pH1"
                    def w1mm(e, tc=tc, ht=ht, si=si, pH=pH, xgf=xgf):
                        r = None
                        for kc in range(ND):
                            r = e.matmul(
                                g[pH][:],
                                g["w_ring"][:, si * 8192 + kc * 1024 + (ht % 8) * 128:
                                            si * 8192 + kc * 1024 + (ht % 8) * 128 + 128],
                                g[xgf][:, kc * 1024 + tc * 512: kc * 1024 + tc * 512 + 512],
                                start=(kc == 0), stop=(kc == ND - 1))
                        return r
                    wt = list(slab_ld[si]) + [xgf_done, prev_h_read]
                    if ht >= 2:
                        wt.append(hrelu[ht - 2])
                    p.op("tensor", w1mm, wait=wt, inc="T")
                    w1_d = p.now("T")
                    p.op("scalar", lambda e, ex=ex, ht=ht, pH=pH: e.activation(
                        g["h_buf"][:, ht * 512:(ht + 1) * 512], g[pH][:], AF.Relu,
                        bias=g["b1_s"][:, ex * 16 + ht: ex * 16 + ht + 1]),
                        wait=[w1_d], inc="S")
                    if ht == 0:
                        hrelu = {}
                    hrelu[ht] = p.now("S")
                hdone = p.now("S")
                if tc == 1:
                    w1_lastuse = p.now("T")
                for dt in range(ND):
                    pOu = "pOu0" if dt % 2 == 0 else "pOu1"
                    def w2mm(e, dt=dt, pOu=pOu):
                        r = None
                        for hc in range(16):
                            si = 2 if hc < 8 else 3
                            r = e.matmul(
                                g[pOu][:],
                                g["w_ring"][:, si * 8192 + (hc % 8) * 1024 + dt * 128:
                                            si * 8192 + (hc % 8) * 1024 + dt * 128 + 128],
                                g["h_buf"][:, hc * 512:(hc + 1) * 512],
                                start=(hc == 0), stop=(hc == 15))
                        return r
                    wt = list(slab_ld[2]) + list(slab_ld[3]) + [hdone]
                    if dt >= 2:
                        wt.append(eoev[dt - 2])
                    p.op("tensor", w2mm, wait=wt, inc="T")
                    w2_d = p.now("T")
                    p.op("scalar", lambda e, ex=ex, dt=dt, pOu=pOu: e.activation(
                        g["eofm0"][:, dt * 512:(dt + 1) * 512], g[pOu][:], AF.Identity,
                        bias=g["b2c"][:, dt * 16 + ex: dt * 16 + ex + 1]),
                        wait=[w2_d, b2c_done], inc="S")
                    if dt == 0:
                        eoev = {}
                    eoev[dt] = p.now("S")
                eofm_done = p.now("S")
                prev_h_read = p.now("T")
                for st4 in range(4):
                    def etr0(e, st4=st4):
                        r = None
                        for dt in range(ND):
                            r = e.transpose(g["pTr2"][:, dt * 128: dt * 128 + 128],
                                            g["eofm0"][:, dt * 512 + st4 * 128: dt * 512 + st4 * 128 + 128],
                                            g["ident_b"][:])
                        return r
                    p.op("tensor", etr0, wait=[eofm_done, p.now("V")], inc="T")
                    p.op("vector", lambda e: e.tensor_copy(g["eot0"][:], g["pTr2"][:]),
                         wait=[p.now("T")] + eo_st[-1:])
                    ev_d = p.now("V")
                    eo_st.append(p.op("sync", lambda e, ex=ex, tc=tc, st4=st4: e.dma_start(
                        out=g["eo0_d"][ex * 1024 + tc * 512 + st4 * 128:
                                       ex * 1024 + tc * 512 + st4 * 128 + 128, :],
                        in_=g["eot0"][:]), wait=[ev_d], dma_="fst"))
            xgf_free[par] = [p.now("T")]
            slot_free = [[w1_lastuse], [w1_lastuse], [p.now("T")], [p.now("T")]]
        eo0_all = p.dma_all("fst")

        sb("grw0", (128, 1024), BF16)
        sb("grw1", (128, 1024), BF16)
        sb("grw2", (128, 1024), BF16)
        sb("combF", (128, 1024), F32)
        SUM, SQ, MEAN1, RSTD1, TMP1 = 0, NT, 2 * NT, 3 * NT, 4 * NT
        xin_prev = {}
        sinrd = {}
        x1store = {}
        xn1store = {}
        for tt in range(NT):
            sl = tt % 2
            wt_in = [gate_done] if tt < 2 else [xin_prev[tt - 2]]
            p.op("sync", lambda e, tt=tt, sl=sl: e.dma_start(
                out=g["xt_s"][:, sl * 1024:(sl + 1) * 1024],
                in_=g["x"][tt * 128:(tt + 1) * 128, :]), wait=wt_in, dma_="ld")
            sld = p.op("sync", lambda e, tt=tt: e.dma_start(
                out=g["sin_s"][:],
                in_=g["s_tm_d"][tt * 128:(tt + 1) * 128, :]),
                wait=wt_in + attn_done_list +
                     ([sinrd[tt - 1]] if tt >= 1 else []), dma_="ld")
            tile_ld = p.dma_all("ld")
            gts = []
            for kk in range(3):
                p.op("vector", lambda e, kk=kk: e.memset(g["grw" + str(kk)][:], 0.0),
                     wait=(eo0_all + [rt0_V] if tt == 0 and kk == 0 else []))
                z_d = p.now("V")
                gts.append(p.op("gpsimd", lambda e, tt=tt, kk=kk: e.indirect_dma_start(
                    out=g["grw" + str(kk)][:], out_offset=None,
                    in_=g["eo0_d"],
                    in_offset=IndirectOffsetOnAxis(
                        ap=g["fcol_sb"][:, tt * 3 + kk: tt * 3 + kk + 1], axis=0),
                    bounds_check=bcF[1], oob_is_err=False),
                    wait=[z_d] + eo0_all, dma_="gld"))
            p.op("vector", lambda e, tt=tt: e.tensor_scalar(
                g["combF"][:], g["grw0"][:], g["gcols"][:, tt * 3: tt * 3 + 1],
                None, ALU.mult), wait=gts + [gate_done])
            p.op("vector", lambda e, tt=tt: e.scalar_tensor_tensor(
                g["combF"][:], g["grw1"][:], g["gcols"][:, tt * 3 + 1: tt * 3 + 2],
                g["combF"][:], ALU.mult, ALU.add))
            p.op("vector", lambda e, tt=tt: e.scalar_tensor_tensor(
                g["combF"][:], g["grw2"][:], g["gcols"][:, tt * 3 + 2: tt * 3 + 3],
                g["combF"][:], ALU.mult, ALU.add))
            cmb0_d = p.now("V")
            p.op("scalar", lambda e: e.activation(
                g["gtile"][:], g["combF"][:], AF.Silu), wait=[cmb0_d], inc="S")
            gt_d = p.now("S")
            p.op("vector", lambda e: e.tensor_tensor(
                g["sg_s"][:], g["sin_s"][:], g["gtile"][:],
                ALU.mult), wait=[gt_d, sld] + tile_ld)
            sinrd[tt] = p.now("V")
            p.op("vector", lambda e, sl=sl: e.tensor_tensor(
                g["x1_s"][:, sl * 1024:(sl + 1) * 1024], g["xt_s"][:, sl * 1024:(sl + 1) * 1024],
                g["sg_s"][:], ALU.add),
                wait=[x1store[tt - 2]] if tt >= 2 else ())
            x1_d_v = p.now("V")
            p.op("scalar", lambda e, tt=tt, sl=sl: e.activation(
                g["sq_scr"][:, 0:1024], g["x1_s"][:, sl * 1024:(sl + 1) * 1024],
                AF.Square, accum_out=g["stat"][:, SQ + tt: SQ + tt + 1]), wait=[x1_d_v])
            p.op("scalar", lambda e, tt=tt, sl=sl: e.activation(
                g["sq_scr"][:, 0:1024], g["x1_s"][:, sl * 1024:(sl + 1) * 1024],
                AF.Copy, accum_out=g["stat"][:, SUM + tt: SUM + tt + 1]), inc="S")
            st_d = p.now("S")
            p.op("vector", lambda e, tt=tt: e.tensor_scalar(
                g["stat"][:, MEAN1 + tt: MEAN1 + tt + 1], g["stat"][:, SUM + tt: SUM + tt + 1],
                1.0 / D, None, ALU.mult), wait=[st_d])
            p.op("vector", lambda e, tt=tt: e.tensor_tensor(
                g["stat"][:, TMP1 + tt: TMP1 + tt + 1], g["stat"][:, MEAN1 + tt: MEAN1 + tt + 1],
                g["stat"][:, MEAN1 + tt: MEAN1 + tt + 1], ALU.mult))
            p.op("vector", lambda e, tt=tt: e.tensor_scalar(
                g["stat"][:, RSTD1 + tt: RSTD1 + tt + 1], g["stat"][:, SQ + tt: SQ + tt + 1],
                1.0 / D, None, ALU.mult))
            p.op("vector", lambda e, tt=tt: e.tensor_tensor(
                g["stat"][:, RSTD1 + tt: RSTD1 + tt + 1], g["stat"][:, RSTD1 + tt: RSTD1 + tt + 1],
                g["stat"][:, TMP1 + tt: TMP1 + tt + 1], ALU.subtract))
            p.op("vector", lambda e, tt=tt: e.tensor_scalar(
                g["stat"][:, RSTD1 + tt: RSTD1 + tt + 1], g["stat"][:, RSTD1 + tt: RSTD1 + tt + 1],
                LN_EPS, None, ALU.add))
            p.op("scalar", lambda e, tt=tt: e.activation(
                g["stat"][:, TMP1 + tt: TMP1 + tt + 1], g["stat"][:, RSTD1 + tt: RSTD1 + tt + 1],
                AF.Ln), wait=[p.now("V")])
            p.op("vector", lambda e, tt=tt: e.tensor_scalar(
                g["stat"][:, TMP1 + tt: TMP1 + tt + 1], g["stat"][:, TMP1 + tt: TMP1 + tt + 1],
                -0.5, None, ALU.mult), wait=[p.now("S")])
            p.op("scalar", lambda e, tt=tt: e.activation(
                g["stat"][:, RSTD1 + tt: RSTD1 + tt + 1], g["stat"][:, TMP1 + tt: TMP1 + tt + 1],
                AF.Exp), wait=[p.now("V")], inc="S")
            p.op("vector", lambda e, tt=tt, sl=sl: e.tensor_scalar(
                g["xn1t"][:], g["x1_s"][:, sl * 1024:(sl + 1) * 1024],
                g["stat"][:, MEAN1 + tt: MEAN1 + tt + 1],
                g["stat"][:, RSTD1 + tt: RSTD1 + tt + 1],
                ALU.subtract, ALU.mult),
                wait=[p.now("S")] + ([xn1store[tt - 1]] if tt >= 1 else []))
            xn1_v = p.now("V")
            x1store[tt] = p.op("sync", lambda e, tt=tt, sl=sl: e.dma_start(
                out=g["x1_d"][tt * 128:(tt + 1) * 128, :],
                in_=g["x1_s"][:, sl * 1024:(sl + 1) * 1024]), wait=[xn1_v], dma_="st")
            xn1store[tt] = p.op("sync", lambda e, tt=tt: e.dma_start(
                out=g["xn1_d"][tt * 128:(tt + 1) * 128, :], in_=g["xn1t"][:]),
                wait=[xn1_v], dma_="st")
            xin_prev[tt] = xn1store[tt]
        l0_done = p.dma_all("st")
        l0_T = p.now("T")
        l0_V = p.now("V")
        l0_S = p.now("S")
        l0_end = [l0_T, l0_V, l0_S] + l0_done
        es_ffn.close()
        es_xnf.close()

        es_l1 = contextlib.ExitStack()

        def sb(name, shape, dtype, _es=es_l1):
            g[name] = _es.enter_context(nc.sbuf_tensor(name, list(shape), dtype))
            return g[name]

        def ps(name, shape, dtype, _es=es_l1):
            g[name] = _es.enter_context(nc.psum_tensor(name, list(shape), dtype))
            return g[name]

        sb("xn1fm", (128, ND * 2056), BF16)
        sb("cnv1", (128, ND * 2048), BF16)
        sb("wl1", (128, 8 * 128 + 8 * 64 + 8 * 1024 + 8 * 16), BF16)
        sb("cwb", (128, ND * 4 + 3 * ND), F32)
        sb("colv", (64, 16), F32)
        sb("dpow", (64, 16), F32)
        sb("BCb", (64, 2 * 2048), BF16)
        sb("u_f", (64, 2048), F32)
        sb("w_f", (64, 2048), F32)
        sb("hp0", (64, 2048), F32)
        sb("hp1", (64, 2048), F32)
        sb("y_b", (64, 2048), BF16)
        sb("scr1", (64, 2048), F32)
        sb("s1scr", (128, 1024), BF16)
        sb("s1fm", (128, ND * 512), BF16)
        sb("xt1", (128, 2048), BF16)
        sb("xnf_t", (128, 1024), BF16)
        sb("nwb", (128, D), BF16)
        sb("rb16", (128, 16), F32)
        sb("iota_s", (128, 16), F32)
        sb("rstat", (128, 64), F32)
        sb("rcl", (128, 128), F32)
        sb("rlg", (128, 256), F32)
        sb("rsel", (128, 256), F32)
        sb("ones_s", (1, 128), BF16)
        ps("q1", (128, 512), F32)
        ps("q2", (128, 512), F32)
        ps("q3", (128, 512), F32)
        ps("qTr", (128, 1024), BF16)
        dram("s1_d", (L, D), BF16)
        dram("xnf_d", (L, D), BF16)
        dram("rout_d", (4, L), F32)

        WPo, WSo, WOo, RWo = 0, 8 * 128, 8 * 128 + 8 * 64, 8 * 128 + 8 * 64 + 8 * 1024
        for kc in range(ND):
            p.op("gpsimd", lambda e, kc=kc: e.dma_start(
                out=g["wl1"][:, WPo + kc * 128: WPo + kc * 128 + 128],
                in_=g["WpN"][kc * 128:(kc + 1) * 128, :]), wait=l0_end, dma_="wf")
            p.op("gpsimd", lambda e, kc=kc: e.dma_start(
                out=g["wl1"][:, WSo + kc * 64: WSo + kc * 64 + 64],
                in_=g["WsN"][kc * 128:(kc + 1) * 128, :]), wait=l0_end, dma_="wf")
            p.op("gpsimd", lambda e, kc=kc: e.dma_start(
                out=g["wl1"][:, RWo + kc * 16: RWo + kc * 16 + 16],
                in_=g["rwT"][kc * 128:(kc + 1) * 128, :]), wait=l0_end, dma_="wf")
        p.op("gpsimd", lambda e: e.dma_start(
            out=g["wl1"][0:64, WOo: WOo + 1024], in_=g["Wo1"]), wait=l0_end, dma_="wf")
        p.op("sync", lambda e: e.dma_start(
            out=g["cwb"][:, 0:ND * 4], in_=g["cw1"]), wait=l0_end, dma_="ld")
        p.op("sync", lambda e: e.dma_start(
            out=g["cwb"][:, ND * 4:ND * 5], in_=g["cb1"]), wait=l0_end, dma_="ld")
        p.op("sync", lambda e: e.dma_start(
            out=g["cwb"][:, ND * 5:ND * 6], in_=g["bo1"]), wait=l0_end, dma_="ld")
        p.op("sync", lambda e: e.dma_start(
            out=g["cwb"][:, ND * 6:ND * 7], in_=g["nw128"]), wait=l0_end, dma_="ld")
        p.op("sync", lambda e: e.dma_start(out=g["colv"][:, 0:1], in_=g["bpB"]),
             wait=l0_end, dma_="ld")
        p.op("sync", lambda e: e.dma_start(out=g["colv"][:, 1:2], in_=g["bpC"]),
             wait=l0_end, dma_="ld")
        p.op("sync", lambda e: e.dma_start(out=g["colv"][:, 2:3], in_=g["bs1"]),
             wait=l0_end, dma_="ld")
        p.op("sync", lambda e: e.dma_start(out=g["colv"][:, 3:5], in_=g["adt"]),
             wait=l0_end, dma_="ld")
        p.op("sync", lambda e: e.dma_start(out=g["iota_s"][:], in_=g["iota16"]),
             wait=l0_end, dma_="ld")
        p.op("gpsimd", lambda e: e.dma_start(out=g["ones_s"][:], in_=g["ones1"]),
             wait=l0_end, dma_="wf")
        l1_ld = p.dma_all("wf") + p.dma_all("ld")
        p.op("gpsimd", lambda e: e.memset(g["xn1fm"][:], 0.0), wait=l0_end)
        pad1_z = p.now("P")

        for kc in range(ND):
            p.op("vector", lambda e, kc=kc: e.tensor_scalar(
                g["wl1"][:, RWo + kc * 16: RWo + kc * 16 + 16],
                g["wl1"][:, RWo + kc * 16: RWo + kc * 16 + 16],
                g["cwb"][:, ND * 6 + kc: ND * 6 + kc + 1], None, ALU.mult),
                wait=l1_ld if kc == 0 else ())
        wrnw_done = p.now("V")

        p.op("gpsimd", lambda e: e.dma_start(out=g["xnf_t"][0:1, 0:1024], in_=g["normw1"]),
             wait=l0_end, dma_="wf")
        nw_ld = p.dma_all("wf")
        for hb in range(2):
            p.op("tensor", lambda e, hb=hb: e.matmul(
                g["q1"][:], g["ones_s"][:],
                g["xnf_t"][0:1, hb * 512:(hb + 1) * 512].bitcast(BF16) if False else
                g["xnf_t"][0:1, hb * 512:(hb + 1) * 512],
                start=True, stop=True), wait=nw_ld + [pad1_z, p.now("V")], inc="T")
            p.op("vector", lambda e, hb=hb: e.tensor_copy(
                g["nwb"][:, hb * 512:(hb + 1) * 512], g["q1"][:]), wait=[p.now("T")])
        p.op("gpsimd", lambda e: e.dma_start(out=g["rb16"][0:1, 0:16], in_=g["rbias"]),
             wait=l0_end, dma_="wf")
        rb_ld = p.dma_all("wf")
        p.op("vector", lambda e: e.tensor_copy(g["xnf_t"][0:1, 0:16], g["rb16"][0:1, 0:16]),
             wait=rb_ld)
        rbcast_src = p.now("V")
        p.op("tensor", lambda e: e.matmul(
            g["q1"][:, 0:16], g["ones_s"][:], g["xnf_t"][0:1, 0:16],
            start=True, stop=True), wait=[rbcast_src], inc="T")
        p.op("vector", lambda e: e.tensor_copy(g["rb16"][:], g["q1"][:, 0:16]),
             wait=[p.now("T")])
        nwb_done = p.now("V")

        p.op("scalar", lambda e: e.activation(
            g["colv"][:, 7:9], g["colv"][:, 3:5], AF.Abs), wait=l1_ld + [p.now("V")])
        p.op("vector", lambda e: e.tensor_scalar(
            g["colv"][:, 7:9], g["colv"][:, 7:9], -1.0, None, ALU.mult), wait=[p.now("S")])
        p.op("scalar", lambda e: e.activation(
            g["colv"][:, 7:9], g["colv"][:, 7:9], AF.Exp), wait=[p.now("V")])
        p.op("scalar", lambda e: e.activation(
            g["colv"][:, 7:9], g["colv"][:, 7:9], AF.Ln, bias=1.0))
        p.op("vector", lambda e: e.tensor_scalar(
            g["colv"][:, 9:11], g["colv"][:, 3:5], 0.0, None, ALU.max), wait=[p.now("S")])
        p.op("vector", lambda e: e.tensor_tensor(
            g["colv"][:, 9:11], g["colv"][:, 9:11], g["colv"][:, 7:9], ALU.add))
        p.op("vector", lambda e: e.tensor_tensor(
            g["colv"][:, 11:12], g["colv"][:, 9:10], g["colv"][:, 10:11], ALU.mult))
        p.op("vector", lambda e: e.tensor_scalar(
            g["colv"][:, 11:12], g["colv"][:, 11:12], -1.0, 1.0, ALU.mult, ALU.add))
        p.op("vector", lambda e: e.tensor_copy(g["dpow"][:, 0:1], g["colv"][:, 11:12]))
        for kk in range(1, 12):
            p.op("vector", lambda e, kk=kk: e.tensor_tensor(
                g["dpow"][:, kk:kk + 1], g["dpow"][:, kk - 1:kk], g["dpow"][:, kk - 1:kk],
                ALU.mult))
        dpow_done = p.now("V")

        xt1_prev = {}
        for tt in range(NT):
            sl = tt % 2
            wt_in = list(l0_done) + (xt1_prev[tt - 2] if tt >= 2 else [])
            ld_ent = p.op("sync", lambda e, tt=tt, sl=sl: e.dma_start(
                out=g["xt1"][:, sl * 1024:(sl + 1) * 1024],
                in_=g["xn1_d"][tt * 128:(tt + 1) * 128, :]), wait=wt_in, dma_="ld")
            def x1tr(e, tt=tt, sl=sl):
                r = None
                for fo in range(ND):
                    r = e.transpose(g["qTr"][:, fo * 128: fo * 128 + 128],
                                    g["xt1"][:, sl * 1024 + fo * 128: sl * 1024 + fo * 128 + 128],
                                    g["ident_b"][:])
                return r
            p.op("tensor", x1tr, wait=[ld_ent, pad1_z, p.now("V")], inc="T")
            def x1ev(e, tt=tt):
                dst = g["xn1fm"][:].rearrange("p (f c) -> p f c", c=2056)
                return e.tensor_copy(dst[:, :, 8 + tt * 128: 8 + tt * 128 + 128],
                                     g["qTr"][:].rearrange("p (f c) -> p f c", c=128))
            p.op("vector", x1ev, wait=[p.now("T")])
            x1fm_v = p.now("V")
            p.op("scalar", lambda e, tt=tt, sl=sl: e.activation(
                g["sq_scr"][:, 0:1024], g["xt1"][:, sl * 1024:(sl + 1) * 1024],
                AF.Square, accum_out=g["rstat"][:, tt:tt + 1]), wait=[ld_ent], inc="S")
            def rlog(e, tt=tt):
                r = None
                for kc in range(ND):
                    r = e.matmul(g["q2"][:, tt * 16: tt * 16 + 16],
                                 g["xn1fm"][:, kc * 2056 + 8 + tt * 128: kc * 2056 + 8 + tt * 128 + 128],
                                 g["wl1"][:, RWo + kc * 16: RWo + kc * 16 + 16],
                                 start=(kc == 0), stop=(kc == ND - 1))
                return r
            p.op("tensor", rlog, wait=[x1fm_v, wrnw_done, p.now("V")], inc="T")
            xt1_prev[tt] = [p.now("T"), p.now("S")]
        xn1fm_done_all = p.now("V")
        pass1_T = p.now("T")
        pass1_S = p.now("S")

        RS = g["rstat"]
        p.op("vector", lambda e: e.tensor_scalar(
            RS[:, 16:32], RS[:, 0:16], 1.0 / D, 1e-5, ALU.mult, ALU.add),
            wait=[pass1_S])
        p.op("scalar", lambda e: e.activation(
            RS[:, 16:32], RS[:, 16:32], AF.Ln), wait=[p.now("V")])
        p.op("vector", lambda e: e.tensor_scalar(
            RS[:, 16:32], RS[:, 16:32], -0.5, None, ALU.mult), wait=[p.now("S")])
        p.op("scalar", lambda e: e.activation(
            RS[:, 16:32], RS[:, 16:32], AF.Exp), wait=[p.now("V")], inc="S")
        rstd_all = p.now("S")

        xnfst = {}
        for tt in range(NT):
            sl = tt % 2
            ld2 = p.op("sync", lambda e, tt=tt, sl=sl: e.dma_start(
                out=g["xt1"][:, sl * 1024:(sl + 1) * 1024],
                in_=g["xn1_d"][tt * 128:(tt + 1) * 128, :]),
                wait=[pass1_T, pass1_S] + (xt1_prev[tt - 2][0:1] if False else []) +
                     ([xnfst[tt - 2]] if tt >= 2 else []), dma_="ld")
            p.op("vector", lambda e, tt=tt, sl=sl: e.tensor_scalar(
                g["xnf_t"][:], g["xt1"][:, sl * 1024:(sl + 1) * 1024],
                RS[:, 16 + tt: 17 + tt], None, ALU.mult),
                wait=[ld2, rstd_all, nwb_done] + ([xnfst[tt - 1]] if tt >= 1 else []))
            p.op("vector", lambda e: e.tensor_tensor(
                g["xnf_t"][:], g["xnf_t"][:], g["nwb"][:], ALU.mult))
            xnf_done = p.now("V")
            xnfst[tt] = p.op("sync", lambda e, tt=tt: e.dma_start(
                out=g["xnf_d"][tt * 128:(tt + 1) * 128, :], in_=g["xnf_t"][:]),
                wait=[xnf_done], dma_="st")

        RC = g["rcl"]
        p.op("vector", lambda e: e.tensor_copy(g["rlg"][:], g["q2"][:, 0:256]),
             wait=[pass1_T])
        for tt in range(NT):
            p.op("vector", lambda e, tt=tt: e.tensor_scalar(
                g["rlg"][:, tt * 16:(tt + 1) * 16], g["rlg"][:, tt * 16:(tt + 1) * 16],
                RS[:, 16 + tt: 17 + tt], None, ALU.mult), wait=[rstd_all] if tt == 0 else ())
        for tt in range(NT):
            p.op("vector", lambda e, tt=tt: e.tensor_tensor(
                g["rlg"][:, tt * 16:(tt + 1) * 16], g["rlg"][:, tt * 16:(tt + 1) * 16],
                g["rb16"][:], ALU.add))
        for tt in range(NT):
            p.op("vector", lambda e, tt=tt: e.tensor_reduce(
                RC[:, tt:tt + 1], g["rlg"][:, tt * 16:(tt + 1) * 16], AX.X, ALU.max))
        for tt in range(NT):
            p.op("vector", lambda e, tt=tt: e.tensor_scalar(
                g["rlg"][:, tt * 16:(tt + 1) * 16], g["rlg"][:, tt * 16:(tt + 1) * 16],
                RC[:, tt:tt + 1], None, ALU.subtract))
        p.op("scalar", lambda e: e.activation(
            g["rlg"][:], g["rlg"][:], AF.Exp), wait=[p.now("V")], inc="S")
        p.op("vector", lambda e: e.tensor_copy(g["rsel"][:], g["rlg"][:]),
             wait=[p.now("S")])
        for tt in range(NT):
            p.op("vector", lambda e, tt=tt: e.tensor_reduce(
                RC[:, 16 + tt: 17 + tt], g["rlg"][:, tt * 16:(tt + 1) * 16], AX.X, ALU.add))
        p.op("vector", lambda e: e.reciprocal(RC[:, 16:32], RC[:, 16:32]))
        for tt in range(NT):
            p.op("vector", lambda e, tt=tt: e.tensor_scalar(
                g["rlg"][:, tt * 16:(tt + 1) * 16], g["rlg"][:, tt * 16:(tt + 1) * 16],
                RC[:, 16 + tt: 17 + tt], None, ALU.mult))
        for tt in range(NT):
            p.op("vector", lambda e, tt=tt: e.tensor_reduce(
                RC[:, 32 + tt: 33 + tt], g["rlg"][:, tt * 16:(tt + 1) * 16], AX.X, ALU.max))
        for tt in range(NT):
            p.op("vector", lambda e, tt=tt: e.tensor_scalar(
                g["rsel"][:, tt * 16:(tt + 1) * 16], g["rlg"][:, tt * 16:(tt + 1) * 16],
                RC[:, 32 + tt: 33 + tt], None, ALU.is_ge))
        p.op("vector", lambda e: e.tensor_scalar(
            g["rsel"][:], g["rsel"][:], -99.0, 99.0, ALU.mult, ALU.add))
        for tt in range(NT):
            p.op("vector", lambda e, tt=tt: e.tensor_tensor(
                g["rsel"][:, tt * 16:(tt + 1) * 16], g["rsel"][:, tt * 16:(tt + 1) * 16],
                g["iota_s"][:], ALU.add))
        for tt in range(NT):
            p.op("vector", lambda e, tt=tt: e.tensor_reduce(
                RC[:, 48 + tt: 49 + tt], g["rsel"][:, tt * 16:(tt + 1) * 16], AX.X, ALU.min))
        for tt in range(NT):
            p.op("vector", lambda e, tt=tt: e.tensor_scalar(
                g["rsel"][:, tt * 16:(tt + 1) * 16], g["iota_s"][:],
                RC[:, 48 + tt: 49 + tt], None, ALU.is_equal))
        p.op("vector", lambda e: e.tensor_scalar(
            g["rsel"][:], g["rsel"][:], 2.0, None, ALU.mult))
        p.op("vector", lambda e: e.tensor_tensor(
            g["rlg"][:], g["rlg"][:], g["rsel"][:], ALU.subtract))
        for tt in range(NT):
            p.op("vector", lambda e, tt=tt: e.tensor_reduce(
                RC[:, 64 + tt: 65 + tt], g["rlg"][:, tt * 16:(tt + 1) * 16], AX.X, ALU.max))
        for tt in range(NT):
            p.op("vector", lambda e, tt=tt: e.tensor_scalar(
                g["rsel"][:, tt * 16:(tt + 1) * 16], g["rlg"][:, tt * 16:(tt + 1) * 16],
                RC[:, 64 + tt: 65 + tt], None, ALU.is_ge))
        p.op("vector", lambda e: e.tensor_scalar(
            g["rsel"][:], g["rsel"][:], -99.0, 99.0, ALU.mult, ALU.add))
        for tt in range(NT):
            p.op("vector", lambda e, tt=tt: e.tensor_tensor(
                g["rsel"][:, tt * 16:(tt + 1) * 16], g["rsel"][:, tt * 16:(tt + 1) * 16],
                g["iota_s"][:], ALU.add))
        for tt in range(NT):
            p.op("vector", lambda e, tt=tt: e.tensor_reduce(
                RC[:, 80 + tt: 81 + tt], g["rsel"][:, tt * 16:(tt + 1) * 16], AX.X, ALU.min))
        rout_done = p.now("V")
        for j, c0 in enumerate((48, 80, 32, 64)):
            for tt in range(NT):
                p.op("sync", lambda e, j=j, tt=tt, c0=c0: e.dma_start(
                    out=g["rout_d"][j, tt * 128:(tt + 1) * 128].rearrange(
                        "(p one) -> p one", one=1),
                    in_=RC[:, c0 + tt: c0 + tt + 1]), wait=[rout_done], dma_="st")
        l1r_st = p.dma_all("st")

        for ft in range(ND):
            p.op("vector", lambda e, ft=ft: e.tensor_scalar(
                g["cnv1"][:, ft * 2048:(ft + 1) * 2048],
                g["xn1fm"][:, ft * 2056 + 8: ft * 2056 + 2056],
                g["cwb"][:, ft * 4 + 3: ft * 4 + 4],
                g["cwb"][:, ND * 4 + ft: ND * 4 + ft + 1],
                ALU.mult, ALU.add), wait=[xn1fm_done_all] if ft == 0 else ())
            for tap in (2, 1, 0):
                p.op("vector", lambda e, ft=ft, tap=tap: e.scalar_tensor_tensor(
                    g["cnv1"][:, ft * 2048:(ft + 1) * 2048],
                    g["xn1fm"][:, ft * 2056 + 5 + tap: ft * 2056 + 5 + tap + 2048],
                    g["cwb"][:, ft * 4 + tap: ft * 4 + tap + 1],
                    g["cnv1"][:, ft * 2048:(ft + 1) * 2048],
                    ALU.mult, ALU.add))
        cnv_done = p.now("V")

        for tc in range(4):
            for which, col0, bcol, dst0 in (("B", WPo, 0, 0), ("C", WPo + 64, 1, 2048)):
                def pmm(e, tc=tc, col0=col0):
                    r = None
                    for kc in range(ND):
                        r = e.matmul(g["q1"][0:64, 0:512],
                                     g["wl1"][:, col0 + kc * 128: col0 + kc * 128 + 64],
                                     g["cnv1"][:, kc * 2048 + tc * 512: kc * 2048 + tc * 512 + 512],
                                     start=(kc == 0), stop=(kc == ND - 1))
                    return r
                p.op("tensor", pmm, wait=[cnv_done, p.now("V"), p.now("S")], inc="T")
                pm_d = p.now("T")
                if which == "B":
                    p.op("vector", lambda e: e.tensor_scalar(
                        g["scr1"][:, 512:1024], g["q1"][0:64, 0:512], g["colv"][:, 0:1],
                        None, ALU.add), wait=[pm_d])
                    p.op("scalar", lambda e: e.activation(
                        g["scr1"][:, 0:512], g["scr1"][:, 512:1024], AF.Abs),
                        wait=[p.now("V")])
                    p.op("vector", lambda e: e.tensor_scalar(
                        g["scr1"][:, 0:512], g["scr1"][:, 0:512], -1.0, None, ALU.mult),
                        wait=[p.now("S")])
                    p.op("scalar", lambda e: e.activation(
                        g["scr1"][:, 0:512], g["scr1"][:, 0:512], AF.Exp), wait=[p.now("V")])
                    p.op("scalar", lambda e: e.activation(
                        g["scr1"][:, 0:512], g["scr1"][:, 0:512], AF.Ln, bias=1.0))
                    p.op("vector", lambda e: e.tensor_scalar(
                        g["scr1"][:, 512:1024], g["scr1"][:, 512:1024], 0.0, None, ALU.max),
                        wait=[p.now("S")])
                    p.op("vector", lambda e, tc=tc: e.tensor_tensor(
                        g["BCb"][0:64, tc * 512:(tc + 1) * 512], g["scr1"][:, 512:1024],
                        g["scr1"][:, 0:512], ALU.add))
                else:
                    p.op("scalar", lambda e, tc=tc: e.activation(
                        g["BCb"][0:64, 2048 + tc * 512: 2048 + tc * 512 + 512],
                        g["q1"][0:64, 0:512], AF.Tanh,
                        bias=g["colv"][:, 1:2]), wait=[pm_d], inc="S")
            def umm(e, tc=tc):
                r = None
                for kc in range(ND):
                    r = e.matmul(g["q2"][0:64, 0:512],
                                 g["wl1"][:, WSo + kc * 64: WSo + kc * 64 + 64],
                                 g["xn1fm"][:, kc * 2056 + 8 + tc * 512: kc * 2056 + 8 + tc * 512 + 512],
                                 start=(kc == 0), stop=(kc == ND - 1))
                return r
            p.op("tensor", umm, wait=[xn1fm_done_all, p.now("V")], inc="T")
            p.op("vector", lambda e, tc=tc: e.tensor_scalar(
                g["u_f"][0:64, tc * 512:(tc + 1) * 512], g["q2"][0:64, 0:512],
                g["colv"][:, 2:3], None, ALU.add), wait=[p.now("T")])
        bcu_done = p.now("V")
        bcu_done_s = p.now("S")

        p.op("vector", lambda e: e.tensor_scalar(
            g["w_f"][0:64, :], g["u_f"][0:64, :], g["colv"][:, 10:11], None, ALU.mult),
            wait=[bcu_done, bcu_done_s, dpow_done])
        p.op("vector", lambda e: e.tensor_tensor(
            g["w_f"][0:64, :], g["w_f"][0:64, :], g["BCb"][0:64, 0:2048], ALU.mult))
        p.op("vector", lambda e: e.tensor_copy(g["hp0"][0:64, :], g["w_f"][0:64, :]))
        cur, nxt = "hp0", "hp1"
        for kk in range(11):
            sh = 1 << kk
            p.op("vector", lambda e, cur=cur, nxt=nxt, sh=sh: e.tensor_copy(
                g[nxt][0:64, 0:sh], g[cur][0:64, 0:sh]))
            p.op("vector", lambda e, cur=cur, nxt=nxt, sh=sh, kk=kk: e.scalar_tensor_tensor(
                g[nxt][0:64, sh:2048], g[cur][0:64, 0:2048 - sh],
                g["dpow"][:, kk:kk + 1], g[cur][0:64, sh:2048], ALU.mult, ALU.add))
            cur, nxt = nxt, cur
        p.op("vector", lambda e, cur=cur: e.tensor_tensor(
            g["y_b"][0:64, :], g[cur][0:64, :], g["BCb"][0:64, 2048:4096], ALU.mult))
        y_done = p.now("V")

        for tc in range(4):
            s1ev = {}
            for dt in range(ND):
                pq = "q1" if dt % 2 == 0 else "q2"
                wt = [y_done] + l1_ld
                if dt >= 2:
                    wt.append(s1ev[dt - 2])
                elif tc > 0:
                    wt.append(s1tr_done)
                p.op("tensor", lambda e, tc=tc, dt=dt, pq=pq: e.matmul(
                    g[pq][:, 0:512],
                    g["wl1"][0:64, WOo + dt * 128: WOo + dt * 128 + 128],
                    g["y_b"][0:64, tc * 512:(tc + 1) * 512],
                    start=True, stop=True), wait=wt, inc="T")
                p.op("scalar", lambda e, dt=dt, pq=pq: e.activation(
                    g["s1fm"][:, dt * 512:(dt + 1) * 512], g[pq][:, 0:512], AF.Identity,
                    bias=g["cwb"][:, ND * 5 + dt: ND * 5 + dt + 1]),
                    wait=[p.now("T")], inc="S")
                s1ev[dt] = p.now("S")
            s1fm_done = p.now("S")
            for ts4 in range(4):
                def s1tr(e, tc=tc, ts4=ts4):
                    r = None
                    for dt in range(ND):
                        r = e.transpose(g["qTr"][:, dt * 128: dt * 128 + 128],
                                        g["s1fm"][:, dt * 512 + ts4 * 128: dt * 512 + ts4 * 128 + 128],
                                        g["ident_b"][:])
                    return r
                p.op("tensor", s1tr, wait=[s1fm_done, p.now("V")], inc="T")
                p.op("vector", lambda e: e.tensor_copy(g["s1scr"][:], g["qTr"][:]),
                     wait=[p.now("T")] + ([s1st_prev] if (tc + ts4) > 0 else []))
                s1v = p.now("V")
                s1st_prev = p.op("sync", lambda e, tc=tc, ts4=ts4: e.dma_start(
                    out=g["s1_d"][tc * 512 + ts4 * 128: tc * 512 + ts4 * 128 + 128, :],
                    in_=g["s1scr"][:]), wait=[s1v], dma_="st")
            s1tr_done = p.now("T")
        l1a_done = p.dma_all("st")
        es_l1.close()

        if "stat" in taps:
            dout("tap_stat", (128, 6 * NT), F32)
            p.op("sync", lambda e: e.dma_start(out=g["tap_stat"], in_=g["stat"][:]),
                 wait=[rstd_done], dma_="st")
        if "xn_fm" in taps:
            dout("tap_xn_fm", (128, ND * 2048), BF16)
            p.op("sync", lambda e: e.dma_start(out=g["tap_xn_fm"], in_=g["xnFMb"][:]),
                 wait=[xn_fm_done], dma_="st")
        if "x1" in taps:
            dout("tap_x1", (L, D), F32)
            p.op("sync", lambda e: e.dma_start(out=g["tap_x1"], in_=g["x1_d"]),
                 wait=l0_done, dma_="st")
        if "xn1" in taps:
            dout("tap_xn1", (L, D), BF16)
            p.op("sync", lambda e: e.dma_start(out=g["tap_xn1"], in_=g["xn1_d"]),
                 wait=l0_done, dma_="st")
        if "s1" in taps:
            dout("tap_s1", (L, D), BF16)
            p.op("sync", lambda e: e.dma_start(out=g["tap_s1"], in_=g["s1_d"]),
                 wait=l1a_done, dma_="st")
        if "xnf" in taps:
            dout("tap_xnf", (L, D), BF16)
            p.op("sync", lambda e: e.dma_start(out=g["tap_xnf"], in_=g["xnf_d"]),
                 wait=l1a_done, dma_="st")
        if "rout" in taps:
            dout("tap_rout", (4, L), F32)
            p.op("sync", lambda e: e.dma_start(out=g["tap_rout"], in_=g["rout_d"]),
                 wait=l1a_done, dma_="st")
        if "s_tm" in taps:
            dout("tap_s_tm", (L, D), BF16)
            p.op("sync", lambda e: e.dma_start(out=g["tap_s_tm"], in_=g["s_tm_d"]),
                 wait=attn_done_list, dma_="st")

        if fused:
            din("ohc", (128, 1), F32)
            din("rshift", (128, 1), F32)
            for nm in ("fc1A", "fc1B"):
                din(nm, (D, 4096), F32)
            for nm in ("fc2A", "fc2B"):
                din(nm, (4096, D), F32)
            din("b1A", (128, 32), F32)
            din("b1B", (128, 32), F32)
            din("b2A", (128, 8), F32)
            din("b2B", (128, 8), F32)
            din("cvm", (2, D), F32)
            dram("xnf_all", (NCORES * L, D), BF16)
            dram("rout_all", (32, L), F32)
            dram("eo_sh", (3072, D), BF16)
            dram("eo_all", (NCORES * 3072, D), BF16)
            dram("idxmap", (3072, 1), U32)
            dram("ext_scr", (6, L), F32)
            dout("x2_sh", (L, D), F32)
            g["x1own"] = g["x1_d"]
            g["s1own"] = g["s1_d"]
            g["xnfown"] = g["xnf_d"]

            allst = (p.dma_all("st") + p.dma_all("wf") + p.dma_all("ld") +
                     p.dma_all("wld"))
            p.op("gpsimd", lambda e: e.collective_compute(
                "AllGather", ALU.bypass, replica_groups=[list(range(NCORES))],
                ins=[g["rout_d"]], outs=[g["rout_all"]]), wait=l1r_st)
            cg_rout = p.now("P")
            p.op("gpsimd", lambda e: e.collective_compute(
                "AllGather", ALU.bypass, replica_groups=[list(range(NCORES))],
                ins=[g["xnf_d"]], outs=[g["xnf_all"]]))
            cg1 = p.now("P")

            es_moe = contextlib.ExitStack()
            es_rt = contextlib.ExitStack()

            def sbm(name, shape, dtype, _es=es_moe):
                g[name] = _es.enter_context(nc.sbuf_tensor(name, list(shape), dtype))

            def psm(name, shape, dtype, _es=es_moe):
                g[name] = _es.enter_context(nc.psum_tensor(name, list(shape), dtype))

            def sb_arr(name, shape, dtype, _es=es_rt):
                g[name] = _es.enter_context(
                    nc.sbuf_tensor(name, list(shape), dtype, side="right"))

            routing_buffers(sbm, sb_arr, din, want_w=True, skip_ident=True)
            psm("pR", (128, 512), F32)
            sbm("ohc_s", (128, 1), F32)
            sbm("rshift_s", (128, 1), F32)
            sbm("obuf", (128, 8), U32)
            sbm("zinit", (128, 24), U32)
            sbm("tokids", (128, 128), U32)
            sbm("scr32", (128, 64), F32)
            sbm("idxc32", (128, 32), U32)
            sbm("b1A_s", (128, 32), F32)
            sbm("b1B_s", (128, 32), F32)
            sbm("b2A_s", (128, 8), F32)
            sbm("b2B_s", (128, 8), F32)
            emit_routing_loads(p, g, skip_ident=True)
            p.op("sync", lambda e: e.dma_start(out=g["ohc_s"][:], in_=g["ohc"]),
                 dma_="mld")
            p.op("sync", lambda e: e.dma_start(out=g["rshift_s"][:], in_=g["rshift"]),
                 dma_="mld")
            for nm in ("b1A", "b1B", "b2A", "b2B"):
                p.op("sync", lambda e, nm=nm: e.dma_start(out=g[nm + "_s"][:], in_=g[nm]),
                     dma_="mld")
            bias_ld = p.dma_all("mld")
            bcreg = [None, None, None, None]

            def mkregs(e):
                for i, v in enumerate((3071, NCORES * L - 1, NCORES * 3072 - 1, 1)):
                    bcreg[i] = e.alloc_register(name="bc" + str(i))
                    e.reg_mov(bcreg[i], v)
                return None
            p.op("gpsimd", mkregs)
            rt_done = emit_routing(p, g, nc, want_flat_rows=True, want_w=True,
                                   pre=[cg_rout])
            ex_st = emit_own_extract(nc, p, g, rt_done)
            sc_done = emit_moe_scatter(nc, p, g, rt_done, bcreg, 0, pre=ex_st)
            rt_T = p.now("T")
            rt_V = p.now("V")
            es_rt.close()
            sbm("xgt", (128, 1024), BF16)
            sbm("xgf", (128, ND * 3072), BF16)
            sbm("wslab", (128, 2 * 8192), BF16)
            sbm("hbuf", (128, 32 * 512), BF16)
            sbm("eot", (128, 1024), BF16)
            sbm("eofm", (128, ND * 512), BF16)
            psm("pF0", (128, 512), F32)
            psm("pF1", (128, 512), F32)
            psm("pG", (128, 1024), BF16)
            psm("pO0", (128, 512), F32)
            psm("pO1", (128, 512), F32)
            psm("pO2", (128, 512), F32)
            psm("pO3", (128, 512), F32)
            sbm("cols", (128, 16), F32)
            sbm("idx4", (128, 4), U32)
            sbm("grow1", (128, D), BF16)
            sbm("grow2", (128, D), BF16)
            sbm("cvrow1", (128, D), BF16)
            sbm("cvrow2", (128, D), BF16)
            sbm("xnfo_t", (128, D), BF16)
            sbm("x1t", (128, D), F32)
            sbm("s1t", (128, D), BF16)
            sbm("comb", (128, D), F32)
            sbm("x2t", (128, 2 * D), F32)
            emit_mlp_compute(nc, p, g, bcreg, 1, sc_done, [rt_T, rt_V, cg1], bias_ld)
            p.op("gpsimd", lambda e: e.collective_compute(
                "AllGather", ALU.bypass, replica_groups=[list(range(NCORES))],
                ins=[g["eo_sh"]], outs=[g["eo_all"]]), wait=p.dma_all("mst"))
            cg2 = p.now("P")
            emit_combine(nc, p, g, bcreg, 2, 3, ex_st, [rt_T, rt_V] + allst, [cg2])
            es_moe.close()

        fin = p.dma_all("st") + p.dma_all("wf")
        if fused:
            fin += p.dma_all("mst") + p.dma_all("mld") + p.dma_all("mwf")
        p.op("sync", lambda e: None, wait=fin)

        p.flush(block)
    return nc



def stage_full(inputs):
    in_maps = stage_inputs(inputs)
    f = {k: np.asarray(v) for k, v in inputs.items()}
    FULLE = [4, 6, 7, 9, 10, 12, 13, 15]
    HALFE = [5, 8, 11, 14]
    fc1 = f["L1_ffn_fc1_weight"]; fc2 = f["L1_ffn_fc2_weight"]
    b1 = f["L1_ffn_fc1_bias"]; b2 = f["L1_ffn_fc2_bias"]
    for c in range(NCORES):
        m = in_maps[c]
        eA = FULLE[c]
        eB = HALFE[c // 2]
        m["fc1A"] = np.ascontiguousarray(fc1[eA - 4].T)
        m["fc2A"] = np.ascontiguousarray(fc2[eA - 4].T)
        m["fc1B"] = np.ascontiguousarray(fc1[eB - 4].T)
        m["fc2B"] = np.ascontiguousarray(fc2[eB - 4].T)
        m["b1A"] = np.ascontiguousarray(b1[eA - 4].reshape(32, 128).T)
        m["b1B"] = np.ascontiguousarray(b1[eB - 4].reshape(32, 128).T)
        m["b2A"] = np.ascontiguousarray(b2[eA - 4].reshape(8, 128).T)
        m["b2B"] = np.ascontiguousarray(b2[eB - 4].reshape(8, 128).T)
        m["rshift"] = np.full((128, 1), 3072.0 * c, np.float32)
        m["ohc"] = np.zeros((128, 1), np.float32)
        m["ohc"][c * 16, 0] = 1.0
    return in_maps


def kernel(**inputs):
    in_maps = stage_full(inputs)
    nc = build_program(detect_races=False, fused=True)
    res = run_bass_kernel_spmd(nc, in_maps, core_ids=list(range(NCORES)))
    out = np.zeros((B, L, D), np.float32)
    for c in range(NCORES):
        out[c] = np.asarray(res.results[c]["x2_sh"])
    return out


def kernel_unfused(**inputs):
    in_maps = stage_inputs(inputs)
    nc = build_program(taps=("x1", "s1", "xnf", "rout"), detect_races=False)
    res = run_bass_kernel_spmd(nc, in_maps, core_ids=list(range(NCORES)))
    xnf_all = np.zeros((NCORES * L, D), dtype=ml_dtypes.bfloat16)
    rout_all = np.zeros((32, L), np.float32)
    x1_sh = np.zeros((NCORES, L, D), np.float32)
    s1_sh = np.zeros((NCORES, L, D), dtype=ml_dtypes.bfloat16)
    for c in range(NCORES):
        r = res.results[c]
        xnf_all[c * L:(c + 1) * L] = np.asarray(r["tap_xnf"])
        rout_all[4 * c:4 * c + 4] = np.asarray(r["tap_rout"])
        x1_sh[c] = np.asarray(r["tap_x1"])
        s1_sh[c] = np.asarray(r["tap_s1"])
    in_maps2 = stage_mlp(inputs, xnf_all, rout_all)
    nc2 = build_mlp(detect_races=False)
    res2 = run_bass_kernel_spmd(nc2, in_maps2, core_ids=list(range(NCORES)))
    eo_all = np.zeros((24576, D), dtype=ml_dtypes.bfloat16)
    for c in range(NCORES):
        eo_all[c * 3072:(c + 1) * 3072] = np.asarray(res2.results[c]["eo_sh"])
    in_maps3 = stage_cmb(inputs, xnf_all, rout_all, eo_all, x1_sh, s1_sh)
    nc3 = build_cmb(detect_races=False)
    res3 = run_bass_kernel_spmd(nc3, in_maps3, core_ids=list(range(NCORES)))
    out = np.zeros((B, L, D), np.float32)
    for c in range(NCORES):
        out[c] = np.asarray(res3.results[c]["x2_sh"])
    return out


def run_dev(taps, inputs):
    in_maps = stage_inputs(inputs)
    nc = build_program(taps=taps)
    res = run_bass_kernel_spmd(nc, in_maps, core_ids=list(range(NCORES)))
    return res.results



def stage_mlp(inputs, xnf_all, rout_all):
    f = {k: np.asarray(v) for k, v in inputs.items()}
    FULLE = [4, 6, 7, 9, 10, 12, 13, 15]
    HALFE = [5, 8, 11, 14]
    ii16 = np.arange(128)
    Arow = np.full(16, 1.0e7, np.float32)
    Brow = np.zeros(16, np.float32)
    for c, e in enumerate(FULLE):
        Arow[e] = 3072.0 * c
    for i, e in enumerate(HALFE):
        Arow[e] = 3072.0 * (2 * i) + 2048.0
        Brow[e] = 2048.0
    common = {
        "xnf_all": np.ascontiguousarray(xnf_all),
        "rout_all": np.ascontiguousarray(rout_all),
        "ecol": (ii16 % 16).astype(np.float32).reshape(128, 1),
        "Acol": Arow[(ii16 % 16)].reshape(128, 1),
        "Bcol": Brow[(ii16 % 16)].reshape(128, 1),
        "Mfix": ((ii16[:, None] % 16 == ii16[None, :] % 16) &
                 (ii16[:, None] // 16 < ii16[None, :] // 16)).astype(np.float32),
        "gsum": (ii16[:, None] // 16 == ii16[None, :] // 16).astype(np.float32),
        "sel8": np.zeros((8, 128), dtype=ml_dtypes.bfloat16),
        "identb": np.eye(128, dtype=ml_dtypes.bfloat16),
        "ohch": np.zeros((128, 8), np.float32),
    }
    for ch in range(8):
        common["sel8"][ch, ch * 16:(ch + 1) * 16] = 1
        common["ohch"][ch * 16, ch] = 1
    fc1 = f["L1_ffn_fc1_weight"]; fc2 = f["L1_ffn_fc2_weight"]
    b1 = f["L1_ffn_fc1_bias"]; b2 = f["L1_ffn_fc2_bias"]
    in_maps = []
    for c in range(NCORES):
        m = dict(common)
        eA = FULLE[c]
        eB = HALFE[c // 2]
        m["fc1A"] = np.ascontiguousarray(fc1[eA - 4].T)
        m["fc2A"] = np.ascontiguousarray(fc2[eA - 4].T)
        m["fc1B"] = np.ascontiguousarray(fc1[eB - 4].T)
        m["fc2B"] = np.ascontiguousarray(fc2[eB - 4].T)
        m["b1A"] = np.ascontiguousarray(b1[eA - 4].reshape(32, 128).T)
        m["b1B"] = np.ascontiguousarray(b1[eB - 4].reshape(32, 128).T)
        m["b2A"] = np.ascontiguousarray(b2[eA - 4].reshape(8, 128).T)
        m["b2B"] = np.ascontiguousarray(b2[eB - 4].reshape(8, 128).T)
        m["rshift"] = np.full((128, 1), 3072.0 * c, np.float32)
        in_maps.append(m)
    return in_maps


def emit_routing(p, g, nc, want_flat_rows, want_w, pre=()):
    ALUx = ALU

    def arr(name):
        return g[name]

    for rr in range(4):
        p.op("gpsimd", lambda e, rr=rr: e.dma_start(
            out=g["r8b"][0:8, rr * 2048:(rr + 1) * 2048],
            in_=g["rout_all"].rearrange("(c r) t -> c r t", r=4)[:, rr, :]),
            wait=list(pre), dma_="mld")
    r8b_d = p.dma_all("mld")
    names = ["ids1", "ids2"]
    for j, nm in enumerate(names):
        for q in range(4):
            p.op("tensor", lambda e, j=j, q=q: e.matmul(
                g["pR"][:, 0:512], g["sel8_s"][:, 0:128],
                g["r8b"][0:8, j * 2048 + q * 512: j * 2048 + q * 512 + 512],
                start=True, stop=True), wait=[p.now("V")] + p.dma_all("mld"), inc="T")
            p.op("vector", lambda e, nm=nm, q=q: e.tensor_copy(
                g[nm][:, q * 512:(q + 1) * 512], g["pR"][:, 0:512]), wait=[p.now("T")])
    bcast_d = p.now("V")
    p.op("vector", lambda e: e.tensor_scalar(
        g["oh1"][:], g["ids1"][:], g["ecol_s"][:, 0:1], None, ALUx.is_equal),
        wait=[bcast_d] + p.dma_all("mld"))
    p.op("vector", lambda e: e.tensor_scalar(
        g["oh2"][:], g["ids2"][:], g["ecol_s"][:, 0:1], None, ALUx.is_equal))
    for srcn, dstn in (("oh1", "C1"), ("oh2", "C2")):
        p.op("vector", lambda e, srcn=srcn, dstn=dstn: e.tensor_copy(
            g[dstn][:], g[srcn][:]))
        cur, nxt = dstn, "Cp"
        for kk in range(11):
            sh = 1 << kk
            p.op("vector", lambda e, cur=cur, nxt=nxt, sh=sh: e.tensor_copy(
                g[nxt][:, 0:sh], g[cur][:, 0:sh]))
            p.op("vector", lambda e, cur=cur, nxt=nxt, sh=sh: e.tensor_tensor(
                g[nxt][:, sh:2048], g[cur][:, sh:2048], g[cur][:, 0:2048 - sh], ALUx.add))
            cur, nxt = nxt, cur
        if cur != dstn:
            p.op("vector", lambda e, cur=cur, dstn=dstn: e.tensor_copy(
                g[dstn][:], g[cur][:]))
    cs_d = p.now("V")
    for nmc in ("C1", "C2"):
        p.op("vector", lambda e, nmc=nmc: e.tensor_copy(
            g["tcol"][:, 0:1], g[nmc][:, 2047:2048]), wait=[cs_d])
        p.op("tensor", lambda e: e.matmul(
            g["pR"][:, 0:1], g["Mfix_s"][:], g["tcol"][:, 0:1],
            start=True, stop=True), wait=[p.now("V")] + p.dma_all("mld"), inc="T")
        p.op("vector", lambda e: e.tensor_copy(g["tcol"][:, 1:2], g["pR"][:, 0:1]),
             wait=[p.now("T")])
        p.op("vector", lambda e, nmc=nmc: e.tensor_scalar(
            g[nmc][:], g[nmc][:], g["tcol"][:, 1:2], None, ALUx.add))
    cfix_d = p.now("V")
    p.op("vector", lambda e: e.tensor_tensor(g["Cp"][:], g["C1"][:], g["C2"][:], ALUx.add),
         wait=[cfix_d])
    p.op("vector", lambda e: e.tensor_tensor(g["t1"][:], g["Cp"][:], g["oh2"][:], ALUx.subtract))
    p.op("vector", lambda e: e.tensor_tensor(g["t1"][:], g["t1"][:], g["oh1"][:], ALUx.mult))
    p.op("vector", lambda e: e.tensor_tensor(g["t2"][:], g["Cp"][:], g["oh2"][:], ALUx.mult))
    p.op("vector", lambda e: e.tensor_scalar(
        g["a1"][:], g["oh1"][:], g["Acol_s"][:, 0:1], None, ALUx.mult))
    p.op("vector", lambda e: e.tensor_scalar(
        g["b1x"][:], g["oh1"][:], g["Bcol_s"][:, 0:1], None, ALUx.mult))
    p.op("vector", lambda e: e.tensor_scalar(
        g["a2"][:], g["oh2"][:], g["Acol_s"][:, 0:1], None, ALUx.mult))
    p.op("vector", lambda e: e.tensor_scalar(
        g["b2x"][:], g["oh2"][:], g["Bcol_s"][:, 0:1], None, ALUx.mult))
    pre_d = p.now("V")
    for nmc in ("t1", "t2", "a1", "b1x", "a2", "b2x"):
        for q in range(4):
            p.op("tensor", lambda e, nmc=nmc, q=q: e.matmul(
                g["pR"][:, 0:512], g["gsum_s"][:], g[nmc][:, q * 512:(q + 1) * 512],
                start=True, stop=True), wait=[pre_d, p.now("V")] + p.dma_all("mld"), inc="T")
            p.op("vector", lambda e, nmc=nmc, q=q: e.tensor_copy(
                g[nmc][:, q * 512:(q + 1) * 512], g["pR"][:, 0:512]), wait=[p.now("T")])
    gs_d = p.now("V")
    for kq, (ncount, aa, bb, flat) in enumerate(
            (("t1", "a1", "b1x", "flat1"), ("t2", "a2", "b2x", "flat2"))):
        p.op("vector", lambda e, ncount=ncount: e.tensor_scalar(
            g["Cp"][:], g[ncount][:], -1.0, None, ALUx.add), wait=[gs_d])
        p.op("vector", lambda e: e.tensor_scalar(
            g["t3"][:], g["Cp"][:], 1023.5, None, ALUx.is_ge))
        p.op("vector", lambda e, bb=bb: e.tensor_tensor(
            g["t3"][:], g["t3"][:], g[bb][:], ALUx.mult))
        p.op("vector", lambda e, aa=aa: e.tensor_tensor(
            g["Cp"][:], g["Cp"][:], g[aa][:], ALUx.add))
        p.op("vector", lambda e, flat=flat: e.tensor_tensor(
            g[flat][:], g["Cp"][:], g["t3"][:], ALUx.add))
        p.op("vector", lambda e, ncount=ncount: e.tensor_scalar(
            g["t3"][:], g[ncount][:], 2048.5, None, ALUx.is_gt))
        p.op("vector", lambda e: e.tensor_scalar(
            g["t3"][:], g["t3"][:], 1.0e7, None, ALUx.mult))
        p.op("vector", lambda e, flat=flat: e.tensor_tensor(
            g[flat][:], g[flat][:], g["t3"][:], ALUx.add))
    if want_w:
        for kq, wnm in enumerate(("w1a", "w2a")):
            for q in range(4):
                p.op("tensor", lambda e, kq=kq, q=q: e.matmul(
                    g["pR"][:, 0:512], g["sel8_s"][:, 0:128],
                    g["r8b"][0:8, (2 + kq) * 2048 + q * 512: (2 + kq) * 2048 + q * 512 + 512],
                    start=True, stop=True), wait=[p.now("V")] + p.dma_all("mld"), inc="T")
                p.op("vector", lambda e, wnm=wnm, q=q: e.tensor_copy(
                    g[wnm][:, q * 512:(q + 1) * 512], g["pR"][:, 0:512]), wait=[p.now("T")])
        for kq, (ncount, wnm) in enumerate((("t1", "w1a"), ("t2", "w2a"))):
            p.op("vector", lambda e, ncount=ncount: e.tensor_scalar(
                g["t3"][:], g[ncount][:], 2048.5, None, ALUx.is_le))
            p.op("vector", lambda e, wnm=wnm: e.tensor_tensor(
                g[wnm][:], g[wnm][:], g["t3"][:], ALUx.mult))
    return p.now("V")


def routing_buffers(sb_const, sb_arr, din, want_w, skip_ident=False):
    din("ecol", (128, 1), F32)
    din("Acol", (128, 1), F32)
    din("Bcol", (128, 1), F32)
    din("Mfix", (128, 128), F32)
    din("gsum", (128, 128), F32)
    din("sel8", (8, 128), BF16)
    if not skip_ident:
        din("identb", (128, 128), BF16)
    din("ohch", (128, 8), F32)
    sb_arr("r8b", (8, 8192), BF16)
    sb_arr("rowx", (1, 2048), F32)
    names = ["ids1", "ids2", "oh1", "oh2", "C1", "C2", "Cp",
             "t1", "t2", "t3", "a1", "b1x", "a2", "b2x", "flat1", "flat2"]
    if want_w:
        names += ["w1a", "w2a"]
    for nm in names:
        sb_arr(nm, (128, 2048), F32)
    sb_const("tcol", (128, 4), F32)
    sb_const("ecol_s", (128, 1), F32)
    sb_const("Acol_s", (128, 1), F32)
    sb_const("Bcol_s", (128, 1), F32)
    sb_const("Mfix_s", (128, 128), F32)
    sb_const("gsum_s", (128, 128), F32)
    sb_const("sel8_s", (8, 128), BF16)
    if not skip_ident:
        sb_const("ident_b", (128, 128), BF16)
    sb_const("ohch_s", (128, 8), F32)


def emit_routing_loads(p, g, skip_ident=False):
    for nm in ("ecol", "Acol", "Bcol", "Mfix", "gsum", "ohch"):
        p.op("sync", lambda e, nm=nm: e.dma_start(out=g[nm + "_s"][:], in_=g[nm]), dma_="mld")
    p.op("gpsimd", lambda e: e.dma_start(out=g["sel8_s"][:], in_=g["sel8"]), dma_="mld")
    if not skip_ident:
        p.op("gpsimd", lambda e: e.dma_start(out=g["ident_b"][:], in_=g["identb"]), dma_="mld")


def emit_moe_scatter(nc, p, g, rt_done, bcreg, bx, pre=()):
    p.op("gpsimd", lambda e: e.memset(g["zinit"][:], 10000000))
    zi_d = p.now("P")
    idx_init = [p.op("sync", lambda e: e.dma_start(
        out=g["idxmap"].rearrange("(p t) one -> p (t one)", p=128),
        in_=g["zinit"][:]), wait=[zi_d], dma_="mst")]
    for ch in range(8):
        p.op("gpsimd", lambda e, ch=ch: e.iota(
            g["tokids"][:, ch * 16:(ch + 1) * 16], [[128, 16]],
            base=2048 * ch, channel_multiplier=1))
    tok_d = p.now("P")
    sc_prev = {0: [], 1: []}
    for tr in range(16):
        pp = tr % 2
        c0 = pp * 16
        for kq, arrn in enumerate(("flat1", "flat2")):
            p.op("tensor", lambda e, arrn=arrn, tr=tr, kq=kq, c0=c0: e.matmul(
                g["pR"][:, c0 + kq * 8: c0 + kq * 8 + 8],
                g[arrn][:, tr * 128:(tr + 1) * 128],
                g["ohch_s"][:, 0:8],
                start=True, stop=True),
                wait=[rt_done, p.now("V")] + (list(pre) if tr == 0 else []), inc="T")
        mm_d = p.now("T")
        p.op("vector", lambda e, c0=c0: e.tensor_copy(
            g["scr32"][:, c0:c0 + 16], g["pR"][:, c0:c0 + 16]),
            wait=[mm_d] + sc_prev[pp])
        p.op("vector", lambda e, c0=c0: e.tensor_scalar(
            g["scr32"][:, c0:c0 + 16], g["scr32"][:, c0:c0 + 16],
            g["rshift_s"][:, 0:1], None, ALU.subtract))
        p.op("vector", lambda e, c0=c0: e.tensor_scalar(
            g["scr32"][:, 32 + c0:32 + c0 + 16], g["scr32"][:, c0:c0 + 16],
            0.0, 1.0e7, ALU.is_lt, ALU.mult))
        p.op("vector", lambda e, c0=c0: e.tensor_tensor(
            g["scr32"][:, c0:c0 + 16], g["scr32"][:, c0:c0 + 16],
            g["scr32"][:, 32 + c0:32 + c0 + 16], ALU.add))
        p.op("vector", lambda e, c0=c0: e.tensor_copy(
            g["idxc32"][:, c0:c0 + 16], g["scr32"][:, c0:c0 + 16]))
        v_d = p.now("V")
        ent = []
        for kq in range(2):
            for ch in range(8):
                ent.append(p.op("gpsimd",
                    lambda e, tr=tr, kq=kq, ch=ch, c0=c0: e.indirect_dma_start(
                        out=g["idxmap"],
                        out_offset=IndirectOffsetOnAxis(
                            ap=g["idxc32"][:, c0 + kq * 8 + ch: c0 + kq * 8 + ch + 1],
                            axis=0),
                        in_=g["tokids"][:, ch * 16 + tr: ch * 16 + tr + 1],
                        in_offset=None,
                        bounds_check=bcreg[bx], oob_is_err=False),
                    wait=[v_d, tok_d] + idx_init, dma_="msc"))
        sc_prev[pp] = ent
    return p.dma_all("msc") + idx_init


def emit_mlp_compute(nc, p, g, bcreg, bx, sc_done, pre, bias_ld):
    gat_d = []
    for rt in range(24):
        ld = p.op("sync", lambda e, rt=rt: e.dma_start(
            out=g["obuf"][:, 0:1],
            in_=g["idxmap"][rt * 128:(rt + 1) * 128, :]),
            wait=sc_done + gat_d, dma_="mld")
        p.op("vector", lambda e, rt=rt: e.memset(g["xgt"][:], 0.0),
             wait=[ld] + (list(pre) if rt == 0 else []))
        z_d = p.now("V")
        gt = p.op("gpsimd", lambda e: e.indirect_dma_start(
            out=g["xgt"][:], out_offset=None,
            in_=g["xnf_all"],
            in_offset=IndirectOffsetOnAxis(ap=g["obuf"][:, 0:1], axis=0),
            bounds_check=bcreg[bx], oob_is_err=False),
            wait=[z_d], dma_="mld")
        def xtr(e, rt=rt):
            r = None
            for fo in range(ND):
                r = e.transpose(g["pG"][:, fo * 128: fo * 128 + 128],
                                g["xgt"][:, fo * 128: fo * 128 + 128],
                                g["ident_b"][:])
            return r
        p.op("tensor", xtr, wait=[gt, p.now("V")] + p.dma_all("mld"), inc="T")
        def xev(e, rt=rt):
            dst = g["xgf"][:].rearrange("p (f t) -> p f t", t=3072)
            return e.tensor_copy(dst[:, :, rt * 128:(rt + 1) * 128],
                                 g["pG"][:].rearrange("p (f t) -> p f t", t=128))
        p.op("vector", xev, wait=[p.now("T")])
        gat_d = [p.now("V")]
    xgf_done = p.now("V")

    eo_st = []
    slab_last = [xgf_done, xgf_done]
    eo_evac_prev = xgf_done
    f2_lastT = xgf_done
    for tci in range(6):
        part = "A" if tci < 4 else "B"
        sc0 = tci * 512
        b1n, b2n = "b1" + part, "b2" + part
        for qq in range(4):
            sl = qq % 2
            slab = []
            for kc in range(ND):
                slab.append(p.op("gpsimd", lambda e, part=part, qq=qq, kc=kc, sl=sl:
                    e.dma_start(
                        out=g["wslab"][:, sl * 8192 + kc * 1024: sl * 8192 + kc * 1024 + 1024],
                        in_=g["fc1" + part][kc * 128:(kc + 1) * 128,
                                            qq * 1024:(qq + 1) * 1024]),
                    wait=[slab_last[sl]], dma_="mwf"))
            for hq in range(8):
                ht = qq * 8 + hq
                pF = "pF0" if ht % 2 == 0 else "pF1"
                def f1mm(e, sc0=sc0, hq=hq, sl=sl, pF=pF):
                    r = None
                    for kc in range(ND):
                        r = e.matmul(
                            g[pF][:],
                            g["wslab"][:, sl * 8192 + kc * 1024 + hq * 128:
                                       sl * 8192 + kc * 1024 + hq * 128 + 128],
                            g["xgf"][:, kc * 3072 + sc0: kc * 3072 + sc0 + 512],
                            start=(kc == 0), stop=(kc == ND - 1))
                    return r
                wt = slab + [xgf_done]
                if ht >= 2:
                    wt.append(hged[ht - 2])
                elif tci > 0:
                    wt.append(f2_lastT)
                p.op("tensor", f1mm, wait=wt, inc="T")
                p.op("scalar", lambda e, ht=ht, pF=pF, b1n=b1n: e.activation(
                    g["hbuf"][:, ht * 512:(ht + 1) * 512], g[pF][:], AF.Gelu,
                    bias=g[b1n + "_s"][:, ht:ht + 1]), wait=[p.now("T")] + bias_ld, inc="S")
                if ht == 0:
                    hged = {}
                hged[ht] = p.now("S")
            slab_last[sl] = p.now("T")
        h_done = p.now("S")
        for dh in range(2):
            for qq in range(4):
                sl = qq % 2
                slab = []
                for hc8 in range(ND):
                    slab.append(p.op("gpsimd",
                        lambda e, part=part, qq=qq, hc8=hc8, sl=sl, dh=dh: e.dma_start(
                            out=g["wslab"][:, sl * 8192 + hc8 * 512:
                                           sl * 8192 + hc8 * 512 + 512],
                            in_=g["fc2" + part][qq * 1024 + hc8 * 128:
                                                qq * 1024 + hc8 * 128 + 128,
                                                dh * 512:(dh + 1) * 512]),
                        wait=[slab_last[sl]], dma_="mwf"))
                def f2mm(e, qq=qq, sl=sl, dh=dh):
                    r = None
                    for hc8 in range(ND):
                        hc = qq * 8 + hc8
                        for d4 in range(4):
                            r = e.matmul(
                                g["pO" + str(d4)][:],
                                g["wslab"][:, sl * 8192 + hc8 * 512 + d4 * 128:
                                           sl * 8192 + hc8 * 512 + d4 * 128 + 128],
                                g["hbuf"][:, hc * 512:(hc + 1) * 512],
                                start=(qq == 0 and hc8 == 0),
                                stop=(qq == 3 and hc8 == ND - 1))
                    return r
                wt = slab + [h_done]
                if qq == 0:
                    wt.append(eo_evac_prev)
                p.op("tensor", f2mm, wait=wt, inc="T")
                slab_last[sl] = p.now("T")
            f2_lastT = p.now("T")
            for d4 in range(4):
                dt = dh * 4 + d4
                p.op("scalar", lambda e, d4=d4, dt=dt, b2n=b2n: e.activation(
                    g["eofm"][:, dt * 512:(dt + 1) * 512], g["pO" + str(d4)][:],
                    AF.Identity, bias=g[b2n + "_s"][:, dt:dt + 1]),
                    wait=[f2_lastT] + bias_ld, inc="S")
            eo_evac_prev = p.now("S")
        for st4 in range(4):
            def etr(e, st4=st4):
                r = None
                for dt in range(ND):
                    r = e.transpose(g["pG"][:, dt * 128: dt * 128 + 128],
                                    g["eofm"][:, dt * 512 + st4 * 128: dt * 512 + st4 * 128 + 128],
                                    g["ident_b"][:])
                return r
            p.op("tensor", etr, wait=[eo_evac_prev, p.now("V")], inc="T")
            p.op("vector", lambda e: e.tensor_copy(g["eot"][:], g["pG"][:]),
                 wait=[p.now("T")] + eo_st[-1:])
            ev_d = p.now("V")
            eo_st.append(p.op("sync", lambda e, sc0=sc0, st4=st4: e.dma_start(
                out=g["eo_sh"][sc0 + st4 * 128: sc0 + st4 * 128 + 128, :],
                in_=g["eot"][:]), wait=[ev_d], dma_="mst"))
    return eo_st


def build_mlp(detect_races=True, taps=()):
    nc = bass.Bass("TRN2", target_bir_lowering=False, detect_race_conditions=detect_races)
    es = contextlib.ExitStack()
    g = {}

    def din(name, shape, dtype):
        g[name] = nc.dram_tensor(name, list(shape), dtype, kind="ExternalInput").ap()

    def dout(name, shape, dtype):
        g[name] = nc.dram_tensor(name, list(shape), dtype, kind="ExternalOutput").ap()

    def dram(name, shape, dtype):
        g[name] = nc.dram_tensor(name, list(shape), dtype, kind="Internal").ap()

    def sb(name, shape, dtype):
        g[name] = es.enter_context(nc.sbuf_tensor(name, list(shape), dtype))

    def ps(name, shape, dtype):
        g[name] = es.enter_context(nc.psum_tensor(name, list(shape), dtype))

    din("xnf_all", (NCORES * L, D), BF16)
    din("rout_all", (32, L), F32)
    for nm in ("fc1A", "fc1B"):
        din(nm, (D, 4096), F32)
    for nm in ("fc2A", "fc2B"):
        din(nm, (4096, D), F32)
    din("b1A", (128, 32), F32)
    din("b1B", (128, 32), F32)
    din("b2A", (128, 8), F32)
    din("b2B", (128, 8), F32)
    din("rshift", (128, 1), F32)
    dout("eo_sh", (3072, D), BF16)
    dram("idxmap", (3072, 1), U32)

    with es:
        p = Prog(nc, es)
        block = es.enter_context(nc.Block())
        es_rt = contextlib.ExitStack()

        def sb_arr(name, shape, dtype):
            g[name] = es_rt.enter_context(nc.sbuf_tensor(name, list(shape), dtype, side="right"))

        routing_buffers(sb, sb_arr, din, want_w=False)
        ps("pR", (128, 512), F32)
        sb("rshift_s", (128, 1), F32)
        sb("obuf", (128, 8), U32)
        sb("zinit", (128, 24), U32)
        sb("tokids", (128, 128), U32)
        sb("scr32", (128, 64), F32)
        sb("idxc32", (128, 32), U32)
        ps("pF0", (128, 512), F32)
        ps("pF1", (128, 512), F32)
        ps("pG", (128, 1024), BF16)
        ps("pO0", (128, 512), F32)
        ps("pO1", (128, 512), F32)
        ps("pO2", (128, 512), F32)
        ps("pO3", (128, 512), F32)
        sb("b1A_s", (128, 32), F32)
        sb("b1B_s", (128, 32), F32)
        sb("b2A_s", (128, 8), F32)
        sb("b2B_s", (128, 8), F32)

        emit_routing_loads(p, g)
        for nm in ("b1A", "b1B", "b2A", "b2B"):
            p.op("sync", lambda e, nm=nm: e.dma_start(out=g[nm + "_s"][:], in_=g[nm]),
                 dma_="mld")
        bias_ld = p.dma_all("mld")
        bcreg = [None, None]

        def mkregs(e):
            bcreg[0] = e.alloc_register(name="bc0")
            e.reg_mov(bcreg[0], 3071)
            bcreg[1] = e.alloc_register(name="bc1")
            e.reg_mov(bcreg[1], NCORES * L - 1)
            return None
        p.op("gpsimd", mkregs)
        p.op("sync", lambda e: e.dma_start(out=g["rshift_s"][:], in_=g["rshift"]), dma_="mld")
        rt_done = emit_routing(p, g, nc, want_flat_rows=True, want_w=False)
        sc_done = emit_moe_scatter(nc, p, g, rt_done, bcreg, 0)
        rt_T = p.now("T")
        rt_V = p.now("V")
        es_rt.close()
        sb("xgt", (128, 1024), BF16)
        sb("xgf", (128, ND * 3072), BF16)
        sb("wslab", (128, 2 * 8192), BF16)
        sb("hbuf", (128, 32 * 512), BF16)
        sb("eot", (128, 1024), BF16)
        sb("eofm", (128, ND * 512), BF16)
        emit_mlp_compute(nc, p, g, bcreg, 1, sc_done, [rt_T, rt_V], bias_ld)
        p.op("sync", lambda e: None, wait=p.dma_all("mst") + p.dma_all("mwf"))
        p.flush(block)
    return nc



def stage_cmb(inputs, xnf_all, rout_all, eo_all, x1_sh, s1_sh):
    f = {k: np.asarray(v) for k, v in inputs.items()}
    ii16 = np.arange(128)
    FULLE = [4, 6, 7, 9, 10, 12, 13, 15]
    HALFE = [5, 8, 11, 14]
    Arow = np.full(16, 1.0e7, np.float32)
    Brow = np.zeros(16, np.float32)
    for c, e in enumerate(FULLE):
        Arow[e] = 3072.0 * c
    for i, e in enumerate(HALFE):
        Arow[e] = 3072.0 * (2 * i) + 2048.0
        Brow[e] = 2048.0
    common = {
        "xnf_all": np.ascontiguousarray(xnf_all),
        "rout_all": np.ascontiguousarray(rout_all),
        "eo_all": np.ascontiguousarray(eo_all),
        "cvm": np.ascontiguousarray(f["L1_ffn_constant_vectors"]),
        "ecol": (ii16 % 16).astype(np.float32).reshape(128, 1),
        "Acol": Arow[(ii16 % 16)].reshape(128, 1),
        "Bcol": Brow[(ii16 % 16)].reshape(128, 1),
        "Mfix": ((ii16[:, None] % 16 == ii16[None, :] % 16) &
                 (ii16[:, None] // 16 < ii16[None, :] // 16)).astype(np.float32),
        "gsum": (ii16[:, None] // 16 == ii16[None, :] // 16).astype(np.float32),
        "sel8": np.zeros((8, 128), dtype=ml_dtypes.bfloat16),
        "identb": np.eye(128, dtype=ml_dtypes.bfloat16),
        "ohch": np.zeros((128, 8), np.float32),
    }
    for ch in range(8):
        common["sel8"][ch, ch * 16:(ch + 1) * 16] = 1
        common["ohch"][ch * 16, ch] = 1
    in_maps = []
    for c in range(NCORES):
        m = dict(common)
        m["ohc"] = np.zeros((128, 1), np.float32)
        m["ohc"][c * 16, 0] = 1.0
        m["x1own"] = np.ascontiguousarray(x1_sh[c])
        m["s1own"] = np.ascontiguousarray(s1_sh[c])
        m["xnfown"] = np.ascontiguousarray(xnf_all[c * L:(c + 1) * L])
        in_maps.append(m)
    return in_maps


def emit_own_extract(nc, p, g, rt_done):
    ex_st = []
    for ai, arrn in enumerate(("flat1", "flat2", "w1a", "w2a", "ids1", "ids2")):
        for q in range(4):
            p.op("tensor", lambda e, arrn=arrn, q=q: e.matmul(
                g["pR"][0:1, 0:512], g["ohc_s"][:, 0:1],
                g[arrn][:, q * 512:(q + 1) * 512],
                start=True, stop=True), wait=[rt_done, p.now("V")] + p.dma_all("mld"),
                inc="T")
            p.op("vector", lambda e, q=q: e.tensor_copy(
                g["rowx"][0:1, q * 512:(q + 1) * 512], g["pR"][0:1, 0:512]),
                wait=[p.now("T")])
        rv = p.now("V")
        ex_st.append(p.op("sync", lambda e, ai=ai: e.dma_start(
            out=g["ext_scr"][ai, :].rearrange("(one t) -> one t", one=1),
            in_=g["rowx"][0:1, 0:2048]), wait=[rv], dma_="mst"))
    return ex_st


def emit_combine(nc, p, g, bcreg, beo, bcv, ex_st, first_extra, gath_extra):
    prev_x2 = []
    for tt in range(NT):
        sl = tt % 2
        colld = []
        for ai in range(6):
            colld.append(p.op("sync", lambda e, ai=ai, tt=tt: e.dma_start(
                out=g["cols"][:, ai:ai + 1],
                in_=g["ext_scr"][ai, tt * 128:(tt + 1) * 128].rearrange(
                    "(p one) -> p one", one=1)),
                wait=ex_st + (list(first_extra) if tt == 0 else []), dma_="mld"))
        p.op("vector", lambda e: e.tensor_copy(g["idx4"][:, 0:1], g["cols"][:, 0:1]),
             wait=colld + prev_x2)
        p.op("vector", lambda e: e.tensor_copy(g["idx4"][:, 1:2], g["cols"][:, 1:2]))
        for kq in range(2):
            p.op("vector", lambda e, kq=kq: e.tensor_scalar(
                g["cols"][:, 6 + kq: 7 + kq], g["cols"][:, 4 + kq: 5 + kq],
                -2.0, None, ALU.add))
            p.op("vector", lambda e, kq=kq: e.tensor_scalar(
                g["cols"][:, 8 + kq: 9 + kq], g["cols"][:, 6 + kq: 7 + kq],
                -0.5, None, ALU.is_ge))
            p.op("vector", lambda e, kq=kq: e.tensor_scalar(
                g["cols"][:, 10 + kq: 11 + kq], g["cols"][:, 6 + kq: 7 + kq],
                1.5, None, ALU.is_le))
            p.op("vector", lambda e, kq=kq: e.tensor_tensor(
                g["cols"][:, 8 + kq: 9 + kq], g["cols"][:, 8 + kq: 9 + kq],
                g["cols"][:, 10 + kq: 11 + kq], ALU.mult))
            p.op("vector", lambda e, kq=kq: e.tensor_scalar(
                g["cols"][:, 10 + kq: 11 + kq], g["cols"][:, 8 + kq: 9 + kq],
                -1.0e7, 1.0e7, ALU.mult, ALU.add))
            p.op("vector", lambda e, kq=kq: e.tensor_tensor(
                g["cols"][:, 6 + kq: 7 + kq], g["cols"][:, 6 + kq: 7 + kq],
                g["cols"][:, 10 + kq: 11 + kq], ALU.add))
            p.op("vector", lambda e, kq=kq: e.tensor_copy(
                g["idx4"][:, 2 + kq: 3 + kq], g["cols"][:, 6 + kq: 7 + kq]))
        p.op("vector", lambda e: e.tensor_scalar(
            g["cols"][:, 12:13], g["cols"][:, 4:5], 1.0, None, ALU.is_equal))
        p.op("vector", lambda e: e.tensor_tensor(
            g["cols"][:, 12:13], g["cols"][:, 12:13], g["cols"][:, 2:3], ALU.mult))
        p.op("vector", lambda e: e.tensor_scalar(
            g["cols"][:, 13:14], g["cols"][:, 5:6], 1.0, None, ALU.is_equal))
        p.op("vector", lambda e: e.tensor_tensor(
            g["cols"][:, 13:14], g["cols"][:, 13:14], g["cols"][:, 3:4], ALU.mult))
        p.op("vector", lambda e: e.tensor_tensor(
            g["cols"][:, 12:13], g["cols"][:, 12:13], g["cols"][:, 13:14], ALU.add))
        p.op("vector", lambda e: e.memset(g["grow1"][:], 0.0))
        p.op("vector", lambda e: e.memset(g["grow2"][:], 0.0))
        p.op("vector", lambda e: e.memset(g["cvrow1"][:], 0.0))
        p.op("vector", lambda e: e.memset(g["cvrow2"][:], 0.0))
        idx_d = p.now("V")
        gts = []
        for kq, (dst, icol, srcn, breg) in enumerate((
                ("grow1", 0, "eo_all", beo), ("grow2", 1, "eo_all", beo),
                ("cvrow1", 2, "cvm", bcv), ("cvrow2", 3, "cvm", bcv))):
            gts.append(p.op("gpsimd", lambda e, dst=dst, icol=icol, srcn=srcn, breg=breg:
                e.indirect_dma_start(
                    out=g[dst][:], out_offset=None,
                    in_=g[srcn],
                    in_offset=IndirectOffsetOnAxis(ap=g["idx4"][:, icol:icol + 1], axis=0),
                    bounds_check=bcreg[breg], oob_is_err=False),
                wait=[idx_d] + list(gath_extra), dma_="mld"))
        ld0 = list(first_extra) if tt == 0 else []
        xld = p.op("sync", lambda e, tt=tt: e.dma_start(
            out=g["xnfo_t"][:], in_=g["xnfown"][tt * 128:(tt + 1) * 128, :]),
            wait=prev_x2 + ld0, dma_="mld")
        x1ld = p.op("sync", lambda e, tt=tt: e.dma_start(
            out=g["x1t"][:], in_=g["x1own"][tt * 128:(tt + 1) * 128, :]),
            wait=prev_x2 + ld0, dma_="mld")
        s1ld = p.op("sync", lambda e, tt=tt: e.dma_start(
            out=g["s1t"][:], in_=g["s1own"][tt * 128:(tt + 1) * 128, :]),
            wait=prev_x2 + ld0, dma_="mld")
        p.op("vector", lambda e: e.tensor_tensor(
            g["grow1"][:], g["grow1"][:], g["cvrow1"][:], ALU.add), wait=gts)
        p.op("vector", lambda e: e.tensor_tensor(
            g["grow2"][:], g["grow2"][:], g["cvrow2"][:], ALU.add))
        p.op("vector", lambda e: e.tensor_scalar(
            g["comb"][:], g["grow1"][:], g["cols"][:, 2:3], None, ALU.mult))
        p.op("vector", lambda e: e.tensor_scalar(
            g["grow2"][:], g["grow2"][:], g["cols"][:, 3:4], None, ALU.mult))
        p.op("vector", lambda e: e.tensor_tensor(
            g["comb"][:], g["comb"][:], g["grow2"][:], ALU.add))
        p.op("vector", lambda e: e.tensor_scalar(
            g["xnfo_t"][:], g["xnfo_t"][:], g["cols"][:, 12:13], None, ALU.mult),
            wait=[xld])
        p.op("vector", lambda e: e.tensor_tensor(
            g["comb"][:], g["comb"][:], g["xnfo_t"][:], ALU.add))
        p.op("vector", lambda e: e.tensor_tensor(
            g["cols"][:, 14:15], g["cols"][:, 2:3], g["cols"][:, 3:4], ALU.add))
        p.op("vector", lambda e: e.tensor_scalar(
            g["cols"][:, 15:16], g["cols"][:, 14:15], 0.0, None, ALU.is_gt))
        p.op("vector", lambda e: e.tensor_tensor(
            g["cols"][:, 14:15], g["cols"][:, 14:15], g["cols"][:, 15:16], ALU.mult))
        p.op("vector", lambda e: e.tensor_scalar(
            g["cols"][:, 15:16], g["cols"][:, 15:16], -1.0, 1.0, ALU.mult, ALU.add))
        p.op("vector", lambda e: e.tensor_tensor(
            g["cols"][:, 14:15], g["cols"][:, 14:15], g["cols"][:, 15:16], ALU.add))
        p.op("vector", lambda e: e.reciprocal(g["cols"][:, 15:16], g["cols"][:, 14:15]))
        p.op("vector", lambda e: e.tensor_scalar(
            g["comb"][:], g["comb"][:], g["cols"][:, 15:16], None, ALU.mult))
        cmb_d = p.now("V")
        p.op("scalar", lambda e: e.activation(
            g["comb"][:], g["comb"][:], AF.Silu), wait=[cmb_d], inc="S")
        p.op("vector", lambda e, sl=sl: e.tensor_copy(
            g["x2t"][:, sl * D:(sl + 1) * D], g["comb"][:]), wait=[p.now("S")])
        p.op("vector", lambda e, sl=sl: e.tensor_tensor(
            g["x2t"][:, sl * D:(sl + 1) * D], g["x2t"][:, sl * D:(sl + 1) * D],
            g["s1t"][:], ALU.mult), wait=[s1ld])
        p.op("vector", lambda e, sl=sl: e.tensor_tensor(
            g["x2t"][:, sl * D:(sl + 1) * D], g["x2t"][:, sl * D:(sl + 1) * D],
            g["x1t"][:], ALU.add), wait=[x1ld])
        x2_d = p.now("V")
        prev_x2 = [p.op("sync", lambda e, tt=tt, sl=sl: e.dma_start(
            out=g["x2_sh"][tt * 128:(tt + 1) * 128, :],
            in_=g["x2t"][:, sl * D:(sl + 1) * D]), wait=[x2_d], dma_="mst")]
    return prev_x2


def build_cmb(detect_races=True):
    nc = bass.Bass("TRN2", target_bir_lowering=False, detect_race_conditions=detect_races)
    es = contextlib.ExitStack()
    g = {}

    def din(name, shape, dtype):
        g[name] = nc.dram_tensor(name, list(shape), dtype, kind="ExternalInput").ap()

    def dout(name, shape, dtype):
        g[name] = nc.dram_tensor(name, list(shape), dtype, kind="ExternalOutput").ap()

    def dram(name, shape, dtype):
        g[name] = nc.dram_tensor(name, list(shape), dtype, kind="Internal").ap()

    def sb(name, shape, dtype):
        g[name] = es.enter_context(nc.sbuf_tensor(name, list(shape), dtype))

    def ps(name, shape, dtype):
        g[name] = es.enter_context(nc.psum_tensor(name, list(shape), dtype))

    din("xnf_all", (NCORES * L, D), BF16)
    din("rout_all", (32, L), F32)
    din("eo_all", (24576, D), BF16)
    din("cvm", (2, D), F32)
    din("x1own", (L, D), F32)
    din("s1own", (L, D), BF16)
    din("xnfown", (L, D), BF16)
    din("ohc", (128, 1), F32)
    dout("x2_sh", (L, D), F32)
    dram("ext_scr", (6, L), F32)

    with es:
        p = Prog(nc, es)
        block = es.enter_context(nc.Block())
        es_rt = contextlib.ExitStack()

        def sb_arr(name, shape, dtype):
            g[name] = es_rt.enter_context(nc.sbuf_tensor(name, list(shape), dtype, side="right"))

        routing_buffers(sb, sb_arr, din, want_w=True)
        ps("pR", (128, 512), F32)
        sb("ohc_s", (128, 1), F32)
        sb("cols", (128, 16), F32)
        sb("idx4", (128, 4), U32)
        sb("grow1", (128, D), BF16)
        sb("grow2", (128, D), BF16)
        sb("cvrow1", (128, D), BF16)
        sb("cvrow2", (128, D), BF16)
        sb("xnfo_t", (128, D), BF16)
        sb("x1t", (128, D), F32)
        sb("s1t", (128, D), BF16)
        sb("comb", (128, D), F32)
        sb("x2t", (128, 2 * D), F32)

        emit_routing_loads(p, g)
        p.op("sync", lambda e: e.dma_start(out=g["ohc_s"][:], in_=g["ohc"]), dma_="mld")
        rt_done = emit_routing(p, g, nc, want_flat_rows=True, want_w=True)

        bcreg = [None, None]

        def mkregs(e):
            bcreg[0] = e.alloc_register(name="bc0")
            e.reg_mov(bcreg[0], 24575)
            bcreg[1] = e.alloc_register(name="bc1")
            e.reg_mov(bcreg[1], 1)
            return None
        p.op("gpsimd", mkregs)

        ex_st = emit_own_extract(nc, p, g, rt_done)
        rtT = p.now("T")
        rtV = p.now("V")
        es_rt.close()

        emit_combine(nc, p, g, bcreg, 0, 1, ex_st, [rtT, rtV], [])
        p.op("sync", lambda e: None, wait=p.dma_all("mst") + p.dma_all("mld"))
        p.flush(block)
    return nc
